# version 04
# baseline (speedup 1.0000x reference)
import math
from contextlib import ExitStack

import numpy as np
import ml_dtypes

import concourse.bass as bass
import concourse.mybir as mybir
from concourse.bass_utils import run_bass_kernel_spmd

F32 = mybir.dt.float32
BF16 = mybir.dt.bfloat16
I32 = mybir.dt.int32
AF = mybir.ActivationFunctionType
ALU = mybir.AluOpType
AX = mybir.AxisListType.X
bfnp = ml_dtypes.bfloat16

D = 2048
S = 8192
T = 1024
NH = 16
NC = 16
FI = 8192
NCI = 64
E3 = 6144
SCALE = 1.0 / math.sqrt(128.0)
EPS = 1e-5
NCORES = 8
CROWS = 2 * D + NH
NEG = -1.0e30

STAGES = ("s1", "s2", "s3", "s4")


def rank_of_block(b):
    return b if b <= 7 else 15 - b


def col_of_block(b):
    return 0 if b <= 7 else 512


class Sem:
    def __init__(self, ctx, nc, name):
        self.h = ctx.enter_context(nc.semaphore(name))
        self.v = 0

    def inc(self, instr, n=1):
        instr.then_inc(self.h, n)
        self.v += n
        return self.v


class B:
    pass


def build_nc():
    nc = bass.Bass("TRN2", target_bir_lowering=False)
    b = B()
    b.nc = nc

    def di(name, shape, dt):
        return nc.dram_tensor(name, shape, dt, kind="ExternalInput")

    b.xt = di("xt", [D, T], F32)
    b.cost = di("cost", [1024, T], BF16)
    b.sint = di("sint", [1024, T], BF16)
    b.stair = di("stair", [512, 512], BF16)
    b.maskcol = di("maskcol", [128, 23], F32)
    b.lngb = di("lngb", [128, 32], F32)
    b.bqkvo = di("bqkvo", [128, 64], F32)
    b.bin3 = di("bin3", [128, 48], F32)
    b.bout1 = di("bout1", [128, 16], F32)
    b.gsc = di("gsc", [128, 64], F32)
    b.usc = di("usc", [128, 64], F32)
    b.dsc = di("dsc", [128, 16], F32)
    b.ident = di("ident", [128, 128], BF16)
    b.ident32 = di("ident32", [128, 128], F32)
    b.onescol = di("onescol", [128, 1], BF16)
    b.onesrow32 = di("onesrow32", [1, 128], F32)
    b.perm = di("perm", [128, 128], BF16)
    b.wq = di("wq", [D, D], F32)
    b.wk = di("wk", [D, D], F32)
    b.wv = di("wv", [D, D], F32)
    b.wo = di("wo", [D, D], F32)
    b.win = di("win", [E3, D], F32)
    b.wout = di("wout", [D, D], F32)
    b.gw = di("gw", [FI, D], I32)
    b.uw = di("uw", [FI, D], I32)
    b.dw = di("dw", [D, FI], I32)

    b.yt = nc.dram_tensor("yt", [D, T], F32, kind="ExternalOutput")

    b.wq_b = nc.dram_tensor("wq_b", [D, D], BF16)
    b.wk_b = nc.dram_tensor("wk_b", [D, D], BF16)
    b.wv_b = nc.dram_tensor("wv_b", [D, D], BF16)
    b.wo_b = nc.dram_tensor("wo_b", [D, D], BF16)
    b.win_c = nc.dram_tensor("win_c", [E3, D], BF16)
    b.wout_c = nc.dram_tensor("wout_c", [D, D], BF16)
    b.gw_c = nc.dram_tensor("gw_c", [FI, D], BF16)
    b.uw_c = nc.dram_tensor("uw_c", [FI, D], BF16)
    b.dw_c = nc.dram_tensor("dw_c", [D, FI], BF16)
    b.winT = nc.dram_tensor("winT", [D, E3], BF16)
    b.woutT = nc.dram_tensor("woutT", [D, D], BF16)
    b.gwT = nc.dram_tensor("gwT", [D, FI], BF16)
    b.uwT = nc.dram_tensor("uwT", [D, FI], BF16)
    b.dwT = nc.dram_tensor("dwT", [FI, D], BF16)
    b.kvloc = nc.dram_tensor("kvloc", [CROWS, T], BF16)
    b.kvg = nc.dram_tensor("kvg", [NCORES * CROWS, T], BF16)
    b.abuf_i = nc.dram_tensor("abuf_i", [1, D], F32)
    b.abuf_o = nc.dram_tensor("abuf_o", [1, D], F32)

    ctx = ExitStack()
    with ctx:
        b.qS = Sem(ctx, nc, "qS")
        b.qG = Sem(ctx, nc, "qG")
        b.qG2 = Sem(ctx, nc, "qG2")
        b.cc = Sem(ctx, nc, "cc")
        b.pe = Sem(ctx, nc, "pe")
        b.dv = Sem(ctx, nc, "dv")
        b.ac = Sem(ctx, nc, "ac")
        b.bar = Sem(ctx, nc, "bar")
        b.qW = [Sem(ctx, nc, f"qW{i}") for i in range(4)]
        b.qK = [Sem(ctx, nc, f"qK{i}") for i in range(2)]
        b.qV = [Sem(ctx, nc, f"qV{i}") for i in range(2)]
        b.qTA = [Sem(ctx, nc, f"qTA{i}") for i in range(2)]
        b.qTS = [Sem(ctx, nc, f"qTS{i}") for i in range(2)]
        b.qST = Sem(ctx, nc, "qST")
        b.qRP = Sem(ctx, nc, "qRP")
        b.qKN = Sem(ctx, nc, "qKN")

        def sbt(name, shape, dt):
            return ctx.enter_context(nc.sbuf_tensor(name, shape, dt))

        b.r_sb = sbt("r_sb", [128, NC * T], F32)
        b.xn_sb = sbt("xn_sb", [128, NC * T], BF16)
        b.pA = sbt("pA", [128, NC * T], BF16)
        b.pB = sbt("pB", [128, NC * T], BF16)
        b.stat = sbt("stat", [1, 4 * T], F32)
        b.srow = sbt("srow", [1, 2 * T], F32)
        b.c_lngb = sbt("c_lngb", [128, 32], F32)
        b.c_bqkvo = sbt("c_bqkvo", [128, 64], F32)
        b.c_bin3 = sbt("c_bin3", [128, 48], F32)
        b.c_bout1 = sbt("c_bout1", [128, 16], F32)
        b.c_gsc = sbt("c_gsc", [128, 64], F32)
        b.c_usc = sbt("c_usc", [128, 64], F32)
        b.c_dsc = sbt("c_dsc", [128, 16], F32)
        b.c_mask = sbt("c_mask", [128, 23], F32)
        b.c_id = sbt("c_id", [128, 128], BF16)
        b.c_id32 = sbt("c_id32", [128, 128], F32)
        b.c_oc = sbt("c_oc", [128, 1], BF16)
        b.c_or32 = sbt("c_or32", [1, 128], F32)
        b.c_perm = sbt("c_perm", [128, 128], BF16)

        b.block = ctx.enter_context(nc.Block())
        block = b.block
        qS, qG, qG2, cc, pe, dv, ac = (b.qS, b.qG, b.qG2, b.cc, b.pe, b.dv,
                                       b.ac)
        bar, qW, qK, qV = b.bar, b.qW, b.qK, b.qV
        qTA, qTS, qST, qRP, qKN = b.qTA, b.qTS, b.qST, b.qRP, b.qKN

        b.nbar = [0]

        def barrier():
            b.nbar[0] += 1
            k = 5 * b.nbar[0]
            sy_drain = [(s, s.v) for s in [qS, qST, qRP, qKN] + qW + qTS
                        if s.v > 0]
            sc_drain = [(s, s.v) for s in qTA + qV if s.v > 0]
            g_drain = [(s, s.v) for s in (qG, qG2, cc) if s.v > 0]

            def mk_sync(e):
                for s, v in sy_drain:
                    e.wait_ge(s.h, v)
                bar.inc(e.nop())
                e.wait_ge(bar.h, k)

            def mk_scalar(e):
                if ac.v > 0:
                    e.wait_ge(ac.h, ac.v)
                for s, v in sc_drain:
                    e.wait_ge(s.h, v)
                bar.inc(e.nop())
                e.wait_ge(bar.h, k)

            def mk_gp(e):
                for s, v in g_drain:
                    e.wait_ge(s.h, v)
                bar.inc(e.nop())
                e.wait_ge(bar.h, k)

            def mk_t(e):
                if pe.v > 0:
                    e.wait_ge(pe.h, pe.v)
                bar.inc(e.nop())
                e.wait_ge(bar.h, k)

            def mk_v(e):
                if dv.v > 0:
                    e.wait_ge(dv.h, dv.v)
                bar.inc(e.nop())
                e.wait_ge(bar.h, k)
            block.sync(mk_sync)
            block.tensor(mk_t)
            block.vector(mk_v)
            block.scalar(mk_scalar)
            block.gpsimd(mk_gp)
            bar.v = k

        def p0(sync):
            for d, s in [
                (b.c_lngb, b.lngb), (b.c_bqkvo, b.bqkvo), (b.c_bin3, b.bin3),
                (b.c_bout1, b.bout1), (b.c_gsc, b.gsc), (b.c_usc, b.usc),
                (b.c_dsc, b.dsc), (b.c_mask, b.maskcol), (b.c_id, b.ident),
                (b.c_id32, b.ident32), (b.c_oc, b.onescol),
                (b.c_or32, b.onesrow32), (b.c_perm, b.perm),
            ]:
                qS.inc(sync.dma_start(out=d[:, :], in_=s[:, :]), 16)
            for j in range(4):
                qS.inc(sync.dma_start(
                    out=b.r_sb[:, j * 4 * T:(j + 1) * 4 * T],
                    in_=bass.AP(b.xt, j * 4 * 128 * T,
                                [[T, 128], [128 * T, 4], [1, T]])), 16)
        block.sync(p0)

        cast_done = {}

        def p1_g(g):
            jobs = [
                ("wk", b.wk_b, b.wk, D, qG), ("wv", b.wv_b, b.wv, D, qG),
                ("wq", b.wq_b, b.wq, D, qG), ("wo", b.wo_b, b.wo, D, qG),
                ("gw", b.gw_c, b.gw, FI, qG2), ("uw", b.uw_c, b.uw, FI, qG2),
                ("dw", b.dw_c, b.dw, D, qG2), ("win", b.win_c, b.win, E3, qG2),
                ("wout", b.wout_c, b.wout, D, qG2),
            ]
            for name, dst, src, rows, q in jobs:
                for r0 in range(0, rows, 512):
                    q.inc(g.dma_start(out=dst[r0:r0 + 512, :],
                                      in_=src[r0:r0 + 512, :]), 16)
                cast_done[name] = (q, q.v)
        block.gpsimd(p1_g)
        qG_attn_total = qG.v

        RG = 2048
        tr_plan = []
        for name, src, dst, R, C in [
            ("gw", b.gw_c, b.gwT, FI, D), ("uw", b.uw_c, b.uwT, FI, D),
            ("dw", b.dw_c, b.dwT, D, FI), ("win", b.win_c, b.winT, E3, D),
            ("wout", b.wout_c, b.woutT, D, D),
        ]:
            for c0 in range(0, C, 128):
                for r0 in range(0, R, RG):
                    tr_plan.append((name, src, dst, r0, c0))
        _tstg_cm = [nc.sbuf_tensor(f"tstg{i}", [128, RG], BF16) for i in range(2)]
        tstg = [cm.__enter__() for cm in _tstg_cm]
        nTR = len(tr_plan)
        qG2_cast_total = 16 * sum(
            rows // 512 for rows in (FI, FI, D, E3, D))
        tload = [None] * nTR
        tstore = [None] * nTR

        def p1_s(scalar):
            scalar.wait_ge(qG2.h, qG2_cast_total)
            for i, (name, src, dst, r0, c0) in enumerate(tr_plan):
                if i >= 2:
                    scalar.wait_ge(qTS[i % 2].h, tstore[i - 2])
                qTA[i % 2].inc(scalar.dma_start(
                    out=tstg[i % 2][:, :],
                    in_=src[r0:r0 + RG, c0:c0 + 128], transpose=True), 16)
                tload[i] = qTA[i % 2].v
        for i in range(nTR):
            tload[i] = 16 * (i // 2 + 1)
            tstore[i] = 16 * (i // 2 + 1)
        block.scalar(p1_s)

        def p1_st(sync):
            for i, (name, src, dst, r0, c0) in enumerate(tr_plan):
                sync.wait_ge(qTA[i % 2].h, tload[i])
                qTS[i % 2].inc(sync.dma_start(
                    out=dst[c0:c0 + 128, r0:r0 + RG],
                    in_=tstg[i % 2][:, :]), 16)
        block.sync(p1_st)

        for cm in reversed(_tstg_cm):
            cm.__exit__(None, None, None)
        barrier()

        b.ln_id = [0]

        def emit_ln(ph, double):
            b.ln_id[0] += 1
            u = b.ln_id[0]
            lnctx = ExitStack()
            b.lnA = lnctx.enter_context(nc.sbuf_tensor(f"lnA{u}", [128, T], F32))
            b.lnB = lnctx.enter_context(nc.sbuf_tensor(f"lnB{u}", [128, T], F32))
            p_s1 = lnctx.enter_context(nc.psum_tensor(f"ln_s1_{u}", [1, T], F32))
            p_s2 = lnctx.enter_context(nc.psum_tensor(f"ln_s2_{u}", [1, T], F32))
            p_bc = lnctx.enter_context(nc.psum_tensor(f"ln_bc_{u}", [128, T], F32))
            p_bc2 = lnctx.enter_context(nc.psum_tensor(f"ln_bc2_{u}", [128, T], F32))

            def sA(scalar):
                for c in range(NC):
                    ac.inc(scalar.activation(
                        b.xn_sb[:, c * T:(c + 1) * T],
                        b.r_sb[:, c * T:(c + 1) * T], AF.Copy))
            block.scalar(sA)
            acA = ac.v

            def tA(tensor):
                tensor.wait_ge(ac.h, acA)
                for tt in range(2):
                    for c in range(NC):
                        m = tensor.matmul(
                            p_s1[:, tt * 512:(tt + 1) * 512], b.c_oc[:, :],
                            b.xn_sb[:, c * T + tt * 512:c * T + (tt + 1) * 512],
                            start=(c == 0), stop=(c == NC - 1))
                pe.inc(m)
            block.tensor(tA)
            peA = pe.v

            def sB(scalar):
                scalar.wait_ge(pe.h, peA)
                for c in range(NC):
                    ac.inc(scalar.activation(
                        b.xn_sb[:, c * T:(c + 1) * T],
                        b.r_sb[:, c * T:(c + 1) * T], AF.Square))
            block.scalar(sB)
            acB = ac.v

            def tB(tensor):
                tensor.wait_ge(ac.h, acB)
                for tt in range(2):
                    for c in range(NC):
                        m = tensor.matmul(
                            p_s2[:, tt * 512:(tt + 1) * 512], b.c_oc[:, :],
                            b.xn_sb[:, c * T + tt * 512:c * T + (tt + 1) * 512],
                            start=(c == 0), stop=(c == NC - 1))
                pe.inc(m)
            block.tensor(tB)
            peB = pe.v

            mu = b.stat[:, 0:T]
            var = b.stat[:, T:2 * T]
            al = b.stat[:, 2 * T:3 * T]
            be = b.stat[:, 3 * T:4 * T]
            t1 = b.srow[:, 0:T]
            t2 = b.srow[:, T:2 * T]

            def v1(vector):
                vector.wait_ge(pe.h, peB)
                dv.inc(vector.tensor_scalar(mu, p_s1[:, :], 1.0 / D, None, ALU.mult))
                dv.inc(vector.tensor_scalar(var, p_s2[:, :], 1.0 / D, None, ALU.mult))
                vector.wait_ge(dv.h, dv.v)
                dv.inc(vector.tensor_tensor(t1, mu, mu, ALU.mult))
                vector.wait_ge(dv.h, dv.v)
                dv.inc(vector.tensor_tensor(var, var, t1, ALU.subtract))
                vector.wait_ge(dv.h, dv.v)
            block.vector(v1)
            dv1 = dv.v

            def v1b(vector):
                vector.wait_ge(dv.h, dv1)
                dv.inc(vector.tensor_scalar(t1, var, EPS, None, ALU.add))
                vector.wait_ge(dv.h, dv.v)
            block.vector(v1b)
            dv1b = dv.v

            def s1(scalar):
                scalar.wait_ge(dv.h, dv1b)
                ac.inc(scalar.activation(t1, t1, AF.Sqrt))
            block.scalar(s1)
            ac1 = ac.v

            def v2(vector):
                vector.wait_ge(ac.h, ac1)
                dv.inc(vector.reciprocal(al, t1))
                vector.wait_ge(dv.h, dv.v)
                if double:
                    dv.inc(vector.tensor_tensor(t2, al, al, ALU.mult))
                    vector.wait_ge(dv.h, dv.v)
                    dv.inc(vector.tensor_tensor(t2, var, t2, ALU.mult))
                    vector.wait_ge(dv.h, dv.v)
            block.vector(v2)
            dv2 = dv.v

            if double:
                def v2b(vector):
                    vector.wait_ge(dv.h, dv2)
                    dv.inc(vector.tensor_scalar(t2, t2, EPS, None, ALU.add))
                    vector.wait_ge(dv.h, dv.v)
                block.vector(v2b)
                dv2b = dv.v

                def s2(scalar):
                    scalar.wait_ge(dv.h, dv2b)
                    ac.inc(scalar.activation(t2, t2, AF.Sqrt))
                block.scalar(s2)
                ac2 = ac.v

                def v3(vector):
                    vector.wait_ge(ac.h, ac2)
                    dv.inc(vector.reciprocal(t1, t2))
                    vector.wait_ge(dv.h, dv.v)
                    dv.inc(vector.tensor_tensor(al, al, t1, ALU.mult))
                    vector.wait_ge(dv.h, dv.v)
                block.vector(v3)

            def v4(vector):
                vector.wait_ge(dv.h, dv.v)
                dv.inc(vector.tensor_tensor(be, mu, al, ALU.mult))
                vector.wait_ge(dv.h, dv.v)
                dv.inc(vector.tensor_scalar(be, be, -1.0, None, ALU.mult))
                vector.wait_ge(dv.h, dv.v)
            block.vector(v4)
            dv4 = dv.v

            def tC(tensor):
                tensor.wait_ge(dv.h, dv4)
                for tt in range(2):
                    m = tensor.matmul(p_bc[:, tt * 512:(tt + 1) * 512],
                                      b.c_or32[:, :],
                                      al[0:1, tt * 512:(tt + 1) * 512],
                                      start=True, stop=True)
                    m = tensor.matmul(p_bc2[:, tt * 512:(tt + 1) * 512],
                                      b.c_or32[:, :],
                                      be[0:1, tt * 512:(tt + 1) * 512],
                                      start=True, stop=True)
                pe.inc(m)
            block.tensor(tC)
            peC = pe.v

            def v5(vector):
                vector.wait_ge(pe.h, peC)
                dv.inc(vector.tensor_copy(b.lnA[:, :], p_bc[:, :]))
                dv.inc(vector.tensor_copy(b.lnB[:, :], p_bc2[:, :]))
                vector.wait_ge(dv.h, dv.v)
                for c in range(NC):
                    dv.inc(vector.tensor_tensor(
                        b.xn_sb[:, c * T:(c + 1) * T],
                        b.r_sb[:, c * T:(c + 1) * T], b.lnA[:, :], ALU.mult))
                    vector.wait_ge(dv.h, dv.v)
                    dv.inc(vector.tensor_tensor(
                        b.xn_sb[:, c * T:(c + 1) * T],
                        b.xn_sb[:, c * T:(c + 1) * T], b.lnB[:, :], ALU.add))
                    vector.wait_ge(dv.h, dv.v)
            block.vector(v5)
            dv5 = dv.v

            if double:
                def s3(scalar):
                    scalar.wait_ge(dv.h, dv5)
                    for c in range(NC):
                        ac.inc(scalar.activation(
                            b.xn_sb[:, c * T:(c + 1) * T],
                            b.xn_sb[:, c * T:(c + 1) * T], AF.Identity,
                            bias=b.c_lngb[:, 16 + c:17 + c],
                            scale=b.c_lngb[:, c:c + 1]))
                block.scalar(s3)
            lnctx.close()
            barrier()

        def emit_proj(ph, name, wten, wcol0, n_d, n_e, eg, tts, rhs_fn, evac):
            width = 512 * len(tts)
            NBUF = 4
            prctx = ExitStack()
            wbuf = [prctx.enter_context(
                nc.sbuf_tensor(f"{name}_w{i}", [128, eg * 128], BF16))
                for i in range(NBUF)]
            pp = [prctx.enter_context(
                nc.psum_tensor(f"{name}_p{i}", [128, width], F32))
                for i in range(2 * eg)]
            n_g = n_e // eg
            steps = [(g, dc) for g in range(n_g) for dc in range(n_d)]
            ns = len(steps)
            pe0 = pe.v
            loadm = [qW[i % 4].v + 16 * (i // 4 + 1) for i in range(ns)]
            mmm = [pe0 + (i + 1) for i in range(ns)]
            dv0, ac0 = dv.v, ac.v
            ndv, nac = evac.ndv, evac.nac
            evdv = [dv0 + (g + 1) * ndv * eg for g in range(n_g)]
            evac_ = [ac0 + (g + 1) * nac * eg for g in range(n_g)]

            def psync(sync):
                for i, (g, dc) in enumerate(steps):
                    if i >= NBUF:
                        sync.wait_ge(pe.h, mmm[i - NBUF])
                    qW[i % 4].inc(sync.dma_start(
                        out=wbuf[i % NBUF][:, :],
                        in_=wten[dc * 128:(dc + 1) * 128,
                                 wcol0 + g * eg * 128:
                                 wcol0 + (g + 1) * eg * 128]), 16)
            block.sync(psync)
            assert qW[(ns - 1) % 4].v == loadm[-1]

            def ptens(tensor):
                for i, (g, dc) in enumerate(steps):
                    tensor.wait_ge(qW[i % 4].h, loadm[i])
                    if dc == 0 and g >= 2:
                        if ndv:
                            tensor.wait_ge(dv.h, evdv[g - 2])
                        if nac:
                            tensor.wait_ge(ac.h, evac_[g - 2])
                    m = None
                    for e2 in range(eg):
                        for j, tt in enumerate(tts):
                            m = tensor.matmul(
                                pp[(g % 2) * eg + e2][:, j * 512:(j + 1) * 512],
                                wbuf[i % NBUF][:, e2 * 128:(e2 + 1) * 128],
                                rhs_fn(dc, tt),
                                start=(dc == 0), stop=(dc == n_d - 1))
                    pe.inc(m)
            block.tensor(ptens)
            assert pe.v == mmm[-1]

            if ndv:
                def pvec(vector):
                    for g in range(n_g):
                        vector.wait_ge(pe.h, mmm[(g + 1) * n_d - 1])
                        for e2 in range(eg):
                            evac.dv(vector, g * eg + e2,
                                    pp[(g % 2) * eg + e2])
                block.vector(pvec)
                assert dv.v == evdv[-1], (name, dv.v, evdv[-1])
            if nac:
                def psca(scalar):
                    for g in range(n_g):
                        scalar.wait_ge(pe.h, mmm[(g + 1) * n_d - 1])
                        for e2 in range(eg):
                            evac.ac(scalar, g * eg + e2,
                                    pp[(g % 2) * eg + e2])
                block.scalar(psca)
                assert ac.v == evac_[-1], (name, ac.v, evac_[-1])
            prctx.close()

        class EvacBias:
            ndv, nac = 0, 1

            def __init__(self, dst, dstoff, bias_sb, bcol0, width):
                self.dst, self.off, self.bias, self.b0 = dst, dstoff, bias_sb, bcol0
                self.w = width

            def ac(self, scalar, ec, pap):
                ac.inc(scalar.activation(
                    self.dst[:, self.off + ec * self.w:
                             self.off + (ec + 1) * self.w],
                    pap[:, 0:self.w], AF.Identity,
                    bias=self.bias[:, self.b0 + ec:self.b0 + ec + 1]))

        class EvacResid:
            ndv, nac = 1, 0

            def __init__(self, bias_sb, bcol0, width):
                self.bias, self.b0, self.w = bias_sb, bcol0, width

            def dv(self, vector, ec, pap):
                if self.bias is not None:
                    dv.inc(vector.scalar_tensor_tensor(
                        b.r_sb[:, ec * T:ec * T + self.w],
                        pap[:, 0:self.w],
                        self.bias[:, self.b0 + ec:self.b0 + ec + 1],
                        b.r_sb[:, ec * T:ec * T + self.w],
                        ALU.add, ALU.add))
                else:
                    dv.inc(vector.tensor_tensor(
                        b.r_sb[:, ec * T:ec * T + self.w],
                        pap[:, 0:self.w],
                        b.r_sb[:, ec * T:ec * T + self.w], ALU.add))

        if "s1" in STAGES:
            with ExitStack() as ph:
                emit_ln(ph, double=True)

                c_stair = ph.enter_context(
                    nc.sbuf_tensor("c_stair", [128, 4 * 512], BF16))

                def ldst(sync):
                    qS.inc(sync.dma_start(
                        out=c_stair[:, :],
                        in_=bass.AP(b.stair, 0,
                                    [[512, 128], [128 * 512, 4], [1, 512]])),
                        16)
                block.sync(ldst)

                emit_proj(ph, "kp", b.wk_b, 0, NC, NC, 2, (0, 1),
                          lambda dc, tt: b.xn_sb[:, dc * T + tt * 512:
                                                 dc * T + (tt + 1) * 512],
                          EvacBias(b.pA, 0, b.c_bqkvo, 16, T))
                barrier()

                knrow = ph.enter_context(nc.sbuf_tensor("knrow", [1, T], BF16))
                sqt = ph.enter_context(nc.sbuf_tensor("sqt", [128, T], BF16))
                knctx = ExitStack()
                p_kn = knctx.enter_context(nc.psum_tensor("p_kn", [1, T], F32))
                pe_prev = pe.v
                dv_prev = dv.v
                qs_prev = qST.v
                for h in range(NH):
                    def ksq(scalar, h=h, pe_prev=pe_prev):
                        scalar.wait_ge(pe.h, pe_prev)
                        ac.inc(scalar.activation(
                            sqt[:, :], b.pA[:, h * T:(h + 1) * T], AF.Square))
                    block.scalar(ksq)
                    ac_h = ac.v

                    def kmm(tensor, h=h, ac_h=ac_h, dv_prev=dv_prev):
                        tensor.wait_ge(ac.h, ac_h)
                        tensor.wait_ge(dv.h, dv_prev)
                        for tt in range(2):
                            m = tensor.matmul(
                                p_kn[:, tt * 512:(tt + 1) * 512], b.c_oc[:, :],
                                sqt[:, tt * 512:(tt + 1) * 512],
                                start=True, stop=True)
                        pe.inc(m)
                    block.tensor(kmm)
                    pe_h = pe.v
                    pe_prev = pe_h

                    def kev(vector, h=h, pe_h=pe_h, qs_prev=qs_prev):
                        vector.wait_ge(pe.h, pe_h)
                        vector.wait_ge(qST.h, qs_prev)
                        dv.inc(vector.tensor_copy(knrow[:, :], p_kn[:, :]))
                        vector.wait_ge(dv.h, dv.v)
                    block.vector(kev)
                    dv_h = dv.v
                    dv_prev = dv_h

                    def kst(sync, h=h, dv_h=dv_h):
                        sync.wait_ge(dv.h, dv_h)
                        qST.inc(sync.dma_start(
                            out=b.kvloc[2 * D + h:2 * D + h + 1, :],
                            in_=knrow[:, :]), 16)
                        qST.inc(sync.dma_start(
                            out=b.kvloc[h * 128:(h + 1) * 128, :],
                            in_=b.pA[:, h * T:(h + 1) * T]), 16)
                    block.sync(kst)
                    qs_prev = qST.v
                knctx.close()
                barrier()

                emit_proj(ph, "vp", b.wv_b, 0, NC, NC, 2, (0, 1),
                          lambda dc, tt: b.xn_sb[:, dc * T + tt * 512:
                                                 dc * T + (tt + 1) * 512],
                          EvacBias(b.pA, 0, b.c_bqkvo, 32, T))
                barrier()

                def vst(sync):
                    for h in range(NH):
                        qST.inc(sync.dma_start(
                            out=b.kvloc[D + h * 128:D + (h + 1) * 128, :],
                            in_=b.pA[:, h * T:(h + 1) * T]), 16)
                block.sync(vst)
                qs_vst = qST.v

                def gat(g):
                    g.wait_ge(qST.h, qs_vst)
                    cc.inc(g.collective_compute(
                        "AllGather", ALU.bypass,
                        replica_groups=[list(range(NCORES))],
                        ins=[b.kvloc.ap().opt()], outs=[b.kvg.ap().opt()]))
                block.gpsimd(gat)
                cc_gather = cc.v

                def qgate(scalar):
                    scalar.wait_ge(qST.h, qs_vst)
                block.scalar(qgate)
                emit_proj(ph, "qp", b.wq_b, 0, NC, NC, 2, (0, 1),
                          lambda dc, tt: b.xn_sb[:, dc * T + tt * 512:
                                                 dc * T + (tt + 1) * 512],
                          EvacBias(b.pA, 0, b.c_bqkvo, 0, T))
                barrier()

                bias_hb = ph.enter_context(
                    nc.sbuf_tensor("bias_hb", [128, 32], F32))
                qnctx = ExitStack()
                knall = qnctx.enter_context(
                    nc.sbuf_tensor("knall", [1, 4 * T], BF16))
                p_qn = qnctx.enter_context(nc.psum_tensor("p_qn", [1, T], F32))
                p_bb = qnctx.enter_context(nc.psum_tensor("p_bb", [128, 32], F32))
                mrow = b.srow
                pe_prev = pe.v
                dv_prev = dv.v
                for h in range(NH):
                    def qsq(scalar, h=h, pe_prev=pe_prev):
                        scalar.wait_ge(pe.h, pe_prev)
                        ac.inc(scalar.activation(
                            sqt[:, :], b.pA[:, h * T:(h + 1) * T], AF.Square))
                    block.scalar(qsq)
                    ac_h = ac.v

                    def qmm(tensor, h=h, ac_h=ac_h, dv_prev=dv_prev):
                        tensor.wait_ge(ac.h, ac_h)
                        tensor.wait_ge(dv.h, dv_prev)
                        for tt in range(2):
                            m = tensor.matmul(
                                p_qn[:, tt * 512:(tt + 1) * 512], b.c_oc[:, :],
                                sqt[:, tt * 512:(tt + 1) * 512],
                                start=True, stop=True)
                        pe.inc(m)
                    block.tensor(qmm)
                    pe_h = pe.v
                    pe_prev = pe_h

                    def qld(sync, h=h, dv_prev=dv_prev):
                        sync.wait_ge(cc.h, cc_gather)
                        sync.wait_ge(dv.h, dv_prev)
                        qKN.inc(sync.dma_start(
                            out=knall[:, :],
                            in_=bass.AP(b.kvg, (2 * D + h) * T,
                                        [[CROWS * T, 4], [1, T]])), 16)
                    block.sync(qld)
                    qs_h1 = qKN.v

                    def qrd(vector, h=h, pe_h=pe_h, qs_h1=qs_h1):
                        vector.wait_ge(pe.h, pe_h)
                        vector.wait_ge(dv.h, dv.v)
                        for blk in range(2):
                            dv.inc(vector.reduce_max(
                                mrow[:, 2 * h + blk:2 * h + blk + 1],
                                p_qn[:, blk * 512:(blk + 1) * 512], AX))
                        vector.wait_ge(qKN.h, qs_h1)
                        vector.wait_ge(dv.h, dv.v)
                        dv.inc(vector.reduce_max(
                            mrow[:, 32 + h:32 + h + 1], knall[:, :], AX))
                        vector.wait_ge(dv.h, dv.v)
                    block.vector(qrd)
                    dv_mid = dv.v

                    def qld2(sync, h=h, dv_mid=dv_mid):
                        sync.wait_ge(dv.h, dv_mid)
                        qKN.inc(sync.dma_start(
                            out=knall[:, :],
                            in_=bass.AP(b.kvg, (4 * CROWS + 2 * D + h) * T,
                                        [[CROWS * T, 4], [1, T]])), 16)
                    block.sync(qld2)
                    qs_h2 = qKN.v

                    def qrd2(vector, h=h, qs_h2=qs_h2):
                        vector.wait_ge(qKN.h, qs_h2)
                        vector.wait_ge(dv.h, dv.v)
                        dv.inc(vector.reduce_max(
                            mrow[:, 160 + h:161 + h], knall[:, :], AX))
                        vector.wait_ge(dv.h, dv.v)
                        dv.inc(vector.tensor_tensor(
                            mrow[:, 32 + h:33 + h], mrow[:, 32 + h:33 + h],
                            mrow[:, 160 + h:161 + h], ALU.max))
                        vector.wait_ge(dv.h, dv.v)
                    block.vector(qrd2)
                    dv_prev = dv.v

                def mkb(vector):
                    vector.wait_ge(dv.h, dv.v)
                    dv.inc(vector.tensor_tensor(
                        bass.AP(b.srow, 96, [[2048, 1], [2, 16], [1, 2]]),
                        bass.AP(b.srow, 0, [[2048, 1], [2, 16], [1, 2]]),
                        bass.AP(b.srow, 32, [[2048, 1], [1, 16], [0, 2]]),
                        ALU.mult))
                    vector.wait_ge(dv.h, dv.v)
                block.vector(mkb)
                dv_m = dv.v

                def mkb2(scalar):
                    scalar.wait_ge(dv.h, dv_m)
                    ac.inc(scalar.activation(
                        mrow[:, 96:128], mrow[:, 96:128], AF.Sqrt))
                block.scalar(mkb2)
                ac_m = ac.v

                def mkb3(vector):
                    vector.wait_ge(ac.h, ac_m)
                    dv.inc(vector.tensor_scalar(
                        mrow[:, 96:128], mrow[:, 96:128], -SCALE, None, ALU.mult))
                    vector.wait_ge(dv.h, dv.v)
                block.vector(mkb3)
                dv_m2 = dv.v

                def mkb4(tensor):
                    tensor.wait_ge(dv.h, dv_m2)
                    pe.inc(tensor.matmul(p_bb[:, :], b.c_or32[:, :],
                                         mrow[0:1, 96:128], start=True, stop=True))
                block.tensor(mkb4)
                pe_m = pe.v

                def mkb5(vector):
                    vector.wait_ge(pe.h, pe_m)
                    dv.inc(vector.tensor_copy(bias_hb[:, :], p_bb[:, :]))
                    vector.wait_ge(dv.h, dv.v)
                block.vector(mkb5)
                qnctx.close()
                barrier()

                flctx = ExitStack()
                pST = [flctx.enter_context(
                    nc.psum_tensor(f"pST{i}", [128, 1024], F32))
                    for i in range(2)]
                pAV = flctx.enter_context(nc.psum_tensor("pAV", [128, 512], F32))
                pL = flctx.enter_context(nc.psum_tensor("pL", [1, 512], F32))
                pLB = flctx.enter_context(nc.psum_tensor("pLB", [128, 512], F32))
                kbuf = [flctx.enter_context(
                    nc.sbuf_tensor(f"kb{i}", [128, 256], BF16)) for i in range(2)]
                vbuf = [flctx.enter_context(
                    nc.sbuf_tensor(f"vb{i}", [128, 256], BF16)) for i in range(2)]
                ptb = [flctx.enter_context(
                    nc.sbuf_tensor(f"pt{i}", [128, 1024], BF16)) for i in range(2)]
                btile = flctx.enter_context(nc.sbuf_tensor("btile", [128, 2], F32))
                linv = flctx.enter_context(nc.sbuf_tensor("linv", [1, 512], F32))
                lbc = flctx.enter_context(nc.sbuf_tensor("lbc", [128, 512], F32))

                def tiles_of(blk):
                    out = []
                    for j in range(2):
                        out.append(("loc", 0, blk * 512 + 256 * j, 22, j))
                    nb = 7 if blk == 0 else 15
                    for bb_ in range(nb):
                        rb = rank_of_block(bb_) * CROWS
                        cb = col_of_block(bb_)
                        mi = bb_ if blk == 0 else 7 + bb_
                        for j in range(2):
                            out.append(("g", rb, cb + 256 * j, mi, None))
                    return out

                for h in range(NH):
                    for blk in range(2):
                        tl = tiles_of(blk)
                        nt = len(tl)
                        dv_ep0 = dv.v
                        pe_blk0 = pe.v
                        kld = [qK[i % 2].v + 16 * (i // 2 + 1)
                               for i in range(nt)]
                        vld = [qV[i % 2].v + 32 * (i // 2 + 1)
                               for i in range(nt)]
                        dv0 = dv.v
                        dvm = []
                        acc_ = dv0
                        for i, t in enumerate(tl):
                            acc_ += 1 + (2 if t[4] is not None else 0)
                            dvm.append(acc_)
                        ac0 = ac.v
                        expm = [ac0 + (i + 1) for i in range(nt)]
                        pe0 = pe.v
                        stm = [pe0 + 2 * (i + 1) - 1 for i in range(nt)]
                        pvm = [pe0 + 2 * (i + 1) for i in range(nt)]
                        stm = [None] * nt
                        pvm = [None] * nt
                        cnt = pe0
                        for i in range(nt):
                            cnt += 1
                            stm[i] = cnt
                            if i >= 1:
                                cnt += 1
                                pvm[i - 1] = cnt
                        cnt += 1
                        pvm[nt - 1] = cnt
                        pe_last = cnt

                        def fsync(sync, tl=tl, kld=kld, h=h):
                            for i, (src, rb, c0, mi, dj) in enumerate(tl):
                                if src == "loc":
                                    sync.wait_ge(qST.h, qs_vst)
                                    ten, roff = b.kvloc, h * 128
                                else:
                                    sync.wait_ge(cc.h, cc_gather)
                                    ten, roff = b.kvg, rb + h * 128
                                if i >= 2:
                                    sync.wait_ge(pe.h, stm[i - 2])
                                else:
                                    sync.wait_ge(pe.h, pe_blk0)
                                qK[i % 2].inc(sync.dma_start(
                                    out=kbuf[i % 2][:, :],
                                    in_=ten[roff:roff + 128, c0:c0 + 256]), 16)
                        block.sync(fsync)
                        assert qK[(nt - 1) % 2].v == kld[-1]

                        def fscal(scalar, tl=tl, vld=vld, kld=kld, h=h,
                                  blk=blk, expm=expm, dvm=dvm, stm=stm,
                                  pvm=pvm, nt=nt):
                            for i in range(nt + 2):
                                if i >= 2:
                                    t = i - 2
                                    scalar.wait_ge(pe.h, stm[t])
                                    scalar.wait_ge(dv.h, dvm[t])
                                    if t >= 2:
                                        scalar.wait_ge(pe.h, pvm[t - 2])
                                    ac.inc(scalar.activation(
                                        ptb[t % 2][:, :], pST[t % 2][:, :],
                                        AF.Exp,
                                        bias=btile[:, t % 2:t % 2 + 1],
                                        scale=SCALE))
                                if i < nt:
                                    src, rb, c0, mi, dj = tl[i]
                                    if src == "loc":
                                        ten, roff = b.kvloc, D + h * 128
                                        scalar.wait_ge(qST.h, qs_vst)
                                    else:
                                        ten, roff = b.kvg, rb + D + h * 128
                                        scalar.wait_ge(cc.h, cc_gather)
                                    if i >= 2:
                                        scalar.wait_ge(pe.h, pvm[i - 2])
                                    else:
                                        scalar.wait_ge(pe.h, pe_blk0)
                                    for j in range(2):
                                        qV[i % 2].inc(scalar.dma_start(
                                            out=vbuf[i % 2][:, j * 128:(j + 1) * 128],
                                            in_=ten[roff:roff + 128,
                                                    c0 + j * 128:c0 + (j + 1) * 128],
                                            transpose=True), 16)
                        block.scalar(fscal)
                        assert ac.v == expm[-1]
                        assert qV[(nt - 1) % 2].v == vld[-1]

                        def fvec(vector, tl=tl, h=h, blk=blk, stm=stm,
                                 expm=expm, nt=nt):
                            for i, (src, rb, c0, mi, dj) in enumerate(tl):
                                if i >= 2:
                                    vector.wait_ge(ac.h, expm[i - 2])
                                dv.inc(vector.tensor_tensor(
                                    btile[:, i % 2:i % 2 + 1],
                                    bias_hb[:, 2 * h + blk:2 * h + blk + 1],
                                    b.c_mask[:, mi:mi + 1], ALU.add))
                                if dj is not None:
                                    vector.wait_ge(pe.h, stm[i])
                                    for j in range(2):
                                        dv.inc(vector.tensor_tensor(
                                            pST[i % 2][:, j * 512:(j + 1) * 512],
                                            pST[i % 2][:, j * 512:(j + 1) * 512],
                                            c_stair[:, (2 * dj + j) * 512:
                                                    (2 * dj + j + 1) * 512],
                                            ALU.add))
                        block.vector(fvec)
                        assert dv.v == dvm[-1]

                        def ftens(tensor, tl=tl, kld=kld, vld=vld, h=h,
                                  blk=blk, expm=expm, dvm=dvm, nt=nt):
                            qslice = b.pA[:, h * T + blk * 512:
                                          h * T + (blk + 1) * 512]
                            for i in range(nt):
                                tensor.wait_ge(qK[i % 2].h, kld[i])
                                if i >= 2:
                                    tensor.wait_ge(ac.h, expm[i - 2])
                                if i == 0:
                                    tensor.wait_ge(dv.h, dv_ep0)
                                for j in range(2):
                                    m = tensor.matmul(
                                        pST[i % 2][:, j * 512:(j + 1) * 512],
                                        kbuf[i % 2][:, j * 128:(j + 1) * 128],
                                        qslice, start=True, stop=True)
                                pe.inc(m)
                                if i >= 1:
                                    t = i - 1
                                    tensor.wait_ge(ac.h, expm[t])
                                    tensor.wait_ge(qV[t % 2].h, vld[t])
                                    for j in range(2):
                                        tensor.matmul(
                                            pAV[:, :],
                                            vbuf[t % 2][:, j * 128:(j + 1) * 128],
                                            ptb[t % 2][:, j * 512:(j + 1) * 512],
                                            start=(t == 0 and j == 0),
                                            stop=(t == nt - 1 and j == 1))
                                    for j in range(2):
                                        m = tensor.matmul(
                                            pL[:, :], b.c_oc[:, :],
                                            ptb[t % 2][:, j * 512:(j + 1) * 512],
                                            start=(t == 0 and j == 0),
                                            stop=(t == nt - 1 and j == 1))
                                    pe.inc(m)
                            t = nt - 1
                            tensor.wait_ge(ac.h, expm[t])
                            tensor.wait_ge(qV[t % 2].h, vld[t])
                            for j in range(2):
                                tensor.matmul(
                                    pAV[:, :], vbuf[t % 2][:, j * 128:(j + 1) * 128],
                                    ptb[t % 2][:, j * 512:(j + 1) * 512],
                                    start=False, stop=(j == 1))
                            for j in range(2):
                                m = tensor.matmul(
                                    pL[:, :], b.c_oc[:, :],
                                    ptb[t % 2][:, j * 512:(j + 1) * 512],
                                    start=False, stop=(j == 1))
                            pe.inc(m)
                        block.tensor(ftens)
                        assert pe.v == pe_last

                        def fep1(vector, h=h, blk=blk, pl=pe_last):
                            vector.wait_ge(pe.h, pl)
                            dv.inc(vector.reciprocal(linv[:, :], pL[:, :]))
                            vector.wait_ge(dv.h, dv.v)
                        block.vector(fep1)
                        dv_r = dv.v

                        def fep2(tensor, h=h, blk=blk):
                            tensor.wait_ge(dv.h, dv_r)
                            pe.inc(tensor.matmul(
                                pLB[:, :], b.c_or32[:, :], linv[:, :],
                                start=True, stop=True))
                        block.tensor(fep2)
                        pe_b = pe.v

                        def fep3(vector, h=h, blk=blk):
                            vector.wait_ge(pe.h, pe_b)
                            dv.inc(vector.tensor_copy(lbc[:, :], pLB[:, :]))
                            vector.wait_ge(dv.h, dv.v)
                            dv.inc(vector.tensor_tensor(
                                b.pB[:, h * T + blk * 512:
                                     h * T + (blk + 1) * 512],
                                pAV[:, :], lbc[:, :], ALU.mult))
                            vector.wait_ge(dv.h, dv.v)
                        block.vector(fep3)

                flctx.close()
                barrier()

                emit_proj(ph, "op", b.wo_b, 0, NC, NC, 2, (0, 1),
                          lambda dc, tt: b.pB[:, dc * T + tt * 512:
                                              dc * T + (tt + 1) * 512],
                          EvacResid(b.c_bqkvo, 48, T))
                barrier()

        def hsl(ec):
            if ec < 32:
                return b.pA[:, ec * 512:(ec + 1) * 512]
            return b.pB[:, (ec - 32) * 512:(ec - 31) * 512]

        def emit_ffn(tag):
            with ExitStack() as ph:
                emit_ln(ph, double=False)
                csb = ph.enter_context(nc.sbuf_tensor(f"{tag}csb", [128, 512], F32))
                tmp2 = [ph.enter_context(
                    nc.sbuf_tensor(f"{tag}tp{i}", [128, 512], F32))
                    for i in range(2)]

                for tt in range(2):
                    csctx = ExitStack()
                    p_cs = csctx.enter_context(
                        nc.psum_tensor(f"{tag}pcs{tt}", [1, 512], F32))
                    p_cb = csctx.enter_context(
                        nc.psum_tensor(f"{tag}pcb{tt}", [128, 512], F32))
                    def cs1(tensor, tt=tt):
                        for c in range(NC):
                            m = tensor.matmul(
                                p_cs[:, :], b.c_oc[:, :],
                                b.xn_sb[:, c * T + tt * 512:c * T + (tt + 1) * 512],
                                start=(c == 0), stop=(c == NC - 1))
                        pe.inc(m)
                    block.tensor(cs1)
                    pe_cs = pe.v

                    def cs2(vector, tt=tt):
                        vector.wait_ge(pe.h, pe_cs)
                        dv.inc(vector.tensor_scalar(
                            b.srow[:, 0:512], p_cs[:, :], 128.0, None, ALU.mult))
                        vector.wait_ge(dv.h, dv.v)
                    block.vector(cs2)
                    dv_cs = dv.v

                    def cs3(tensor, tt=tt):
                        tensor.wait_ge(dv.h, dv_cs)
                        pe.inc(tensor.matmul(p_cb[:, :], b.c_or32[:, :],
                                             b.srow[0:1, 0:512],
                                             start=True, stop=True))
                    block.tensor(cs3)
                    pe_cb = pe.v

                    def cs4(vector, tt=tt):
                        vector.wait_ge(pe.h, pe_cb)
                        dv.inc(vector.tensor_copy(csb[:, :], p_cb[:, :]))
                        vector.wait_ge(dv.h, dv.v)
                    block.vector(cs4)
                    csctx.close()
                    barrier()

                    class EvacG:
                        ndv, nac = 1, 1

                        def __init__(self):
                            self.dv0 = dv.v
                            self.ac0 = ac.v

                        def dv(self, vector, ec, pap):
                            vector.wait_ge(dv.h, dv.v)
                            if ec >= 2:
                                vector.wait_ge(ac.h, self.ac0 + ec - 1)
                            dv.inc(vector.tensor_tensor(
                                tmp2[ec % 2][:, :], pap[:, 0:512], csb[:, :],
                                ALU.subtract))

                        def ac(self, scalar, ec, pap):
                            scalar.wait_ge(dv.h, self.dv0 + ec + 1)
                            ac.inc(scalar.activation(
                                hsl(ec), tmp2[ec % 2][:, :], AF.Silu,
                                scale=b.c_gsc[:, ec:ec + 1]))
                    emit_proj(ph, f"{tag}g{tt}", b.gwT, 0, NC, NCI, 4, (tt,),
                              lambda dc, _tt: b.xn_sb[
                                  :, dc * T + tt * 512:dc * T + (tt + 1) * 512],
                              EvacG())
                    barrier()

                    class EvacU:
                        ndv, nac = 2, 0

                        def dv(self, vector, ec, pap):
                            vector.wait_ge(dv.h, dv.v)
                            dv.inc(vector.tensor_tensor(
                                tmp2[ec % 2][:, :], pap[:, 0:512], csb[:, :],
                                ALU.subtract))
                            vector.wait_ge(dv.h, dv.v)
                            dv.inc(vector.scalar_tensor_tensor(
                                hsl(ec), tmp2[ec % 2][:, :],
                                b.c_usc[:, ec:ec + 1], hsl(ec),
                                ALU.mult, ALU.mult))
                    emit_proj(ph, f"{tag}u{tt}", b.uwT, 0, NC, NCI, 4, (tt,),
                              lambda dc, _tt: b.xn_sb[
                                  :, dc * T + tt * 512:dc * T + (tt + 1) * 512],
                              EvacU())
                    barrier()

                    csctx2 = ExitStack()
                    p_cs = csctx2.enter_context(
                        nc.psum_tensor(f"{tag}pch{tt}", [1, 512], F32))
                    p_cb = csctx2.enter_context(
                        nc.psum_tensor(f"{tag}pcb2{tt}", [128, 512], F32))
                    def ch1(tensor, tt=tt):
                        for c in range(NCI):
                            m = tensor.matmul(
                                p_cs[:, :], b.c_oc[:, :], hsl(c),
                                start=(c == 0), stop=(c == NCI - 1))
                        pe.inc(m)
                    block.tensor(ch1)
                    pe_ch = pe.v

                    def ch2(vector, tt=tt):
                        vector.wait_ge(pe.h, pe_ch)
                        dv.inc(vector.tensor_scalar(
                            b.srow[:, 0:512], p_cs[:, :], 128.0, None, ALU.mult))
                        vector.wait_ge(dv.h, dv.v)
                    block.vector(ch2)
                    dv_ch = dv.v

                    def ch3(tensor, tt=tt):
                        tensor.wait_ge(dv.h, dv_ch)
                        pe.inc(tensor.matmul(p_cb[:, :], b.c_or32[:, :],
                                             b.srow[0:1, 0:512],
                                             start=True, stop=True))
                    block.tensor(ch3)
                    pe_cb2 = pe.v

                    def ch4(vector, tt=tt):
                        vector.wait_ge(pe.h, pe_cb2)
                        dv.inc(vector.tensor_copy(csb[:, :], p_cb[:, :]))
                        vector.wait_ge(dv.h, dv.v)
                    block.vector(ch4)
                    csctx2.close()
                    barrier()

                    class EvacD:
                        ndv, nac = 2, 0

                        def __init__(self, tt):
                            self.tt = tt

                        def dv(self, vector, ec, pap):
                            vector.wait_ge(dv.h, dv.v)
                            dv.inc(vector.tensor_tensor(
                                tmp2[ec % 2][:, :], pap[:, 0:512], csb[:, :],
                                ALU.subtract))
                            vector.wait_ge(dv.h, dv.v)
                            rsl = b.r_sb[:, ec * T + self.tt * 512:
                                         ec * T + (self.tt + 1) * 512]
                            dv.inc(vector.scalar_tensor_tensor(
                                rsl, tmp2[ec % 2][:, :],
                                b.c_dsc[:, ec:ec + 1], rsl,
                                ALU.mult, ALU.add))
                    emit_proj(ph, f"{tag}d{tt}", b.dwT, 0, NCI, NC, 4, (tt,),
                              lambda dc, _tt: hsl(dc),
                              EvacD(tt))
                    barrier()

        if "s2" in STAGES:
            emit_ffn("f2")

        if "s3" in STAGES:
            with ExitStack() as ph:
                emit_ln(ph, double=False)
                cs_sb = ph.enter_context(nc.sbuf_tensor("s3cs", [128, T], BF16))
                sn_sb = ph.enter_context(nc.sbuf_tensor("s3sn", [128, T], BF16))
                tmpf = ph.enter_context(nc.sbuf_tensor("s3tf", [128, T], F32))
                invc = ph.enter_context(nc.sbuf_tensor("s3inv", [128, 8], BF16))
                acol = ph.enter_context(nc.sbuf_tensor("s3acol", [128, 16], F32))
                invrow = ph.enter_context(nc.sbuf_tensor("s3ivr", [1, T], BF16))
                kvtok = ph.enter_context(nc.sbuf_tensor("s3kvt", [128, 2048], BF16))
                vstage = [ph.enter_context(
                    nc.sbuf_tensor(f"s3vs{i}", [128, T], BF16)) for i in range(2)]
                arow = b.srow[0:1, 0:2048]

                emit_proj(ph, "bk", b.winT, D, NC, NC, 2, (0, 1),
                          lambda dc, tt: b.xn_sb[:, dc * T + tt * 512:
                                                 dc * T + (tt + 1) * 512],
                          EvacBias(b.pA, 0, b.c_bin3, 16, T))
                barrier()

                b.rp_id = [0]

                def emit_rope(buf):
                    b.rp_id[0] += 1
                    rpctx = ExitStack()
                    p_rh = rpctx.enter_context(
                        nc.psum_tensor(f"p_rh{b.rp_id[0]}", [128, T], F32))
                    dv_prev = dv.v
                    for c in range(8):
                        def rl(sync, c=c, dv_prev=dv_prev):
                            if c > 0:
                                sync.wait_ge(dv.h, dv_prev)
                            qRP.inc(sync.dma_start(
                                out=cs_sb[:, :],
                                in_=b.cost[c * 128:(c + 1) * 128, :]), 16)
                            qRP.inc(sync.dma_start(
                                out=sn_sb[:, :],
                                in_=b.sint[c * 128:(c + 1) * 128, :]), 16)
                        block.sync(rl)
                        qs_c = qRP.v

                        def rp(tensor, c=c, dv_prev=dv_prev):
                            if c > 0:
                                tensor.wait_ge(dv.h, dv_prev)
                            for tt in range(2):
                                m = tensor.matmul(
                                    p_rh[:, tt * 512:(tt + 1) * 512],
                                    b.c_perm[:, :],
                                    buf[:, c * T + tt * 512:c * T + (tt + 1) * 512],
                                    start=True, stop=True)
                            pe.inc(m)
                        block.tensor(rp)
                        pe_c = pe.v

                        def rv(vector, c=c):
                            vector.wait_ge(qRP.h, qs_c)
                            vector.wait_ge(pe.h, pe_c)
                            vector.wait_ge(dv.h, dv.v)
                            dv.inc(vector.tensor_tensor(
                                tmpf[:, :], buf[:, c * T:(c + 1) * T],
                                cs_sb[:, :], ALU.mult))
                            vector.wait_ge(dv.h, dv.v)
                            dv.inc(vector.tensor_tensor(
                                p_rh[:, :], p_rh[:, :], sn_sb[:, :], ALU.mult))
                            vector.wait_ge(dv.h, dv.v)
                            dv.inc(vector.tensor_tensor(
                                buf[:, c * T:(c + 1) * T], tmpf[:, :],
                                p_rh[:, :], ALU.add))
                            vector.wait_ge(dv.h, dv.v)
                        block.vector(rv)
                        dv_prev = dv.v
                    rpctx.close()
                    barrier()
                emit_rope(b.pA)

                sq3 = vstage[0]
                knctx3 = ExitStack()
                p_kn = knctx3.enter_context(nc.psum_tensor("s3pkn", [1, T], F32))
                pe_prev = pe.v
                for c in range(NC):
                    def k2(scalar, c=c, pe_prev=pe_prev):
                        scalar.wait_ge(pe.h, pe_prev)
                        ac.inc(scalar.activation(
                            sq3[:, :], b.pA[:, c * T:(c + 1) * T], AF.Square))
                    block.scalar(k2)
                    ac_c = ac.v

                    def k2m(tensor, c=c, ac_c=ac_c):
                        tensor.wait_ge(ac.h, ac_c)
                        for tt in range(2):
                            m = tensor.matmul(
                                p_kn[:, tt * 512:(tt + 1) * 512], b.c_oc[:, :],
                                sq3[:, tt * 512:(tt + 1) * 512],
                                start=(c == 0), stop=(c == NC - 1))
                        pe.inc(m)
                    block.tensor(k2m)
                    pe_prev = pe.v
                barrier()

                def inv1(scalar):
                    ac.inc(scalar.activation(
                        b.srow[:, 0:T], p_kn[:, :], AF.Sqrt))
                block.scalar(inv1)
                ac_i = ac.v

                def inv2(vector):
                    vector.wait_ge(ac.h, ac_i)
                    dv.inc(vector.tensor_scalar(
                        b.srow[:, 0:T], b.srow[:, 0:T], 1e-12, None, ALU.max))
                    vector.wait_ge(dv.h, dv.v)
                    dv.inc(vector.reciprocal(b.srow[:, T:2 * T], b.srow[:, 0:T]))
                    vector.wait_ge(dv.h, dv.v)
                    dv.inc(vector.tensor_copy(invrow[:, :], b.srow[:, T:2 * T]))
                    vector.wait_ge(dv.h, dv.v)
                block.vector(inv2)
                dv_i = dv.v

                barrier()
                knctx3.close()
                trctx = ExitStack()
                p_tr32b = trctx.enter_context(
                    nc.psum_tensor("s3ptrb", [128, 8], F32))
                for j in range(8):
                    def trj(tensor, j=j):
                        pe.inc(tensor.matmul(
                            p_tr32b[:, j:j + 1],
                            invrow[0:1, j * 128:(j + 1) * 128],
                            b.c_oc[0:1, 0:1], start=True, stop=True))
                    block.tensor(trj)
                    pe_j = pe.v

                    def cvj(vector, j=j):
                        vector.wait_ge(pe.h, pe_j)
                        dv.inc(vector.tensor_copy(
                            invc[:, j:j + 1], p_tr32b[:, j:j + 1]))
                        vector.wait_ge(dv.h, dv.v)
                    block.vector(cvj)
                trctx.close()
                barrier()

                class EvacV2:
                    ndv, nac = 1, 1

                    def __init__(self):
                        self.ac0 = ac.v
                        self.dv0 = dv.v

                    def ac(self, scalar, ec, pap):
                        if ec >= 2:
                            scalar.wait_ge(dv.h, self.dv0 + ec - 1)
                        ac.inc(scalar.activation(
                            vstage[ec % 2][:, :], pap[:, 0:T], AF.Identity,
                            bias=b.c_bin3[:, 32 + ec:33 + ec]))

                    def dv(self, vector, ec, pap):
                        vector.wait_ge(ac.h, self.ac0 + ec + 1)
                        dv.inc(vector.tensor_tensor(
                            b.pA[:, ec * T:(ec + 1) * T],
                            b.pA[:, ec * T:(ec + 1) * T],
                            vstage[ec % 2][:, :], ALU.mult))
                emit_proj(ph, "bv", b.winT, 2 * D, NC, NC, 2, (0, 1),
                          lambda dc, tt: b.xn_sb[:, dc * T + tt * 512:
                                                 dc * T + (tt + 1) * 512],
                          EvacV2())
                barrier()

                actx = ExitStack()
                p_tr = actx.enter_context(
                    nc.psum_tensor("s3ptr2", [128, 128], BF16))
                p_a = actx.enter_context(nc.psum_tensor("s3pa", [1, 2048], F32))
                amm_prev = pe.v
                for tc in range(8):
                    dv_ckv_prev = dv.v
                    for c in range(NC):
                        def tkv(tensor, tc=tc, c=c, amm_prev=amm_prev):
                            tensor.wait_ge(dv.h, dv.v)
                            if c == 0:
                                tensor.wait_ge(pe.h, amm_prev)
                            pe.inc(tensor.transpose(
                                p_tr[:, :],
                                b.pA[:, c * T + tc * 128:c * T + (tc + 1) * 128],
                                b.c_id[:, :]))
                        block.tensor(tkv)
                        pe_t = pe.v

                        def ckv(vector, tc=tc, c=c, pe_t=pe_t):
                            vector.wait_ge(pe.h, pe_t)
                            dv.inc(vector.tensor_copy(
                                kvtok[:, c * 128:(c + 1) * 128], p_tr[:, :]))
                            vector.wait_ge(dv.h, dv.v)
                        block.vector(ckv)
                    dv_t = dv.v

                    def amm(tensor, tc=tc, dv_t=dv_t):
                        tensor.wait_ge(dv.h, dv_t)
                        for j in range(4):
                            m = tensor.matmul(
                                p_a[:, j * 512:(j + 1) * 512],
                                invc[:, tc:tc + 1],
                                kvtok[:, j * 512:(j + 1) * 512],
                                start=(tc == 0), stop=(tc == 7))
                        pe.inc(m)
                    block.tensor(amm)
                    amm_prev = pe.v
                barrier()

                def aev(vector):
                    vector.wait_ge(pe.h, pe.v)
                    dv.inc(vector.tensor_copy(arow[:, :], p_a[:, :]))
                    vector.wait_ge(dv.h, dv.v)
                block.vector(aev)
                dv_a = dv.v

                def ast(g):
                    g.wait_ge(dv.h, dv_a)
                    qG2.inc(g.dma_start(out=b.abuf_i[:, :], in_=arow[:, :]), 16)
                    g.wait_ge(qG2.h, qG2.v)
                    cc.inc(g.collective_compute(
                        "AllReduce", ALU.add,
                        replica_groups=[list(range(NCORES))],
                        ins=[b.abuf_i.ap().opt()], outs=[b.abuf_o.ap().opt()]))
                    g.wait_ge(cc.h, cc.v)
                    qG2.inc(g.dma_start(out=arow[:, :], in_=b.abuf_o[:, :]), 16)
                    g.wait_ge(qG2.h, qG2.v)
                block.gpsimd(ast)
                actx.close()
                barrier()

                acctx = ExitStack()
                p_tr32 = acctx.enter_context(
                    nc.psum_tensor("s3ptr32", [128, 16], F32))
                for j in range(NC):
                    def taj(tensor, j=j):
                        pe.inc(tensor.matmul(
                            p_tr32[:, j:j + 1],
                            arow[0:1, j * 128:(j + 1) * 128],
                            b.c_or32[0:1, 0:1], start=True, stop=True))
                    block.tensor(taj)
                    pe_j = pe.v

                    def caj(vector, j=j):
                        vector.wait_ge(pe.h, pe_j)
                        dv.inc(vector.tensor_copy(
                            acol[:, j:j + 1], p_tr32[:, j:j + 1]))
                        vector.wait_ge(dv.h, dv.v)
                    block.vector(caj)
                acctx.close()
                barrier()

                emit_proj(ph, "bq", b.winT, 0, NC, NC, 2, (0, 1),
                          lambda dc, tt: b.xn_sb[:, dc * T + tt * 512:
                                                 dc * T + (tt + 1) * 512],
                          EvacBias(b.pB, 0, b.c_bin3, 0, T))
                barrier()
                emit_rope(b.pB)

                def yq(vector):
                    for c in range(NC):
                        dv.inc(vector.tensor_scalar(
                            b.pB[:, c * T:(c + 1) * T],
                            b.pB[:, c * T:(c + 1) * T],
                            acol[:, c:c + 1], None, ALU.mult))
                        vector.wait_ge(dv.h, dv.v)
                block.vector(yq)
                barrier()

                class EvacWO:
                    ndv, nac = 2, 0

                    def dv(self, vector, ec, pap):
                        rsl = b.r_sb[:, ec * T:(ec + 1) * T]
                        xsl = b.xn_sb[:, ec * T:(ec + 1) * T]
                        vector.wait_ge(dv.h, dv.v)
                        dv.inc(vector.scalar_tensor_tensor(
                            tmpf[:, :], pap[:, 0:T],
                            b.c_bout1[:, ec:ec + 1], xsl,
                            ALU.add, ALU.add))
                        vector.wait_ge(dv.h, dv.v)
                        dv.inc(vector.tensor_tensor(
                            rsl, rsl, tmpf[:, :], ALU.add))
                emit_proj(ph, "bo", b.woutT, 0, NC, NC, 2, (0, 1),
                          lambda dc, tt: b.pB[:, dc * T + tt * 512:
                                              dc * T + (tt + 1) * 512],
                          EvacWO())
                barrier()

        if "s4" in STAGES:
            emit_ffn("f4")

        def fin(sync):
            for j in range(4):
                qS.inc(sync.dma_start(
                    out=bass.AP(b.yt, j * 4 * 128 * T,
                                [[T, 128], [128 * T, 4], [1, T]]),
                    in_=b.r_sb[:, j * 4 * T:(j + 1) * 4 * T]), 16)
            sync.wait_ge(qS.h, qS.v)
        block.sync(fin)

    return nc


_NC_CACHE = {}


def _get_nc():
    key = STAGES
    if key not in _NC_CACHE:
        _NC_CACHE[key] = build_nc()
    return _NC_CACHE[key]


def _host_consts():
    rot = 1024
    half = rot // 2
    invf = 1.0 / (10000.0 ** (np.arange(0, rot, 2, dtype=np.float64) / rot))
    pos = np.arange(S, dtype=np.float64)
    fr = pos[:, None] * invf[None, :]
    fr = np.repeat(fr, 2, axis=1)
    cosf = np.cos(fr).astype(np.float32)
    sinf = np.sin(fr).astype(np.float32)
    stair = np.where(np.arange(512)[None, :] >= np.arange(512)[:, None],
                     0.0, -1e9).astype(np.float32).astype(bfnp)
    ident = np.eye(128, dtype=np.float32).astype(bfnp)
    ident32 = np.eye(128, dtype=np.float32)
    onescol = np.ones((128, 1), np.float32).astype(bfnp)
    onesrow32 = np.ones((1, 128), np.float32)
    perm = np.zeros((128, 128), np.float32)
    for jj in range(64):
        perm[2 * jj + 1, 2 * jj] = -1.0
        perm[2 * jj, 2 * jj + 1] = 1.0
    perm = perm.astype(bfnp)
    return cosf, sinf, stair, ident, ident32, onescol, onesrow32, perm


def _chunkcols(vec):
    return np.ascontiguousarray(np.asarray(vec, np.float32).reshape(-1, 128).T)


def kernel(**inputs):
    nc = _get_nc()
    cosf, sinf, stair, ident, ident32, onescol, onesrow32, perm = _host_consts()

    X = np.asarray(inputs["L0_attn_a_X"], np.float32).reshape(S, D)
    shared = dict(
        stair=stair, ident=ident, ident32=ident32, onescol=onescol,
        onesrow32=onesrow32, perm=perm,
        lngb=np.concatenate([_chunkcols(inputs["L0_attn_a_ln_g"]),
                             _chunkcols(inputs["L0_attn_a_ln_b"])], axis=1),
        bqkvo=np.concatenate([_chunkcols(inputs["L0_attn_a_bq"]),
                              _chunkcols(inputs["L0_attn_a_bk"]),
                              _chunkcols(inputs["L0_attn_a_bv"]),
                              _chunkcols(inputs["L0_attn_a_bo"])], axis=1),
        bin3=_chunkcols(inputs["L0_attn_b_b_in"]),
        bout1=_chunkcols(inputs["L0_attn_b_b_out"]),
        gsc=_chunkcols(inputs["L0_ffn_gate_scale"]),
        usc=_chunkcols(inputs["L0_ffn_up_scale"]),
        dsc=_chunkcols(inputs["L0_ffn_down_scale"]),
        wq=np.asarray(inputs["L0_attn_a_Wq"], np.float32),
        wk=np.asarray(inputs["L0_attn_a_Wk"], np.float32),
        wv=np.asarray(inputs["L0_attn_a_Wv"], np.float32),
        wo=np.asarray(inputs["L0_attn_a_Wo"], np.float32),
        win=np.asarray(inputs["L0_attn_b_W_in"], np.float32),
        wout=np.asarray(inputs["L0_attn_b_W_out"], np.float32),
        gw=np.ascontiguousarray(np.asarray(inputs["L0_ffn_gate_w"], np.int32)),
        uw=np.ascontiguousarray(np.asarray(inputs["L0_ffn_up_w"], np.int32)),
        dw=np.ascontiguousarray(np.asarray(inputs["L0_ffn_down_w"], np.int32)),
    )

    in_maps = []
    for c in range(NCORES):
        bA, bB = c, 15 - c
        toks = np.concatenate([np.arange(bA * 512, (bA + 1) * 512),
                               np.arange(bB * 512, (bB + 1) * 512)])
        xt = np.ascontiguousarray(X[toks, :].T)
        cost = np.ascontiguousarray(cosf[toks, :].T).astype(bfnp)
        sint = np.ascontiguousarray(sinf[toks, :].T).astype(bfnp)
        mask = np.zeros((128, 23), np.float32)
        for bb_ in range(7):
            if bb_ > c - 1:
                mask[:, bb_] = NEG
        for bb_ in range(15):
            if bb_ > 14 - c:
                mask[:, 7 + bb_] = NEG
        m = dict(shared)
        m.update(xt=xt, cost=cost, sint=sint, maskcol=mask)
        in_maps.append(m)

    res = run_bass_kernel_spmd(nc, in_maps, core_ids=list(range(NCORES)))

    out = np.zeros((1, S, D), np.float32)
    for c in range(NCORES):
        ytc = np.asarray(res.results[c]["yt"], np.float32)
        bA, bB = c, 15 - c
        out[0, bA * 512:(bA + 1) * 512, :] = ytc[:, 0:512].T
        out[0, bB * 512:(bB + 1) * 512, :] = ytc[:, 512:1024].T
    return out


# version 06
# speedup vs baseline: 1.1859x; 1.1859x over previous
import math
from contextlib import ExitStack

import numpy as np
import ml_dtypes

import concourse.bass as bass
import concourse.mybir as mybir
from concourse.bass_utils import run_bass_kernel_spmd

F32 = mybir.dt.float32
BF16 = mybir.dt.bfloat16
I32 = mybir.dt.int32
AF = mybir.ActivationFunctionType
ALU = mybir.AluOpType
AX = mybir.AxisListType.X
bfnp = ml_dtypes.bfloat16

D = 2048
S = 8192
T = 1024
NH = 16
NC = 16
FI = 8192
NCI = 64
E3 = 6144
SCALE = 1.0 / math.sqrt(128.0)
EPS = 1e-5
NCORES = 8
CROWS = 2 * D + NH
NEG = -1.0e30

STAGES = ("s1", "s2", "s3", "s4")


def rank_of_block(b):
    return b if b <= 7 else 15 - b


def col_of_block(b):
    return 0 if b <= 7 else 512


class Sem:
    def __init__(self, ctx, nc, name):
        self.h = ctx.enter_context(nc.semaphore(name))
        self.v = 0

    def inc(self, instr, n=1):
        instr.then_inc(self.h, n)
        self.v += n
        return self.v


class B:
    pass


def build_nc():
    nc = bass.Bass("TRN2", target_bir_lowering=False)
    b = B()
    b.nc = nc

    def di(name, shape, dt):
        return nc.dram_tensor(name, shape, dt, kind="ExternalInput")

    b.xt = di("xt", [D, T], F32)
    b.cost = di("cost", [1024, T], BF16)
    b.sint = di("sint", [1024, T], BF16)
    b.stair = di("stair", [512, 512], BF16)
    b.maskcol = di("maskcol", [128, 23], F32)
    b.lngb = di("lngb", [128, 32], F32)
    b.bqkvo = di("bqkvo", [128, 64], F32)
    b.bin3 = di("bin3", [128, 48], F32)
    b.bout1 = di("bout1", [128, 16], F32)
    b.gsc = di("gsc", [128, 64], F32)
    b.usc = di("usc", [128, 64], F32)
    b.dsc = di("dsc", [128, 16], F32)
    b.ident = di("ident", [128, 128], BF16)
    b.ident32 = di("ident32", [128, 128], F32)
    b.onescol = di("onescol", [128, 1], BF16)
    b.onesrow32 = di("onesrow32", [1, 128], F32)
    b.onescol32 = di("onescol32", [128, 1], F32)
    b.perm = di("perm", [128, 128], BF16)
    b.wq = di("wq", [D, D], F32)
    b.wk = di("wk", [D, D], F32)
    b.wv = di("wv", [D, D], F32)
    b.wo = di("wo", [D, D], F32)
    b.win = di("win", [E3, D], F32)
    b.wout = di("wout", [D, D], F32)
    b.gw = di("gw", [FI, D], I32)
    b.uw = di("uw", [FI, D], I32)
    b.dw = di("dw", [D, FI], I32)

    b.yt = nc.dram_tensor("yt", [D, T], F32, kind="ExternalOutput")

    b.wq_b = nc.dram_tensor("wq_b", [D, D], BF16)
    b.wk_b = nc.dram_tensor("wk_b", [D, D], BF16)
    b.wv_b = nc.dram_tensor("wv_b", [D, D], BF16)
    b.wo_b = nc.dram_tensor("wo_b", [D, D], BF16)
    b.win_c = nc.dram_tensor("win_c", [E3, D], BF16)
    b.wout_c = nc.dram_tensor("wout_c", [D, D], BF16)
    b.gw_c = nc.dram_tensor("gw_c", [FI, D], BF16)
    b.uw_c = nc.dram_tensor("uw_c", [FI, D], BF16)
    b.dw_c = nc.dram_tensor("dw_c", [D, FI], BF16)
    b.winT = nc.dram_tensor("winT", [D, E3], BF16)
    b.woutT = nc.dram_tensor("woutT", [D, D], BF16)
    b.gwT = nc.dram_tensor("gwT", [D, FI], BF16)
    b.uwT = nc.dram_tensor("uwT", [D, FI], BF16)
    b.dwT = nc.dram_tensor("dwT", [FI, D], BF16)
    b.kvloc = nc.dram_tensor("kvloc", [CROWS, T], BF16)
    b.kvg = nc.dram_tensor("kvg", [NCORES * CROWS, T], BF16)
    b.vtokg = nc.dram_tensor("vtokg", [S, D], BF16)
    b.vtokl = nc.dram_tensor("vtokl", [T, D], BF16)
    b.abuf_i = nc.dram_tensor("abuf_i", [1, D], F32)
    b.abuf_o = nc.dram_tensor("abuf_o", [1, D], F32)

    ctx = ExitStack()
    with ctx:
        b.qS = Sem(ctx, nc, "qS")
        b.cc = Sem(ctx, nc, "cc")
        b.pe = Sem(ctx, nc, "pe")
        b.dv = Sem(ctx, nc, "dv")
        b.ac = Sem(ctx, nc, "ac")
        b.bars = {n: Sem(ctx, nc, f"bar_{n}")
                  for n in ("sync", "tensor", "vector", "scalar", "gpsimd")}
        b.qW = [Sem(ctx, nc, f"qW{i}") for i in range(4)]
        b.qK = [Sem(ctx, nc, f"qK{i}") for i in range(2)]
        b.qV = [Sem(ctx, nc, f"qV{i}") for i in range(2)]
        b.qTA = [Sem(ctx, nc, f"qTA{i}") for i in range(4)]
        b.qTS = [Sem(ctx, nc, f"qTS{i}") for i in range(4)]
        b.qST = Sem(ctx, nc, "qST")
        b.qRP = Sem(ctx, nc, "qRP")
        b.qKN = Sem(ctx, nc, "qKN")
        b.qSTR = Sem(ctx, nc, "qSTR")
        b.qAB = Sem(ctx, nc, "qAB")
        b.qC = {n: Sem(ctx, nc, f"qC_{n}")
                for n in ("wk", "wv", "wq", "wo", "gw", "uw", "dw",
                          "win", "wout")}

        def sbt(name, shape, dt):
            return ctx.enter_context(nc.sbuf_tensor(name, shape, dt))

        b.r_sb = sbt("r_sb", [128, NC * T], F32)
        b.xn_sb = sbt("xn_sb", [128, NC * T], BF16)
        b.pA = sbt("pA", [128, NC * T], BF16)
        b.pB = sbt("pB", [128, NC * T], BF16)
        b.stat = sbt("stat", [1, 3 * T], F32)
        b.srow = sbt("srow", [1, 2 * T], F32)
        b.c_lngb = sbt("c_lngb", [128, 32], F32)
        b.c_bqkvo = sbt("c_bqkvo", [128, 64], F32)
        b.c_bin3 = sbt("c_bin3", [128, 48], F32)
        b.c_bout1 = sbt("c_bout1", [128, 16], F32)
        b.c_gsc = sbt("c_gsc", [128, 64], F32)
        b.c_usc = sbt("c_usc", [128, 64], F32)
        b.c_dsc = sbt("c_dsc", [128, 16], F32)
        b.c_mask = sbt("c_mask", [128, 23], F32)
        b.c_id = sbt("c_id", [128, 128], BF16)
        b.c_id32 = sbt("c_id32", [128, 128], F32)
        b.c_oc = sbt("c_oc", [128, 1], BF16)
        b.c_or32 = sbt("c_or32", [1, 128], F32)
        b.c_oc32 = sbt("c_oc32", [128, 1], F32)
        b.c_perm = sbt("c_perm", [128, 128], BF16)

        b.block = ctx.enter_context(nc.Block())
        block = b.block
        qS, cc, pe, dv, ac = b.qS, b.cc, b.pe, b.dv, b.ac
        bars, qW, qK, qV = b.bars, b.qW, b.qK, b.qV
        qTA, qTS, qST, qRP, qKN = b.qTA, b.qTS, b.qST, b.qRP, b.qKN
        qSTR, qAB, qC = b.qSTR, b.qAB, b.qC

        def barrier(gp=True):
            drains = {
                "sync": [(s, s.v) for s in
                         [qS, qST, qRP, qKN, qSTR] + qW + qTA if s.v > 0],
                "scalar": ([(ac, ac.v)] if ac.v else []) +
                          [(s, s.v) for s in qV if s.v > 0],
                "gpsimd": [(s, s.v) for s in
                           [cc, qAB] + qTS + list(qC.values()) if s.v > 0],
                "tensor": [(pe, pe.v)] if pe.v else [],
                "vector": [(dv, dv.v)] if dv.v else [],
            }
            names = ["sync", "tensor", "vector", "scalar"] + (
                ["gpsimd"] if gp else [])
            for n in names:
                bars[n].v += 1
            waits = [(bars[n].h, bars[n].v) for n in names]

            def mk(n):
                def f(e):
                    for s, v in drains[n]:
                        e.wait_ge(s.h, v)
                    e.nop().then_inc(bars[n].h, 1)
                    for hh, vv in waits:
                        e.wait_ge(hh, vv)
                return f
            for n in names:
                getattr(block, n)(mk(n))

        def gate_p0():
            bars["sync"].v += 1
            k = bars["sync"].v

            def g_sync(e):
                e.wait_ge(qS.h, qS.v)
                e.nop().then_inc(bars["sync"].h, 1)
            block.sync(g_sync)

            def g_wait(e):
                e.wait_ge(bars["sync"].h, k)
            for en in ("tensor", "vector", "scalar", "gpsimd"):
                getattr(block, en)(g_wait)

        def p0(sync):
            for d, s in [
                (b.c_lngb, b.lngb), (b.c_bqkvo, b.bqkvo), (b.c_bin3, b.bin3),
                (b.c_bout1, b.bout1), (b.c_gsc, b.gsc), (b.c_usc, b.usc),
                (b.c_dsc, b.dsc), (b.c_mask, b.maskcol), (b.c_id, b.ident),
                (b.c_id32, b.ident32), (b.c_oc, b.onescol),
                (b.c_or32, b.onesrow32), (b.c_oc32, b.onescol32),
                (b.c_perm, b.perm),
            ]:
                qS.inc(sync.dma_start(out=d[:, :], in_=s[:, :]), 16)
            for j in range(4):
                qS.inc(sync.dma_start(
                    out=b.r_sb[:, j * 4 * T:(j + 1) * 4 * T],
                    in_=bass.AP(b.xt, j * 4 * 128 * T,
                                [[T, 128], [128 * T, 4], [1, T]])), 16)
        block.sync(p0)
        gate_p0()

        cast_done = {}

        def p1_g(g):
            jobs = [
                ("wk", b.wk_b, b.wk, D), ("wv", b.wv_b, b.wv, D),
                ("wq", b.wq_b, b.wq, D),
                ("gw", b.gw_c, b.gw, FI), ("uw", b.uw_c, b.uw, FI),
                ("dw", b.dw_c, b.dw, D), ("win", b.win_c, b.win, E3),
                ("wout", b.wout_c, b.wout, D), ("wo", b.wo_b, b.wo, D),
            ]
            for name, dst, src, rows in jobs:
                q = qC[name]
                for r0 in range(0, rows, 512):
                    q.inc(g.dma_start(out=dst[r0:r0 + 512, :],
                                      in_=src[r0:r0 + 512, :]), 16)
                cast_done[name] = (q, q.v)
        block.gpsimd(p1_g)

        RG = 1024
        tr_plan = []
        for name, src_, dst, R, C in [
            ("gw", b.gw_c, b.gwT, FI, D), ("uw", b.uw_c, b.uwT, FI, D),
            ("dw", b.dw_c, b.dwT, D, FI), ("win", b.win_c, b.winT, E3, D),
            ("wout", b.wout_c, b.woutT, D, D),
        ]:
            for c0 in range(0, C, 128):
                for r0 in range(0, R, RG):
                    tr_plan.append((name, src_, dst, r0, c0))
        nTR = len(tr_plan)
        b.tr_i = [0]
        b.xp_n = [0]

        def emit_xpose(items):
            marks = {}
            base = b.xp_n[0]
            b.xp_n[0] += len(items)

            def t_sync(sync):
                for j, (waits, src_ap, dst_ap) in enumerate(items):
                    i = base + j
                    for sh, sv in waits:
                        sync.wait_ge(sh, sv)
                    if i >= 4:
                        sync.wait_ge(qTS[i % 4].h, 16 * (i // 4))
                    qTA[i % 4].inc(sync.dma_start(
                        out=b.tstg[i % 4][:, :], in_=src_ap,
                        transpose=True), 16)
                    marks[i] = qTA[i % 4].v
            block.sync(t_sync)

            def t_gp(g):
                for j, (waits, src_ap, dst_ap) in enumerate(items):
                    i = base + j
                    g.wait_ge(qTA[i % 4].h, marks[i])
                    qTS[i % 4].inc(g.dma_start(
                        out=dst_ap, in_=b.tstg[i % 4][:, :]), 16)
            block.gpsimd(t_gp)

        def emit_tr_batch(n):
            lo = b.tr_i[0]
            hi = min(lo + n, nTR)
            if hi <= lo:
                return
            b.tr_i[0] = hi
            items = []
            for i in range(lo, hi):
                name, src_, dst, r0, c0 = tr_plan[i]
                q, v = cast_done[name]
                items.append((
                    [(q.h, v)],
                    src_[r0:r0 + RG, c0:c0 + 128],
                    dst[c0:c0 + 128, r0:r0 + RG]))
            emit_xpose(items)


        b.ln_id = [0]

        def emit_ln(ph, double):
            b.ln_id[0] += 1
            u = b.ln_id[0]
            lnctx = ExitStack()
            b.lnA = lnctx.enter_context(nc.sbuf_tensor(f"lnA{u}", [128, T], F32))
            b.lnB = lnctx.enter_context(nc.sbuf_tensor(f"lnB{u}", [128, T], F32))
            p_s1 = lnctx.enter_context(nc.psum_tensor(f"ln_s1_{u}", [1, T], F32))
            p_s2 = lnctx.enter_context(nc.psum_tensor(f"ln_s2_{u}", [1, T], F32))
            p_bc = lnctx.enter_context(nc.psum_tensor(f"ln_bc_{u}", [128, T], F32))
            p_bc2 = lnctx.enter_context(nc.psum_tensor(f"ln_bc2_{u}", [128, T], F32))

            def sA(scalar):
                for c in range(NC):
                    ac.inc(scalar.activation(
                        b.xn_sb[:, c * T:(c + 1) * T],
                        b.r_sb[:, c * T:(c + 1) * T], AF.Copy))
            block.scalar(sA)
            acA = ac.v

            def tA(tensor):
                tensor.wait_ge(ac.h, acA)
                for tt in range(2):
                    for c in range(NC):
                        m = tensor.matmul(
                            p_s1[:, tt * 512:(tt + 1) * 512], b.c_oc[:, :],
                            b.xn_sb[:, c * T + tt * 512:c * T + (tt + 1) * 512],
                            start=(c == 0), stop=(c == NC - 1))
                pe.inc(m)
            block.tensor(tA)
            peA = pe.v

            def sB(scalar):
                scalar.wait_ge(pe.h, peA)
                for c in range(NC):
                    ac.inc(scalar.activation(
                        b.xn_sb[:, c * T:(c + 1) * T],
                        b.r_sb[:, c * T:(c + 1) * T], AF.Square))
            block.scalar(sB)
            acB = ac.v

            def tB(tensor):
                tensor.wait_ge(ac.h, acB)
                for tt in range(2):
                    for c in range(NC):
                        m = tensor.matmul(
                            p_s2[:, tt * 512:(tt + 1) * 512], b.c_oc[:, :],
                            b.xn_sb[:, c * T + tt * 512:c * T + (tt + 1) * 512],
                            start=(c == 0), stop=(c == NC - 1))
                pe.inc(m)
            block.tensor(tB)
            peB = pe.v

            mu = b.stat[:, 0:T]
            var = b.stat[:, T:2 * T]
            al = b.stat[:, 2 * T:3 * T]
            be = b.stat[:, T:2 * T]
            t1 = b.srow[:, 0:T]
            t2 = b.srow[:, T:2 * T]

            def v1(vector):
                vector.wait_ge(pe.h, peB)
                dv.inc(vector.tensor_scalar(mu, p_s1[:, :], 1.0 / D, None, ALU.mult))
                dv.inc(vector.tensor_scalar(var, p_s2[:, :], 1.0 / D, None, ALU.mult))
                vector.wait_ge(dv.h, dv.v)
                dv.inc(vector.tensor_tensor(t1, mu, mu, ALU.mult))
                vector.wait_ge(dv.h, dv.v)
                dv.inc(vector.tensor_tensor(var, var, t1, ALU.subtract))
                vector.wait_ge(dv.h, dv.v)
            block.vector(v1)
            dv1 = dv.v

            def v1b(vector):
                vector.wait_ge(dv.h, dv1)
                dv.inc(vector.tensor_scalar(t1, var, EPS, None, ALU.add))
                vector.wait_ge(dv.h, dv.v)
            block.vector(v1b)
            dv1b = dv.v

            def s1(scalar):
                scalar.wait_ge(dv.h, dv1b)
                ac.inc(scalar.activation(t1, t1, AF.Sqrt))
            block.scalar(s1)
            ac1 = ac.v

            def v2(vector):
                vector.wait_ge(ac.h, ac1)
                dv.inc(vector.reciprocal(al, t1))
                vector.wait_ge(dv.h, dv.v)
                if double:
                    dv.inc(vector.tensor_tensor(t2, al, al, ALU.mult))
                    vector.wait_ge(dv.h, dv.v)
                    dv.inc(vector.tensor_tensor(t2, var, t2, ALU.mult))
                    vector.wait_ge(dv.h, dv.v)
            block.vector(v2)
            dv2 = dv.v

            if double:
                def v2b(vector):
                    vector.wait_ge(dv.h, dv2)
                    dv.inc(vector.tensor_scalar(t2, t2, EPS, None, ALU.add))
                    vector.wait_ge(dv.h, dv.v)
                block.vector(v2b)
                dv2b = dv.v

                def s2(scalar):
                    scalar.wait_ge(dv.h, dv2b)
                    ac.inc(scalar.activation(t2, t2, AF.Sqrt))
                block.scalar(s2)
                ac2 = ac.v

                def v3(vector):
                    vector.wait_ge(ac.h, ac2)
                    dv.inc(vector.reciprocal(t1, t2))
                    vector.wait_ge(dv.h, dv.v)
                    dv.inc(vector.tensor_tensor(al, al, t1, ALU.mult))
                    vector.wait_ge(dv.h, dv.v)
                block.vector(v3)

            def v4(vector):
                vector.wait_ge(dv.h, dv.v)
                dv.inc(vector.tensor_tensor(be, mu, al, ALU.mult))
                vector.wait_ge(dv.h, dv.v)
                dv.inc(vector.tensor_scalar(be, be, -1.0, None, ALU.mult))
                vector.wait_ge(dv.h, dv.v)
            block.vector(v4)
            dv4 = dv.v

            def tC(tensor):
                tensor.wait_ge(dv.h, dv4)
                for tt in range(2):
                    m = tensor.matmul(p_bc[:, tt * 512:(tt + 1) * 512],
                                      b.c_or32[:, :],
                                      al[0:1, tt * 512:(tt + 1) * 512],
                                      start=True, stop=True)
                    m = tensor.matmul(p_bc2[:, tt * 512:(tt + 1) * 512],
                                      b.c_or32[:, :],
                                      be[0:1, tt * 512:(tt + 1) * 512],
                                      start=True, stop=True)
                pe.inc(m)
            block.tensor(tC)
            peC = pe.v

            def v5(vector):
                vector.wait_ge(pe.h, peC)
                dv.inc(vector.tensor_copy(b.lnA[:, :], p_bc[:, :]))
                dv.inc(vector.tensor_copy(b.lnB[:, :], p_bc2[:, :]))
                vector.wait_ge(dv.h, dv.v)
                for c in range(NC):
                    dv.inc(vector.tensor_tensor(
                        b.xn_sb[:, c * T:(c + 1) * T],
                        b.r_sb[:, c * T:(c + 1) * T], b.lnA[:, :], ALU.mult))
                    vector.wait_ge(dv.h, dv.v)
                    dv.inc(vector.tensor_tensor(
                        b.xn_sb[:, c * T:(c + 1) * T],
                        b.xn_sb[:, c * T:(c + 1) * T], b.lnB[:, :], ALU.add))
                    vector.wait_ge(dv.h, dv.v)
            block.vector(v5)
            dv5 = dv.v

            if double:
                def s3(scalar):
                    scalar.wait_ge(dv.h, dv5)
                    for c in range(NC):
                        ac.inc(scalar.activation(
                            b.xn_sb[:, c * T:(c + 1) * T],
                            b.xn_sb[:, c * T:(c + 1) * T], AF.Identity,
                            bias=b.c_lngb[:, 16 + c:17 + c],
                            scale=b.c_lngb[:, c:c + 1]))
                block.scalar(s3)
            lnctx.close()
            barrier(gp=False)

        def emit_proj(ph, name, wten, wcol0, n_d, n_e, eg, tts, rhs_fn, evac,
                      pre_wait=()):
            width = 512 * len(tts)
            NBUF = 4
            prctx = ExitStack()
            wbuf = [prctx.enter_context(
                nc.sbuf_tensor(f"{name}_w{i}", [128, eg * 128], BF16))
                for i in range(NBUF)]
            pp = [prctx.enter_context(
                nc.psum_tensor(f"{name}_p{i}", [128, width], F32))
                for i in range(2 * eg)]
            n_g = n_e // eg
            steps = [(g, dc) for g in range(n_g) for dc in range(n_d)]
            ns = len(steps)
            pe0 = pe.v
            loadm = [qW[i % 4].v + 16 * (i // 4 + 1) for i in range(ns)]
            mmm = [pe0 + (i + 1) for i in range(ns)]
            dv0, ac0 = dv.v, ac.v
            ndv, nac = evac.ndv, evac.nac
            evdv = [dv0 + (g + 1) * ndv * eg for g in range(n_g)]
            evac_ = [ac0 + (g + 1) * nac * eg for g in range(n_g)]

            def psync(sync):
                for s_, v_ in pre_wait:
                    sync.wait_ge(s_.h, v_)
                for i, (g, dc) in enumerate(steps):
                    if i >= NBUF:
                        sync.wait_ge(pe.h, mmm[i - NBUF])
                    qW[i % 4].inc(sync.dma_start(
                        out=wbuf[i % NBUF][:, :],
                        in_=wten[dc * 128:(dc + 1) * 128,
                                 wcol0 + g * eg * 128:
                                 wcol0 + (g + 1) * eg * 128]), 16)
            block.sync(psync)
            assert qW[(ns - 1) % 4].v == loadm[-1]

            def ptens(tensor):
                for i, (g, dc) in enumerate(steps):
                    tensor.wait_ge(qW[i % 4].h, loadm[i])
                    if dc == 0 and g >= 2:
                        if ndv:
                            tensor.wait_ge(dv.h, evdv[g - 2])
                        if nac:
                            tensor.wait_ge(ac.h, evac_[g - 2])
                    m = None
                    for e2 in range(eg):
                        for j, tt in enumerate(tts):
                            m = tensor.matmul(
                                pp[(g % 2) * eg + e2][:, j * 512:(j + 1) * 512],
                                wbuf[i % NBUF][:, e2 * 128:(e2 + 1) * 128],
                                rhs_fn(dc, tt),
                                start=(dc == 0), stop=(dc == n_d - 1))
                    pe.inc(m)
            block.tensor(ptens)
            assert pe.v == mmm[-1]

            if ndv:
                def pvec(vector):
                    for g in range(n_g):
                        vector.wait_ge(pe.h, mmm[(g + 1) * n_d - 1])
                        for e2 in range(eg):
                            evac.dv(vector, g * eg + e2,
                                    pp[(g % 2) * eg + e2])
                block.vector(pvec)
                assert dv.v == evdv[-1], (name, dv.v, evdv[-1])
            if nac:
                def psca(scalar):
                    for g in range(n_g):
                        scalar.wait_ge(pe.h, mmm[(g + 1) * n_d - 1])
                        for e2 in range(eg):
                            evac.ac(scalar, g * eg + e2,
                                    pp[(g % 2) * eg + e2])
                block.scalar(psca)
                assert ac.v == evac_[-1], (name, ac.v, evac_[-1])
            prctx.close()

        class EvacBias:
            ndv, nac = 0, 1

            def __init__(self, dst, dstoff, bias_sb, bcol0, width):
                self.dst, self.off, self.bias, self.b0 = dst, dstoff, bias_sb, bcol0
                self.w = width

            def ac(self, scalar, ec, pap):
                ac.inc(scalar.activation(
                    self.dst[:, self.off + ec * self.w:
                             self.off + (ec + 1) * self.w],
                    pap[:, 0:self.w], AF.Identity,
                    bias=self.bias[:, self.b0 + ec:self.b0 + ec + 1]))

        class EvacResid:
            ndv, nac = 1, 0

            def __init__(self, bias_sb, bcol0, width):
                self.bias, self.b0, self.w = bias_sb, bcol0, width

            def dv(self, vector, ec, pap):
                if self.bias is not None:
                    dv.inc(vector.scalar_tensor_tensor(
                        b.r_sb[:, ec * T:ec * T + self.w],
                        pap[:, 0:self.w],
                        self.bias[:, self.b0 + ec:self.b0 + ec + 1],
                        b.r_sb[:, ec * T:ec * T + self.w],
                        ALU.add, ALU.add))
                else:
                    dv.inc(vector.tensor_tensor(
                        b.r_sb[:, ec * T:ec * T + self.w],
                        pap[:, 0:self.w],
                        b.r_sb[:, ec * T:ec * T + self.w], ALU.add))

        if "s1" in STAGES:
            with ExitStack() as ph:
                b.tstg = [ph.enter_context(
                    nc.sbuf_tensor(f"tstg{i}", [128, RG], BF16))
                    for i in range(4)]
                emit_ln(ph, double=True)

                c_stair = ph.enter_context(
                    nc.sbuf_tensor("c_stair", [128, 4 * 512], BF16))

                def ldst(sync):
                    qSTR.inc(sync.dma_start(
                        out=c_stair[:, :],
                        in_=bass.AP(b.stair, 0,
                                    [[512, 128], [128 * 512, 4], [1, 512]])),
                        16)
                block.sync(ldst)

                emit_proj(ph, "kp", b.wk_b, 0, NC, NC, 2, (0, 1),
                          lambda dc, tt: b.xn_sb[:, dc * T + tt * 512:
                                                 dc * T + (tt + 1) * 512],
                          EvacBias(b.pA, 0, b.c_bqkvo, 16, T),
                          pre_wait=[(qC["wk"], cast_done["wk"][1])])
                barrier(gp=False)

                knctx = ExitStack()
                knrow = knctx.enter_context(nc.sbuf_tensor("knrow", [1, T], BF16))
                sqt = knctx.enter_context(nc.sbuf_tensor("sqt", [128, T], BF16))
                p_kn = knctx.enter_context(nc.psum_tensor("p_kn", [1, T], F32))
                pe_prev = pe.v
                dv_prev = dv.v
                qs_prev = qST.v
                for h in range(NH):
                    def ksq(scalar, h=h, pe_prev=pe_prev):
                        scalar.wait_ge(pe.h, pe_prev)
                        ac.inc(scalar.activation(
                            sqt[:, :], b.pA[:, h * T:(h + 1) * T], AF.Square))
                    block.scalar(ksq)
                    ac_h = ac.v

                    def kmm(tensor, h=h, ac_h=ac_h, dv_prev=dv_prev):
                        tensor.wait_ge(ac.h, ac_h)
                        tensor.wait_ge(dv.h, dv_prev)
                        for tt in range(2):
                            m = tensor.matmul(
                                p_kn[:, tt * 512:(tt + 1) * 512], b.c_oc[:, :],
                                sqt[:, tt * 512:(tt + 1) * 512],
                                start=True, stop=True)
                        pe.inc(m)
                    block.tensor(kmm)
                    pe_h = pe.v
                    pe_prev = pe_h

                    def kev(vector, h=h, pe_h=pe_h, qs_prev=qs_prev):
                        vector.wait_ge(pe.h, pe_h)
                        vector.wait_ge(qST.h, qs_prev)
                        dv.inc(vector.tensor_copy(knrow[:, :], p_kn[:, :]))
                        vector.wait_ge(dv.h, dv.v)
                    block.vector(kev)
                    dv_h = dv.v
                    dv_prev = dv_h

                    def kst(sync, h=h, dv_h=dv_h):
                        sync.wait_ge(dv.h, dv_h)
                        qST.inc(sync.dma_start(
                            out=b.kvloc[2 * D + h:2 * D + h + 1, :],
                            in_=knrow[:, :]), 16)
                        qST.inc(sync.dma_start(
                            out=b.kvloc[h * 128:(h + 1) * 128, :],
                            in_=b.pA[:, h * T:(h + 1) * T]), 16)
                    block.sync(kst)
                    qs_prev = qST.v
                knctx.close()
                barrier(gp=False)

                emit_proj(ph, "vp", b.wv_b, 0, NC, NC, 2, (0, 1),
                          lambda dc, tt: b.xn_sb[:, dc * T + tt * 512:
                                                 dc * T + (tt + 1) * 512],
                          EvacBias(b.pA, 0, b.c_bqkvo, 32, T),
                          pre_wait=[(qC["wv"], cast_done["wv"][1])])
                barrier(gp=False)

                def vst(sync):
                    for h in range(NH):
                        qST.inc(sync.dma_start(
                            out=b.kvloc[D + h * 128:D + (h + 1) * 128, :],
                            in_=b.pA[:, h * T:(h + 1) * T]), 16)
                block.sync(vst)
                qs_vst = qST.v

                def gat(g):
                    g.wait_ge(qST.h, qs_vst)
                    cc.inc(g.collective_compute(
                        "AllGather", ALU.bypass,
                        replica_groups=[list(range(NCORES))],
                        ins=[b.kvloc.ap().opt()], outs=[b.kvg.ap().opt()]))
                block.gpsimd(gat)
                cc_gather = cc.v

                def qgate(scalar):
                    scalar.wait_ge(qST.h, qs_vst)
                block.scalar(qgate)
                emit_proj(ph, "qp", b.wq_b, 0, NC, NC, 2, (0, 1),
                          lambda dc, tt: b.xn_sb[:, dc * T + tt * 512:
                                                 dc * T + (tt + 1) * 512],
                          EvacBias(b.pA, 0, b.c_bqkvo, 0, T),
                          pre_wait=[(qC["wq"], cast_done["wq"][1])])
                barrier(gp=False)

                vt_items = []
                for r in range(NCORES):
                    for tc in range(8):
                        for hf in range(2):
                            vt_items.append((
                                [(cc.h, cc_gather)],
                                b.kvg[r * CROWS + D + hf * 1024:
                                      r * CROWS + D + (hf + 1) * 1024,
                                      tc * 128:(tc + 1) * 128],
                                b.vtokg[r * T + tc * 128:r * T + (tc + 1) * 128,
                                        hf * 1024:(hf + 1) * 1024]))
                for tc in range(8):
                    for hf in range(2):
                        vt_items.append((
                            [(qST.h, qs_vst)],
                            b.kvloc[D + hf * 1024:D + (hf + 1) * 1024,
                                    tc * 128:(tc + 1) * 128],
                            b.vtokl[tc * 128:(tc + 1) * 128,
                                    hf * 1024:(hf + 1) * 1024]))
                emit_xpose(vt_items)
                vt_done = [(s, s.v) for s in qTS]

                bias_hb = ph.enter_context(
                    nc.sbuf_tensor("bias_hb", [128, 32], F32))
                qnctx = ExitStack()
                sqt = qnctx.enter_context(nc.sbuf_tensor("sqt2", [128, T], BF16))
                knall = qnctx.enter_context(
                    nc.sbuf_tensor("knall", [1, 2 * T], BF16))
                p_qn = qnctx.enter_context(nc.psum_tensor("p_qn", [1, T], F32))
                p_bb = qnctx.enter_context(nc.psum_tensor("p_bb", [128, 32], F32))
                mrow = b.srow
                pe_prev = pe.v
                dv_prev = dv.v
                for h in range(NH):
                    def qsq(scalar, h=h, pe_prev=pe_prev):
                        scalar.wait_ge(pe.h, pe_prev)
                        ac.inc(scalar.activation(
                            sqt[:, :], b.pA[:, h * T:(h + 1) * T], AF.Square))
                    block.scalar(qsq)
                    ac_h = ac.v

                    def qmm(tensor, h=h, ac_h=ac_h, dv_prev=dv_prev):
                        tensor.wait_ge(ac.h, ac_h)
                        tensor.wait_ge(dv.h, dv_prev)
                        for tt in range(2):
                            m = tensor.matmul(
                                p_qn[:, tt * 512:(tt + 1) * 512], b.c_oc[:, :],
                                sqt[:, tt * 512:(tt + 1) * 512],
                                start=True, stop=True)
                        pe.inc(m)
                    block.tensor(qmm)
                    pe_h = pe.v
                    pe_prev = pe_h

                    for pp_ in range(4):
                        def qld(sync, h=h, pp_=pp_, dv_prev=dv_prev):
                            sync.wait_ge(cc.h, cc_gather)
                            sync.wait_ge(dv.h, dv_prev)
                            qKN.inc(sync.dma_start(
                                out=knall[:, :],
                                in_=bass.AP(b.kvg,
                                            (2 * pp_ * CROWS + 2 * D + h) * T,
                                            [[CROWS * T, 2], [1, T]])), 16)
                        block.sync(qld)
                        qs_hp = qKN.v

                        def qrd(vector, h=h, pp_=pp_, qs_hp=qs_hp,
                                pe_h=pe_h):
                            if pp_ == 0:
                                vector.wait_ge(pe.h, pe_h)
                                vector.wait_ge(dv.h, dv.v)
                                for blk in range(2):
                                    dv.inc(vector.reduce_max(
                                        mrow[:, 2 * h + blk:2 * h + blk + 1],
                                        p_qn[:, blk * 512:(blk + 1) * 512], AX))
                            vector.wait_ge(qKN.h, qs_hp)
                            vector.wait_ge(dv.h, dv.v)
                            slot = 32 if pp_ == 0 else 160
                            dv.inc(vector.reduce_max(
                                mrow[:, slot + h:slot + h + 1], knall[:, :], AX))
                            vector.wait_ge(dv.h, dv.v)
                            if pp_ > 0:
                                dv.inc(vector.tensor_tensor(
                                    mrow[:, 32 + h:33 + h],
                                    mrow[:, 32 + h:33 + h],
                                    mrow[:, 160 + h:161 + h], ALU.max))
                                vector.wait_ge(dv.h, dv.v)
                        block.vector(qrd)
                        dv_prev = dv.v
                    dv_prev = dv.v

                def mkb(vector):
                    vector.wait_ge(dv.h, dv.v)
                    dv.inc(vector.tensor_tensor(
                        bass.AP(b.srow, 96, [[2048, 1], [2, 16], [1, 2]]),
                        bass.AP(b.srow, 0, [[2048, 1], [2, 16], [1, 2]]),
                        bass.AP(b.srow, 32, [[2048, 1], [1, 16], [0, 2]]),
                        ALU.mult))
                    vector.wait_ge(dv.h, dv.v)
                block.vector(mkb)
                dv_m = dv.v

                def mkb2(scalar):
                    scalar.wait_ge(dv.h, dv_m)
                    ac.inc(scalar.activation(
                        mrow[:, 96:128], mrow[:, 96:128], AF.Sqrt))
                block.scalar(mkb2)
                ac_m = ac.v

                def mkb3(vector):
                    vector.wait_ge(ac.h, ac_m)
                    dv.inc(vector.tensor_scalar(
                        mrow[:, 96:128], mrow[:, 96:128], -SCALE, None, ALU.mult))
                    vector.wait_ge(dv.h, dv.v)
                block.vector(mkb3)
                dv_m2 = dv.v

                def mkb4(tensor):
                    tensor.wait_ge(dv.h, dv_m2)
                    pe.inc(tensor.matmul(p_bb[:, :], b.c_or32[:, :],
                                         mrow[0:1, 96:128], start=True, stop=True))
                block.tensor(mkb4)
                pe_m = pe.v

                def mkb5(vector):
                    vector.wait_ge(pe.h, pe_m)
                    dv.inc(vector.tensor_copy(bias_hb[:, :], p_bb[:, :]))
                    vector.wait_ge(dv.h, dv.v)
                block.vector(mkb5)
                qnctx.close()
                barrier(gp=False)

                flctx = ExitStack()
                pST = [flctx.enter_context(
                    nc.psum_tensor(f"pST{i}", [128, 1024], F32))
                    for i in range(2)]
                pAV = flctx.enter_context(nc.psum_tensor("pAV", [128, 512], F32))
                pL = flctx.enter_context(nc.psum_tensor("pL", [1, 512], F32))
                pLB = flctx.enter_context(nc.psum_tensor("pLB", [128, 512], F32))
                kbuf = [flctx.enter_context(
                    nc.sbuf_tensor(f"kb{i}", [128, 256], BF16)) for i in range(2)]
                vbuf = [flctx.enter_context(
                    nc.sbuf_tensor(f"vb{i}", [128, 256], BF16)) for i in range(2)]
                ptb = [flctx.enter_context(
                    nc.sbuf_tensor(f"pt{i}", [128, 1024], BF16)) for i in range(2)]
                linv = b.srow[0:1, 1024:1536]
                lbc = flctx.enter_context(nc.sbuf_tensor("lbc", [128, 512], F32))
                btab = flctx.enter_context(
                    nc.sbuf_tensor("btab", [128, 32 * 23], F32))
                lacc = flctx.enter_context(
                    nc.sbuf_tensor("lacc", [128, 512], F32))

                def mk_btab(vector):
                    vector.wait_ge(dv.h, dv.v)
                    dv.inc(vector.tensor_tensor(
                        bass.AP(btab, 0, [[736, 128], [23, 32], [1, 23]]),
                        bass.AP(bias_hb, 0, [[32, 128], [1, 32], [0, 23]]),
                        bass.AP(b.c_mask, 0, [[23, 128], [0, 32], [1, 23]]),
                        ALU.add))
                    vector.wait_ge(dv.h, dv.v)
                block.vector(mk_btab)
                dv_btab = dv.v

                def tiles_of(blk):
                    out = []
                    for j in range(2):
                        out.append(("loc", 0, blk * 512 + 256 * j, 22, j))
                    nb = 7 if blk == 0 else 15
                    for bb_ in range(nb):
                        rb = rank_of_block(bb_) * CROWS
                        cb = col_of_block(bb_)
                        mi = bb_ if blk == 0 else 7 + bb_
                        for j in range(2):
                            out.append(("g", rb, cb + 256 * j, mi, None))
                    return out

                for h in range(NH):
                    for blk in range(2):
                        tl = tiles_of(blk)
                        nt = len(tl)
                        dv_ep0 = dv.v
                        pe_blk0 = pe.v
                        kld = [qK[i % 2].v + 16 * (i // 2 + 1)
                               for i in range(nt)]
                        vld = [qV[i % 2].v + 16 * (i // 2 + 1)
                               for i in range(nt)]
                        ac0 = ac.v
                        expm = [ac0 + (i + 1) for i in range(nt)]
                        dv0 = dv.v
                        stairm = [None] * nt
                        laccm = [None] * nt
                        cnt_dv = dv0 + 1
                        for i, t in enumerate(tl):
                            if t[4] is not None:
                                cnt_dv += 2
                            stairm[i] = cnt_dv
                            cnt_dv += 2
                            laccm[i] = cnt_dv
                        stm = [None] * nt
                        pvm = [None] * nt
                        cnt = pe.v
                        for i in range(nt):
                            cnt += 1
                            stm[i] = cnt
                            if i >= 1:
                                cnt += 1
                                pvm[i - 1] = cnt
                        cnt += 1
                        pvm[nt - 1] = cnt
                        cnt += 1
                        pe_lmm = cnt
                        pe_last = cnt

                        def fsync(sync, tl=tl, kld=kld, h=h, stm=stm,
                                  pe_blk0=pe_blk0):
                            for i, (src, rb, c0, mi, dj) in enumerate(tl):
                                if src == "loc":
                                    sync.wait_ge(qST.h, qs_vst)
                                    ten, roff = b.kvloc, h * 128
                                else:
                                    sync.wait_ge(cc.h, cc_gather)
                                    ten, roff = b.kvg, rb + h * 128
                                if i >= 2:
                                    sync.wait_ge(pe.h, stm[i - 2])
                                else:
                                    sync.wait_ge(pe.h, pe_blk0)
                                qK[i % 2].inc(sync.dma_start(
                                    out=kbuf[i % 2][:, :],
                                    in_=ten[roff:roff + 128, c0:c0 + 256]), 16)
                        block.sync(fsync)
                        assert qK[(nt - 1) % 2].v == kld[-1]
                        emit_tr_batch(18)

                        def fscal(scalar, tl=tl, vld=vld, kld=kld, h=h,
                                  blk=blk, expm=expm, stairm=stairm,
                                  laccm=laccm, stm=stm, pvm=pvm, nt=nt,
                                  pe_blk0=pe_blk0, dv_btab=dv_btab):
                            for i in range(nt + 2):
                                if i >= 2:
                                    t = i - 2
                                    src, rb, c0, mi, dj = tl[t]
                                    scalar.wait_ge(pe.h, stm[t])
                                    if dj is not None:
                                        scalar.wait_ge(dv.h, stairm[t])
                                    if t == 0:
                                        scalar.wait_ge(dv.h, dv_btab)
                                    if t >= 2:
                                        scalar.wait_ge(pe.h, pvm[t - 2])
                                        scalar.wait_ge(dv.h, laccm[t - 2])
                                    ac.inc(scalar.activation(
                                        ptb[t % 2][:, :], pST[t % 2][:, :],
                                        AF.Exp,
                                        bias=btab[:, (2 * h + blk) * 23 + mi:
                                                  (2 * h + blk) * 23 + mi + 1],
                                        scale=SCALE))
                                if i < nt:
                                    src, rb, c0, mi, dj = tl[i]
                                    for sh_, sv_ in vt_done:
                                        scalar.wait_ge(sh_.h, sv_)
                                    if src == "loc":
                                        vten, vroff = b.vtokl, c0
                                    else:
                                        vten, vroff = (b.vtokg,
                                                       (rb // CROWS) * T + c0)
                                    if i >= 2:
                                        scalar.wait_ge(pe.h, pvm[i - 2])
                                    else:
                                        scalar.wait_ge(pe.h, pe_blk0)
                                    qV[i % 2].inc(scalar.dma_start(
                                        out=vbuf[i % 2][:, :],
                                        in_=bass.AP(
                                            vten, vroff * D + h * 128,
                                            [[D, 128], [128 * D, 2], [1, 128]])),
                                        16)
                        block.scalar(fscal)
                        assert ac.v == expm[-1]
                        assert qV[(nt - 1) % 2].v == vld[-1], (qV[(nt-1)%2].v, vld[-1])

                        def fvec(vector, tl=tl, h=h, blk=blk, stm=stm,
                                 expm=expm, nt=nt, pe_lmm_prev=None):
                            vector.wait_ge(pe.h, pe_blk0)
                            vector.wait_ge(dv.h, dv.v)
                            dv.inc(vector.memset(lacc[:, :], 0.0))
                            for i, (src, rb, c0, mi, dj) in enumerate(tl):
                                if dj is not None:
                                    vector.wait_ge(pe.h, stm[i])
                                    vector.wait_ge(dv.h, dv.v)
                                    for j in range(2):
                                        dv.inc(vector.tensor_tensor(
                                            pST[i % 2][:, j * 512:(j + 1) * 512],
                                            pST[i % 2][:, j * 512:(j + 1) * 512],
                                            c_stair[:, (2 * dj + j) * 512:
                                                    (2 * dj + j + 1) * 512],
                                            ALU.add))
                                vector.wait_ge(ac.h, expm[i])
                                vector.wait_ge(dv.h, dv.v)
                                for j in range(2):
                                    dv.inc(vector.tensor_tensor(
                                        lacc[:, :], lacc[:, :],
                                        ptb[i % 2][:, j * 512:(j + 1) * 512],
                                        ALU.add))
                                    vector.wait_ge(dv.h, dv.v)
                        block.vector(fvec)
                        assert dv.v == laccm[-1], (dv.v, laccm[-1])

                        def ftens(tensor, tl=tl, kld=kld, vld=vld, h=h,
                                  blk=blk, expm=expm, stairm=stairm,
                                  laccm=laccm, nt=nt, dv_ep0=dv_ep0):
                            qslice = b.pA[:, h * T + blk * 512:
                                          h * T + (blk + 1) * 512]
                            for i in range(nt):
                                tensor.wait_ge(qK[i % 2].h, kld[i])
                                if i >= 2:
                                    tensor.wait_ge(ac.h, expm[i - 2])
                                    tensor.wait_ge(dv.h, stairm[i - 2])
                                if i == 0:
                                    tensor.wait_ge(dv.h, dv_ep0)
                                for j in range(2):
                                    m = tensor.matmul(
                                        pST[i % 2][:, j * 512:(j + 1) * 512],
                                        kbuf[i % 2][:, j * 128:(j + 1) * 128],
                                        qslice, start=True, stop=True)
                                pe.inc(m)
                                if i >= 1:
                                    t = i - 1
                                    tensor.wait_ge(ac.h, expm[t])
                                    tensor.wait_ge(qV[t % 2].h, vld[t])
                                    for j in range(2):
                                        m = tensor.matmul(
                                            pAV[:, :],
                                            vbuf[t % 2][:, j * 128:(j + 1) * 128],
                                            ptb[t % 2][:, j * 512:(j + 1) * 512],
                                            start=(t == 0 and j == 0),
                                            stop=(t == nt - 1 and j == 1))
                                    pe.inc(m)
                            t = nt - 1
                            tensor.wait_ge(ac.h, expm[t])
                            tensor.wait_ge(qV[t % 2].h, vld[t])
                            for j in range(2):
                                m = tensor.matmul(
                                    pAV[:, :], vbuf[t % 2][:, j * 128:(j + 1) * 128],
                                    ptb[t % 2][:, j * 512:(j + 1) * 512],
                                    start=False, stop=(j == 1))
                            pe.inc(m)
                            tensor.wait_ge(dv.h, laccm[nt - 1])
                            pe.inc(tensor.matmul(
                                pL[:, :], b.c_oc32[:, :], lacc[:, :],
                                start=True, stop=True))
                        block.tensor(ftens)
                        assert pe.v == pe_last, (pe.v, pe_last)

                        def fep1(vector, h=h, blk=blk, pl=pe_last):
                            vector.wait_ge(pe.h, pl)
                            vector.wait_ge(dv.h, dv.v)
                            dv.inc(vector.reciprocal(linv[:, :], pL[:, :]))
                            vector.wait_ge(dv.h, dv.v)
                        block.vector(fep1)
                        dv_r = dv.v

                        def fep2(tensor, h=h, blk=blk):
                            tensor.wait_ge(dv.h, dv_r)
                            pe.inc(tensor.matmul(
                                pLB[:, :], b.c_or32[:, :], linv[:, :],
                                start=True, stop=True))
                        block.tensor(fep2)
                        pe_b = pe.v

                        def fep3(vector, h=h, blk=blk):
                            vector.wait_ge(pe.h, pe_b)
                            dv.inc(vector.tensor_copy(lbc[:, :], pLB[:, :]))
                            vector.wait_ge(dv.h, dv.v)
                            dv.inc(vector.tensor_tensor(
                                b.pB[:, h * T + blk * 512:
                                     h * T + (blk + 1) * 512],
                                pAV[:, :], lbc[:, :], ALU.mult))
                            vector.wait_ge(dv.h, dv.v)
                        block.vector(fep3)

                emit_tr_batch(nTR)
                b.pre_tr = [(s, s.v) for s in qTS if s.v > 0]

                flctx.close()
                barrier(gp=False)

                emit_proj(ph, "op", b.wo_b, 0, NC, NC, 2, (0, 1),
                          lambda dc, tt: b.pB[:, dc * T + tt * 512:
                                              dc * T + (tt + 1) * 512],
                          EvacResid(b.c_bqkvo, 48, T),
                          pre_wait=[(qC["wo"], cast_done["wo"][1])])
                barrier()

        def hsl(ec):
            if ec < 32:
                return b.pA[:, ec * 512:(ec + 1) * 512]
            return b.pB[:, (ec - 32) * 512:(ec - 31) * 512]

        def emit_ffn(tag):
            with ExitStack() as ph:
                emit_ln(ph, double=False)
                csb = ph.enter_context(nc.sbuf_tensor(f"{tag}csb", [128, 512], F32))
                tmp2 = [ph.enter_context(
                    nc.sbuf_tensor(f"{tag}tp{i}", [128, 512], F32))
                    for i in range(2)]

                for tt in range(2):
                    csctx = ExitStack()
                    p_cs = csctx.enter_context(
                        nc.psum_tensor(f"{tag}pcs{tt}", [1, 512], F32))
                    p_cb = csctx.enter_context(
                        nc.psum_tensor(f"{tag}pcb{tt}", [128, 512], F32))
                    def cs1(tensor, tt=tt):
                        for c in range(NC):
                            m = tensor.matmul(
                                p_cs[:, :], b.c_oc[:, :],
                                b.xn_sb[:, c * T + tt * 512:c * T + (tt + 1) * 512],
                                start=(c == 0), stop=(c == NC - 1))
                        pe.inc(m)
                    block.tensor(cs1)
                    pe_cs = pe.v

                    def cs2(vector, tt=tt):
                        vector.wait_ge(pe.h, pe_cs)
                        dv.inc(vector.tensor_scalar(
                            b.srow[:, 0:512], p_cs[:, :], 128.0, None, ALU.mult))
                        vector.wait_ge(dv.h, dv.v)
                    block.vector(cs2)
                    dv_cs = dv.v

                    def cs3(tensor, tt=tt):
                        tensor.wait_ge(dv.h, dv_cs)
                        pe.inc(tensor.matmul(p_cb[:, :], b.c_or32[:, :],
                                             b.srow[0:1, 0:512],
                                             start=True, stop=True))
                    block.tensor(cs3)
                    pe_cb = pe.v

                    def cs4(vector, tt=tt):
                        vector.wait_ge(pe.h, pe_cb)
                        dv.inc(vector.tensor_copy(csb[:, :], p_cb[:, :]))
                        vector.wait_ge(dv.h, dv.v)
                    block.vector(cs4)
                    csctx.close()
                    barrier()

                    class EvacG:
                        ndv, nac = 1, 1

                        def __init__(self):
                            self.dv0 = dv.v
                            self.ac0 = ac.v

                        def dv(self, vector, ec, pap):
                            vector.wait_ge(dv.h, dv.v)
                            if ec >= 2:
                                vector.wait_ge(ac.h, self.ac0 + ec - 1)
                            dv.inc(vector.tensor_tensor(
                                tmp2[ec % 2][:, :], pap[:, 0:512], csb[:, :],
                                ALU.subtract))

                        def ac(self, scalar, ec, pap):
                            scalar.wait_ge(dv.h, self.dv0 + ec + 1)
                            ac.inc(scalar.activation(
                                hsl(ec), tmp2[ec % 2][:, :], AF.Silu,
                                scale=b.c_gsc[:, ec:ec + 1]))
                    emit_proj(ph, f"{tag}g{tt}", b.gwT, 0, NC, NCI, 4, (tt,),
                              lambda dc, _tt: b.xn_sb[
                                  :, dc * T + tt * 512:dc * T + (tt + 1) * 512],
                              EvacG(), pre_wait=b.pre_tr)
                    barrier()

                    class EvacU:
                        ndv, nac = 2, 0

                        def dv(self, vector, ec, pap):
                            vector.wait_ge(dv.h, dv.v)
                            dv.inc(vector.tensor_tensor(
                                tmp2[ec % 2][:, :], pap[:, 0:512], csb[:, :],
                                ALU.subtract))
                            vector.wait_ge(dv.h, dv.v)
                            dv.inc(vector.scalar_tensor_tensor(
                                hsl(ec), tmp2[ec % 2][:, :],
                                b.c_usc[:, ec:ec + 1], hsl(ec),
                                ALU.mult, ALU.mult))
                    emit_proj(ph, f"{tag}u{tt}", b.uwT, 0, NC, NCI, 4, (tt,),
                              lambda dc, _tt: b.xn_sb[
                                  :, dc * T + tt * 512:dc * T + (tt + 1) * 512],
                              EvacU(), pre_wait=b.pre_tr)
                    barrier()

                    csctx2 = ExitStack()
                    p_cs = csctx2.enter_context(
                        nc.psum_tensor(f"{tag}pch{tt}", [1, 512], F32))
                    p_cb = csctx2.enter_context(
                        nc.psum_tensor(f"{tag}pcb2{tt}", [128, 512], F32))
                    def ch1(tensor, tt=tt):
                        for c in range(NCI):
                            m = tensor.matmul(
                                p_cs[:, :], b.c_oc[:, :], hsl(c),
                                start=(c == 0), stop=(c == NCI - 1))
                        pe.inc(m)
                    block.tensor(ch1)
                    pe_ch = pe.v

                    def ch2(vector, tt=tt):
                        vector.wait_ge(pe.h, pe_ch)
                        dv.inc(vector.tensor_scalar(
                            b.srow[:, 0:512], p_cs[:, :], 128.0, None, ALU.mult))
                        vector.wait_ge(dv.h, dv.v)
                    block.vector(ch2)
                    dv_ch = dv.v

                    def ch3(tensor, tt=tt):
                        tensor.wait_ge(dv.h, dv_ch)
                        pe.inc(tensor.matmul(p_cb[:, :], b.c_or32[:, :],
                                             b.srow[0:1, 0:512],
                                             start=True, stop=True))
                    block.tensor(ch3)
                    pe_cb2 = pe.v

                    def ch4(vector, tt=tt):
                        vector.wait_ge(pe.h, pe_cb2)
                        dv.inc(vector.tensor_copy(csb[:, :], p_cb[:, :]))
                        vector.wait_ge(dv.h, dv.v)
                    block.vector(ch4)
                    csctx2.close()
                    barrier()

                    class EvacD:
                        ndv, nac = 2, 0

                        def __init__(self, tt):
                            self.tt = tt

                        def dv(self, vector, ec, pap):
                            vector.wait_ge(dv.h, dv.v)
                            dv.inc(vector.tensor_tensor(
                                tmp2[ec % 2][:, :], pap[:, 0:512], csb[:, :],
                                ALU.subtract))
                            vector.wait_ge(dv.h, dv.v)
                            rsl = b.r_sb[:, ec * T + self.tt * 512:
                                         ec * T + (self.tt + 1) * 512]
                            dv.inc(vector.scalar_tensor_tensor(
                                rsl, tmp2[ec % 2][:, :],
                                b.c_dsc[:, ec:ec + 1], rsl,
                                ALU.mult, ALU.add))
                    emit_proj(ph, f"{tag}d{tt}", b.dwT, 0, NCI, NC, 4, (tt,),
                              lambda dc, _tt: hsl(dc),
                              EvacD(tt), pre_wait=b.pre_tr)
                    barrier()

        if "s2" in STAGES:
            emit_ffn("f2")

        if "s3" in STAGES:
            with ExitStack() as ph:
                emit_ln(ph, double=False)
                cs_sb = ph.enter_context(nc.sbuf_tensor("s3cs", [128, T], BF16))
                sn_sb = ph.enter_context(nc.sbuf_tensor("s3sn", [128, T], BF16))
                tmpf = ph.enter_context(nc.sbuf_tensor("s3tf", [128, T], F32))
                invc = ph.enter_context(nc.sbuf_tensor("s3inv", [128, 8], BF16))
                acol = ph.enter_context(nc.sbuf_tensor("s3acol", [128, 16], F32))
                invrow = ph.enter_context(nc.sbuf_tensor("s3ivr", [1, T], BF16))
                kvtok = ph.enter_context(nc.sbuf_tensor("s3kvt", [128, 2048], BF16))
                vstage = [ph.enter_context(
                    nc.sbuf_tensor(f"s3vs{i}", [128, T], BF16)) for i in range(2)]
                arow = b.srow[0:1, 0:2048]

                emit_proj(ph, "bk", b.winT, D, NC, NC, 2, (0, 1),
                          lambda dc, tt: b.xn_sb[:, dc * T + tt * 512:
                                                 dc * T + (tt + 1) * 512],
                          EvacBias(b.pA, 0, b.c_bin3, 16, T), pre_wait=b.pre_tr)
                barrier()

                b.rp_id = [0]

                def emit_rope(buf):
                    b.rp_id[0] += 1
                    rpctx = ExitStack()
                    p_rh = rpctx.enter_context(
                        nc.psum_tensor(f"p_rh{b.rp_id[0]}", [128, T], F32))
                    dv_prev = dv.v
                    for c in range(8):
                        def rl(sync, c=c, dv_prev=dv_prev):
                            if c > 0:
                                sync.wait_ge(dv.h, dv_prev)
                            qRP.inc(sync.dma_start(
                                out=cs_sb[:, :],
                                in_=b.cost[c * 128:(c + 1) * 128, :]), 16)
                            qRP.inc(sync.dma_start(
                                out=sn_sb[:, :],
                                in_=b.sint[c * 128:(c + 1) * 128, :]), 16)
                        block.sync(rl)
                        qs_c = qRP.v

                        def rp(tensor, c=c, dv_prev=dv_prev):
                            if c > 0:
                                tensor.wait_ge(dv.h, dv_prev)
                            for tt in range(2):
                                m = tensor.matmul(
                                    p_rh[:, tt * 512:(tt + 1) * 512],
                                    b.c_perm[:, :],
                                    buf[:, c * T + tt * 512:c * T + (tt + 1) * 512],
                                    start=True, stop=True)
                            pe.inc(m)
                        block.tensor(rp)
                        pe_c = pe.v

                        def rv(vector, c=c):
                            vector.wait_ge(qRP.h, qs_c)
                            vector.wait_ge(pe.h, pe_c)
                            vector.wait_ge(dv.h, dv.v)
                            dv.inc(vector.tensor_tensor(
                                tmpf[:, :], buf[:, c * T:(c + 1) * T],
                                cs_sb[:, :], ALU.mult))
                            vector.wait_ge(dv.h, dv.v)
                            dv.inc(vector.tensor_tensor(
                                p_rh[:, :], p_rh[:, :], sn_sb[:, :], ALU.mult))
                            vector.wait_ge(dv.h, dv.v)
                            dv.inc(vector.tensor_tensor(
                                buf[:, c * T:(c + 1) * T], tmpf[:, :],
                                p_rh[:, :], ALU.add))
                            vector.wait_ge(dv.h, dv.v)
                        block.vector(rv)
                        dv_prev = dv.v
                    rpctx.close()
                    barrier()
                emit_rope(b.pA)

                sq3 = vstage[0]
                knctx3 = ExitStack()
                p_kn = knctx3.enter_context(nc.psum_tensor("s3pkn", [1, T], F32))
                pe_prev = pe.v
                for c in range(NC):
                    def k2(scalar, c=c, pe_prev=pe_prev):
                        scalar.wait_ge(pe.h, pe_prev)
                        ac.inc(scalar.activation(
                            sq3[:, :], b.pA[:, c * T:(c + 1) * T], AF.Square))
                    block.scalar(k2)
                    ac_c = ac.v

                    def k2m(tensor, c=c, ac_c=ac_c):
                        tensor.wait_ge(ac.h, ac_c)
                        for tt in range(2):
                            m = tensor.matmul(
                                p_kn[:, tt * 512:(tt + 1) * 512], b.c_oc[:, :],
                                sq3[:, tt * 512:(tt + 1) * 512],
                                start=(c == 0), stop=(c == NC - 1))
                        pe.inc(m)
                    block.tensor(k2m)
                    pe_prev = pe.v
                barrier()

                def inv1(scalar):
                    ac.inc(scalar.activation(
                        b.srow[:, 0:T], p_kn[:, :], AF.Sqrt))
                block.scalar(inv1)
                ac_i = ac.v

                def inv2(vector):
                    vector.wait_ge(ac.h, ac_i)
                    dv.inc(vector.tensor_scalar(
                        b.srow[:, 0:T], b.srow[:, 0:T], 1e-12, None, ALU.max))
                    vector.wait_ge(dv.h, dv.v)
                    dv.inc(vector.reciprocal(b.srow[:, T:2 * T], b.srow[:, 0:T]))
                    vector.wait_ge(dv.h, dv.v)
                    dv.inc(vector.tensor_copy(invrow[:, :], b.srow[:, T:2 * T]))
                    vector.wait_ge(dv.h, dv.v)
                block.vector(inv2)
                dv_i = dv.v

                barrier()
                knctx3.close()
                trctx = ExitStack()
                p_tr32b = trctx.enter_context(
                    nc.psum_tensor("s3ptrb", [128, 8], F32))
                for j in range(8):
                    def trj(tensor, j=j):
                        pe.inc(tensor.matmul(
                            p_tr32b[:, j:j + 1],
                            invrow[0:1, j * 128:(j + 1) * 128],
                            b.c_oc[0:1, 0:1], start=True, stop=True))
                    block.tensor(trj)
                    pe_j = pe.v

                    def cvj(vector, j=j):
                        vector.wait_ge(pe.h, pe_j)
                        dv.inc(vector.tensor_copy(
                            invc[:, j:j + 1], p_tr32b[:, j:j + 1]))
                        vector.wait_ge(dv.h, dv.v)
                    block.vector(cvj)
                trctx.close()
                barrier()

                class EvacV2:
                    ndv, nac = 1, 1

                    def __init__(self):
                        self.ac0 = ac.v
                        self.dv0 = dv.v

                    def ac(self, scalar, ec, pap):
                        if ec >= 2:
                            scalar.wait_ge(dv.h, self.dv0 + ec - 1)
                        ac.inc(scalar.activation(
                            vstage[ec % 2][:, :], pap[:, 0:T], AF.Identity,
                            bias=b.c_bin3[:, 32 + ec:33 + ec]))

                    def dv(self, vector, ec, pap):
                        vector.wait_ge(ac.h, self.ac0 + ec + 1)
                        dv.inc(vector.tensor_tensor(
                            b.pA[:, ec * T:(ec + 1) * T],
                            b.pA[:, ec * T:(ec + 1) * T],
                            vstage[ec % 2][:, :], ALU.mult))
                emit_proj(ph, "bv", b.winT, 2 * D, NC, NC, 2, (0, 1),
                          lambda dc, tt: b.xn_sb[:, dc * T + tt * 512:
                                                 dc * T + (tt + 1) * 512],
                          EvacV2(), pre_wait=b.pre_tr)
                barrier()

                actx = ExitStack()
                p_tr = actx.enter_context(
                    nc.psum_tensor("s3ptr2", [128, 128], BF16))
                p_a = actx.enter_context(nc.psum_tensor("s3pa", [1, 2048], F32))
                amm_prev = pe.v
                for tc in range(8):
                    dv_ckv_prev = dv.v
                    for c in range(NC):
                        def tkv(tensor, tc=tc, c=c, amm_prev=amm_prev):
                            tensor.wait_ge(dv.h, dv.v)
                            if c == 0:
                                tensor.wait_ge(pe.h, amm_prev)
                            pe.inc(tensor.transpose(
                                p_tr[:, :],
                                b.pA[:, c * T + tc * 128:c * T + (tc + 1) * 128],
                                b.c_id[:, :]))
                        block.tensor(tkv)
                        pe_t = pe.v

                        def ckv(vector, tc=tc, c=c, pe_t=pe_t):
                            vector.wait_ge(pe.h, pe_t)
                            dv.inc(vector.tensor_copy(
                                kvtok[:, c * 128:(c + 1) * 128], p_tr[:, :]))
                            vector.wait_ge(dv.h, dv.v)
                        block.vector(ckv)
                    dv_t = dv.v

                    def amm(tensor, tc=tc, dv_t=dv_t):
                        tensor.wait_ge(dv.h, dv_t)
                        for j in range(4):
                            m = tensor.matmul(
                                p_a[:, j * 512:(j + 1) * 512],
                                invc[:, tc:tc + 1],
                                kvtok[:, j * 512:(j + 1) * 512],
                                start=(tc == 0), stop=(tc == 7))
                        pe.inc(m)
                    block.tensor(amm)
                    amm_prev = pe.v
                barrier()

                def aev(vector):
                    vector.wait_ge(pe.h, pe.v)
                    dv.inc(vector.tensor_copy(arow[:, :], p_a[:, :]))
                    vector.wait_ge(dv.h, dv.v)
                block.vector(aev)
                dv_a = dv.v

                def ast(g):
                    g.wait_ge(dv.h, dv_a)
                    qAB.inc(g.dma_start(out=b.abuf_i[:, :], in_=arow[:, :]), 16)
                    g.wait_ge(qAB.h, qAB.v)
                    cc.inc(g.collective_compute(
                        "AllReduce", ALU.add,
                        replica_groups=[list(range(NCORES))],
                        ins=[b.abuf_i.ap().opt()], outs=[b.abuf_o.ap().opt()]))
                    g.wait_ge(cc.h, cc.v)
                    qAB.inc(g.dma_start(out=arow[:, :], in_=b.abuf_o[:, :]), 16)
                    g.wait_ge(qAB.h, qAB.v)
                block.gpsimd(ast)
                actx.close()
                barrier()

                acctx = ExitStack()
                p_tr32 = acctx.enter_context(
                    nc.psum_tensor("s3ptr32", [128, 16], F32))
                for j in range(NC):
                    def taj(tensor, j=j):
                        pe.inc(tensor.matmul(
                            p_tr32[:, j:j + 1],
                            arow[0:1, j * 128:(j + 1) * 128],
                            b.c_or32[0:1, 0:1], start=True, stop=True))
                    block.tensor(taj)
                    pe_j = pe.v

                    def caj(vector, j=j):
                        vector.wait_ge(pe.h, pe_j)
                        dv.inc(vector.tensor_copy(
                            acol[:, j:j + 1], p_tr32[:, j:j + 1]))
                        vector.wait_ge(dv.h, dv.v)
                    block.vector(caj)
                acctx.close()
                barrier()

                emit_proj(ph, "bq", b.winT, 0, NC, NC, 2, (0, 1),
                          lambda dc, tt: b.xn_sb[:, dc * T + tt * 512:
                                                 dc * T + (tt + 1) * 512],
                          EvacBias(b.pB, 0, b.c_bin3, 0, T), pre_wait=b.pre_tr)
                barrier()
                emit_rope(b.pB)

                def yq(vector):
                    for c in range(NC):
                        dv.inc(vector.tensor_scalar(
                            b.pB[:, c * T:(c + 1) * T],
                            b.pB[:, c * T:(c + 1) * T],
                            acol[:, c:c + 1], None, ALU.mult))
                        vector.wait_ge(dv.h, dv.v)
                block.vector(yq)
                barrier()

                class EvacWO:
                    ndv, nac = 2, 0

                    def dv(self, vector, ec, pap):
                        rsl = b.r_sb[:, ec * T:(ec + 1) * T]
                        xsl = b.xn_sb[:, ec * T:(ec + 1) * T]
                        vector.wait_ge(dv.h, dv.v)
                        dv.inc(vector.scalar_tensor_tensor(
                            tmpf[:, :], pap[:, 0:T],
                            b.c_bout1[:, ec:ec + 1], xsl,
                            ALU.add, ALU.add))
                        vector.wait_ge(dv.h, dv.v)
                        dv.inc(vector.tensor_tensor(
                            rsl, rsl, tmpf[:, :], ALU.add))
                emit_proj(ph, "bo", b.woutT, 0, NC, NC, 2, (0, 1),
                          lambda dc, tt: b.pB[:, dc * T + tt * 512:
                                              dc * T + (tt + 1) * 512],
                          EvacWO(), pre_wait=b.pre_tr)
                barrier()

        if "s4" in STAGES:
            emit_ffn("f4")

        def fin(sync):
            for j in range(4):
                qS.inc(sync.dma_start(
                    out=bass.AP(b.yt, j * 4 * 128 * T,
                                [[T, 128], [128 * T, 4], [1, T]]),
                    in_=b.r_sb[:, j * 4 * T:(j + 1) * 4 * T]), 16)
            sync.wait_ge(qS.h, qS.v)
        block.sync(fin)

    return nc


_NC_CACHE = {}


def _get_nc():
    key = STAGES
    if key not in _NC_CACHE:
        _NC_CACHE[key] = build_nc()
    return _NC_CACHE[key]


def _host_consts():
    rot = 1024
    half = rot // 2
    invf = 1.0 / (10000.0 ** (np.arange(0, rot, 2, dtype=np.float64) / rot))
    pos = np.arange(S, dtype=np.float64)
    fr = pos[:, None] * invf[None, :]
    fr = np.repeat(fr, 2, axis=1)
    cosf = np.cos(fr).astype(np.float32)
    sinf = np.sin(fr).astype(np.float32)
    stair = np.where(np.arange(512)[None, :] >= np.arange(512)[:, None],
                     0.0, -1e9).astype(np.float32).astype(bfnp)
    ident = np.eye(128, dtype=np.float32).astype(bfnp)
    ident32 = np.eye(128, dtype=np.float32)
    onescol = np.ones((128, 1), np.float32).astype(bfnp)
    onescol32 = np.ones((128, 1), np.float32)
    onesrow32 = np.ones((1, 128), np.float32)
    perm = np.zeros((128, 128), np.float32)
    for jj in range(64):
        perm[2 * jj + 1, 2 * jj] = -1.0
        perm[2 * jj, 2 * jj + 1] = 1.0
    perm = perm.astype(bfnp)
    return cosf, sinf, stair, ident, ident32, onescol, onescol32, onesrow32, perm


def _chunkcols(vec):
    return np.ascontiguousarray(np.asarray(vec, np.float32).reshape(-1, 128).T)


def kernel(**inputs):
    nc = _get_nc()
    (cosf, sinf, stair, ident, ident32, onescol, onescol32, onesrow32,
     perm) = _host_consts()

    X = np.asarray(inputs["L0_attn_a_X"], np.float32).reshape(S, D)
    shared = dict(
        stair=stair, ident=ident, ident32=ident32, onescol=onescol,
        onescol32=onescol32, onesrow32=onesrow32, perm=perm,
        lngb=np.concatenate([_chunkcols(inputs["L0_attn_a_ln_g"]),
                             _chunkcols(inputs["L0_attn_a_ln_b"])], axis=1),
        bqkvo=np.concatenate([_chunkcols(inputs["L0_attn_a_bq"]),
                              _chunkcols(inputs["L0_attn_a_bk"]),
                              _chunkcols(inputs["L0_attn_a_bv"]),
                              _chunkcols(inputs["L0_attn_a_bo"])], axis=1),
        bin3=_chunkcols(inputs["L0_attn_b_b_in"]),
        bout1=_chunkcols(inputs["L0_attn_b_b_out"]),
        gsc=_chunkcols(inputs["L0_ffn_gate_scale"]),
        usc=_chunkcols(inputs["L0_ffn_up_scale"]),
        dsc=_chunkcols(inputs["L0_ffn_down_scale"]),
        wq=np.asarray(inputs["L0_attn_a_Wq"], np.float32),
        wk=np.asarray(inputs["L0_attn_a_Wk"], np.float32),
        wv=np.asarray(inputs["L0_attn_a_Wv"], np.float32),
        wo=np.asarray(inputs["L0_attn_a_Wo"], np.float32),
        win=np.asarray(inputs["L0_attn_b_W_in"], np.float32),
        wout=np.asarray(inputs["L0_attn_b_W_out"], np.float32),
        gw=np.ascontiguousarray(np.asarray(inputs["L0_ffn_gate_w"], np.int32)),
        uw=np.ascontiguousarray(np.asarray(inputs["L0_ffn_up_w"], np.int32)),
        dw=np.ascontiguousarray(np.asarray(inputs["L0_ffn_down_w"], np.int32)),
    )

    in_maps = []
    for c in range(NCORES):
        bA, bB = c, 15 - c
        toks = np.concatenate([np.arange(bA * 512, (bA + 1) * 512),
                               np.arange(bB * 512, (bB + 1) * 512)])
        xt = np.ascontiguousarray(X[toks, :].T)
        cost = np.ascontiguousarray(cosf[toks, :].T).astype(bfnp)
        sint = np.ascontiguousarray(sinf[toks, :].T).astype(bfnp)
        mask = np.zeros((128, 23), np.float32)
        for bb_ in range(7):
            if bb_ > c - 1:
                mask[:, bb_] = NEG
        for bb_ in range(15):
            if bb_ > 14 - c:
                mask[:, 7 + bb_] = NEG
        m = dict(shared)
        m.update(xt=xt, cost=cost, sint=sint, maskcol=mask)
        in_maps.append(m)

    res = run_bass_kernel_spmd(nc, in_maps, core_ids=list(range(NCORES)))

    out = np.zeros((1, S, D), np.float32)
    for c in range(NCORES):
        ytc = np.asarray(res.results[c]["yt"], np.float32)
        bA, bB = c, 15 - c
        out[0, bA * 512:(bA + 1) * 512, :] = ytc[:, 0:512].T
        out[0, bB * 512:(bB + 1) * 512, :] = ytc[:, 512:1024].T
    return out


# version 08 (final)
# speedup vs baseline: 1.2920x; 1.0894x over previous
import math
from contextlib import ExitStack

import numpy as np
import ml_dtypes

import concourse.bass as bass
import concourse.mybir as mybir
from concourse.bass_utils import run_bass_kernel_spmd

F32 = mybir.dt.float32
BF16 = mybir.dt.bfloat16
I32 = mybir.dt.int32
AF = mybir.ActivationFunctionType
ALU = mybir.AluOpType
AX = mybir.AxisListType.X
bfnp = ml_dtypes.bfloat16

D = 2048
S = 8192
T = 1024
NH = 16
NC = 16
FI = 8192
NCI = 64
E3 = 6144
SCALE = 1.0 / math.sqrt(128.0)
EPS = 1e-5
NCORES = 8
CROWS = 2 * D + NH
NEG = -1.0e30

STAGES = ("s1", "s2", "s3", "s4")


def rank_of_block(b):
    return b if b <= 7 else 15 - b


def col_of_block(b):
    return 0 if b <= 7 else 512


class Sem:
    def __init__(self, ctx, nc, name):
        self.h = ctx.enter_context(nc.semaphore(name))
        self.v = 0

    def inc(self, instr, n=1):
        instr.then_inc(self.h, n)
        self.v += n
        return self.v


class B:
    pass


def build_nc():
    nc = bass.Bass("TRN2", target_bir_lowering=False)
    b = B()
    b.nc = nc

    def di(name, shape, dt):
        return nc.dram_tensor(name, shape, dt, kind="ExternalInput")

    b.xt = di("xt", [D, T], F32)
    b.cost = di("cost", [1024, T], BF16)
    b.sint = di("sint", [1024, T], BF16)
    b.stair = di("stair", [512, 512], BF16)
    b.maskcol = di("maskcol", [128, 23], F32)
    b.lngb = di("lngb", [128, 32], F32)
    b.bqkvo = di("bqkvo", [128, 64], F32)
    b.bin3 = di("bin3", [128, 48], F32)
    b.bout1 = di("bout1", [128, 16], F32)
    b.gsc = di("gsc", [128, 64], F32)
    b.usc = di("usc", [128, 64], F32)
    b.dsc = di("dsc", [128, 16], F32)
    b.ident = di("ident", [128, 128], BF16)
    b.ident32 = di("ident32", [128, 128], F32)
    b.onescol = di("onescol", [128, 1], BF16)
    b.onesrow32 = di("onesrow32", [1, 128], F32)
    b.onescol32 = di("onescol32", [128, 1], F32)
    b.perm = di("perm", [128, 128], BF16)
    b.wq = di("wq", [D, D], F32)
    b.wk = di("wk", [D, D], F32)
    b.wv = di("wv", [D, D], F32)
    b.wo = di("wo", [D, D], F32)
    b.win = di("win", [E3, D], F32)
    b.wout = di("wout", [D, D], F32)
    b.gw = di("gw", [FI, D], I32)
    b.uw = di("uw", [FI, D], I32)
    b.dw = di("dw", [D, FI], I32)

    b.yt = nc.dram_tensor("yt", [D, T], F32, kind="ExternalOutput")

    b.wq_b = nc.dram_tensor("wq_b", [D, D], BF16)
    b.wk_b = nc.dram_tensor("wk_b", [D, D], BF16)
    b.wv_b = nc.dram_tensor("wv_b", [D, D], BF16)
    b.wo_b = nc.dram_tensor("wo_b", [D, D], BF16)
    b.win_c = nc.dram_tensor("win_c", [E3, D], BF16)
    b.wout_c = nc.dram_tensor("wout_c", [D, D], BF16)
    b.gw_c = nc.dram_tensor("gw_c", [FI, D], BF16)
    b.uw_c = nc.dram_tensor("uw_c", [FI, D], BF16)
    b.dw_c = nc.dram_tensor("dw_c", [D, FI], BF16)
    b.winT = nc.dram_tensor("winT", [D, E3], BF16)
    b.woutT = nc.dram_tensor("woutT", [D, D], BF16)
    b.gwT = nc.dram_tensor("gwT", [D, FI], BF16)
    b.uwT = nc.dram_tensor("uwT", [D, FI], BF16)
    b.dwT = nc.dram_tensor("dwT", [FI, D], BF16)
    b.kvloc = nc.dram_tensor("kvloc", [CROWS, T], BF16)
    b.kvg = nc.dram_tensor("kvg", [NCORES * CROWS, T], BF16)
    b.vtokg = nc.dram_tensor("vtokg", [S, D], BF16)
    b.vtokl = nc.dram_tensor("vtokl", [T, D], BF16)
    b.abuf_i = nc.dram_tensor("abuf_i", [1, D], F32)
    b.abuf_o = nc.dram_tensor("abuf_o", [1, D], F32)

    ctx = ExitStack()
    with ctx:
        b.qS = Sem(ctx, nc, "qS")
        b.cc = Sem(ctx, nc, "cc")
        b.pe = Sem(ctx, nc, "pe")
        b.dv = Sem(ctx, nc, "dv")
        b.ac = Sem(ctx, nc, "ac")
        b.bars = {n: Sem(ctx, nc, f"bar_{n}")
                  for n in ("sync", "tensor", "vector", "scalar", "gpsimd")}
        b.qW = [Sem(ctx, nc, f"qW{i}") for i in range(4)]
        b.qK = [Sem(ctx, nc, f"qK{i}") for i in range(3)]
        b.qV = [Sem(ctx, nc, f"qV{i}") for i in range(3)]
        b.qTA = [Sem(ctx, nc, f"qTA{i}") for i in range(4)]
        b.qTS = [Sem(ctx, nc, f"qTS{i}") for i in range(4)]
        b.qST = Sem(ctx, nc, "qST")
        b.qRP = Sem(ctx, nc, "qRP")
        b.qKN = Sem(ctx, nc, "qKN")
        b.qSTR = Sem(ctx, nc, "qSTR")
        b.qAB = Sem(ctx, nc, "qAB")
        b.qC = {n: Sem(ctx, nc, f"qC_{n}")
                for n in ("wk", "wv", "wq", "wo", "gw", "uw", "dw",
                          "win", "wout")}

        def sbt(name, shape, dt):
            return ctx.enter_context(nc.sbuf_tensor(name, shape, dt))

        b.r_sb = sbt("r_sb", [128, NC * T], F32)
        b.xn_sb = sbt("xn_sb", [128, NC * T], BF16)
        b.pA = sbt("pA", [128, NC * T], BF16)
        b.pB = sbt("pB", [128, NC * T], BF16)
        b.stat = sbt("stat", [1, 3 * T], F32)
        b.srow = sbt("srow", [1, 2 * T], F32)
        b.c_lngb = sbt("c_lngb", [128, 32], F32)
        b.c_bqkvo = sbt("c_bqkvo", [128, 64], F32)
        b.c_bin3 = sbt("c_bin3", [128, 48], F32)
        b.c_bout1 = sbt("c_bout1", [128, 16], F32)
        b.c_gsc = sbt("c_gsc", [128, 64], F32)
        b.c_usc = sbt("c_usc", [128, 64], F32)
        b.c_dsc = sbt("c_dsc", [128, 16], F32)
        b.c_mask = sbt("c_mask", [128, 23], F32)
        b.c_id = sbt("c_id", [128, 128], BF16)
        b.c_id32 = sbt("c_id32", [128, 128], F32)
        b.c_oc = sbt("c_oc", [128, 1], BF16)
        b.c_or32 = sbt("c_or32", [1, 128], F32)
        b.c_oc32 = sbt("c_oc32", [128, 1], F32)
        b.c_perm = sbt("c_perm", [128, 128], BF16)

        b.block = ctx.enter_context(nc.Block())
        block = b.block
        qS, cc, pe, dv, ac = b.qS, b.cc, b.pe, b.dv, b.ac
        bars, qW, qK, qV = b.bars, b.qW, b.qK, b.qV
        qTA, qTS, qST, qRP, qKN = b.qTA, b.qTS, b.qST, b.qRP, b.qKN
        qSTR, qAB, qC = b.qSTR, b.qAB, b.qC

        def barrier(gp=True):
            drains = {
                "sync": [(s, s.v) for s in
                         [qS, qST, qRP, qKN, qSTR] + qW + qTA if s.v > 0],
                "scalar": ([(ac, ac.v)] if ac.v else []) +
                          [(s, s.v) for s in qV if s.v > 0],
                "gpsimd": [(s, s.v) for s in
                           [cc, qAB] + qTS + list(qC.values()) if s.v > 0],
                "tensor": [(pe, pe.v)] if pe.v else [],
                "vector": [(dv, dv.v)] if dv.v else [],
            }
            names = ["sync", "tensor", "vector", "scalar"] + (
                ["gpsimd"] if gp else [])
            for n in names:
                bars[n].v += 1
            waits = [(bars[n].h, bars[n].v) for n in names]

            def mk(n):
                def f(e):
                    for s, v in drains[n]:
                        e.wait_ge(s.h, v)
                    e.nop().then_inc(bars[n].h, 1)
                    for hh, vv in waits:
                        e.wait_ge(hh, vv)
                return f
            for n in names:
                getattr(block, n)(mk(n))

        def gate_p0():
            bars["sync"].v += 1
            k = bars["sync"].v

            def g_sync(e):
                e.wait_ge(qS.h, qS.v)
                e.nop().then_inc(bars["sync"].h, 1)
            block.sync(g_sync)

            def g_wait(e):
                e.wait_ge(bars["sync"].h, k)
            for en in ("tensor", "vector", "scalar", "gpsimd"):
                getattr(block, en)(g_wait)

        def p0(sync):
            for d, s in [
                (b.c_lngb, b.lngb), (b.c_bqkvo, b.bqkvo), (b.c_bin3, b.bin3),
                (b.c_bout1, b.bout1), (b.c_gsc, b.gsc), (b.c_usc, b.usc),
                (b.c_dsc, b.dsc), (b.c_mask, b.maskcol), (b.c_id, b.ident),
                (b.c_id32, b.ident32), (b.c_oc, b.onescol),
                (b.c_or32, b.onesrow32), (b.c_oc32, b.onescol32),
                (b.c_perm, b.perm),
            ]:
                qS.inc(sync.dma_start(out=d[:, :], in_=s[:, :]), 16)
            for j in range(4):
                qS.inc(sync.dma_start(
                    out=b.r_sb[:, j * 4 * T:(j + 1) * 4 * T],
                    in_=bass.AP(b.xt, j * 4 * 128 * T,
                                [[T, 128], [128 * T, 4], [1, T]])), 16)
        block.sync(p0)
        gate_p0()

        cast_done = {}

        def p1_g(g):
            jobs = [
                ("wk", b.wk_b, b.wk, D), ("wv", b.wv_b, b.wv, D),
                ("wq", b.wq_b, b.wq, D),
                ("gw", b.gw_c, b.gw, FI), ("uw", b.uw_c, b.uw, FI),
                ("dw", b.dw_c, b.dw, D), ("win", b.win_c, b.win, E3),
                ("wout", b.wout_c, b.wout, D), ("wo", b.wo_b, b.wo, D),
            ]
            for name, dst, src, rows in jobs:
                q = qC[name]
                for r0 in range(0, rows, 512):
                    q.inc(g.dma_start(out=dst[r0:r0 + 512, :],
                                      in_=src[r0:r0 + 512, :]), 16)
                cast_done[name] = (q, q.v)
        block.gpsimd(p1_g)

        RG = 1024
        tr_plan = []
        for name, src_, dst, R, C in [
            ("gw", b.gw_c, b.gwT, FI, D), ("uw", b.uw_c, b.uwT, FI, D),
            ("dw", b.dw_c, b.dwT, D, FI), ("win", b.win_c, b.winT, E3, D),
            ("wout", b.wout_c, b.woutT, D, D),
        ]:
            for c0 in range(0, C, 128):
                for r0 in range(0, R, RG):
                    tr_plan.append((name, src_, dst, r0, c0))
        nTR = len(tr_plan)
        b.tr_i = [0]
        b.xp_n = [0]

        def emit_xpose(items):
            marks = {}
            base = b.xp_n[0]
            b.xp_n[0] += len(items)

            def t_sync(sync):
                for j, (waits, src_ap, dst_ap) in enumerate(items):
                    i = base + j
                    for sh, sv in waits:
                        sync.wait_ge(sh, sv)
                    if i >= 4:
                        sync.wait_ge(qTS[i % 4].h, 16 * (i // 4))
                    qTA[i % 4].inc(sync.dma_start(
                        out=b.tstg[i % 4][:, :], in_=src_ap,
                        transpose=True), 16)
                    marks[i] = qTA[i % 4].v
            block.sync(t_sync)

            def t_gp(g):
                for j, (waits, src_ap, dst_ap) in enumerate(items):
                    i = base + j
                    g.wait_ge(qTA[i % 4].h, marks[i])
                    qTS[i % 4].inc(g.dma_start(
                        out=dst_ap, in_=b.tstg[i % 4][:, :]), 16)
            block.gpsimd(t_gp)

        def emit_tr_batch(n):
            lo = b.tr_i[0]
            hi = min(lo + n, nTR)
            if hi <= lo:
                return
            b.tr_i[0] = hi
            items = []
            for i in range(lo, hi):
                name, src_, dst, r0, c0 = tr_plan[i]
                q, v = cast_done[name]
                items.append((
                    [(q.h, v)],
                    src_[r0:r0 + RG, c0:c0 + 128],
                    dst[c0:c0 + 128, r0:r0 + RG]))
            emit_xpose(items)


        b.ln_id = [0]

        def emit_ln(ph, double):
            b.ln_id[0] += 1
            u = b.ln_id[0]
            lnctx = ExitStack()
            b.lnA = lnctx.enter_context(nc.sbuf_tensor(f"lnA{u}", [128, T], F32))
            b.lnB = lnctx.enter_context(nc.sbuf_tensor(f"lnB{u}", [128, T], F32))
            p_s1 = lnctx.enter_context(nc.psum_tensor(f"ln_s1_{u}", [1, T], F32))
            p_s2 = lnctx.enter_context(nc.psum_tensor(f"ln_s2_{u}", [1, T], F32))
            p_bc = lnctx.enter_context(nc.psum_tensor(f"ln_bc_{u}", [128, T], F32))
            p_bc2 = lnctx.enter_context(nc.psum_tensor(f"ln_bc2_{u}", [128, T], F32))

            def sA(scalar):
                for c in range(NC):
                    ac.inc(scalar.activation(
                        b.xn_sb[:, c * T:(c + 1) * T],
                        b.r_sb[:, c * T:(c + 1) * T], AF.Copy))
            block.scalar(sA)
            acA = ac.v

            def tA(tensor):
                tensor.wait_ge(ac.h, acA)
                for tt in range(2):
                    for c in range(NC):
                        m = tensor.matmul(
                            p_s1[:, tt * 512:(tt + 1) * 512], b.c_oc[:, :],
                            b.xn_sb[:, c * T + tt * 512:c * T + (tt + 1) * 512],
                            start=(c == 0), stop=(c == NC - 1))
                pe.inc(m)
            block.tensor(tA)
            peA = pe.v

            def sB(scalar):
                scalar.wait_ge(pe.h, peA)
                for c in range(NC):
                    ac.inc(scalar.activation(
                        b.xn_sb[:, c * T:(c + 1) * T],
                        b.r_sb[:, c * T:(c + 1) * T], AF.Square))
            block.scalar(sB)
            acB = ac.v

            def tB(tensor):
                tensor.wait_ge(ac.h, acB)
                for tt in range(2):
                    for c in range(NC):
                        m = tensor.matmul(
                            p_s2[:, tt * 512:(tt + 1) * 512], b.c_oc[:, :],
                            b.xn_sb[:, c * T + tt * 512:c * T + (tt + 1) * 512],
                            start=(c == 0), stop=(c == NC - 1))
                pe.inc(m)
            block.tensor(tB)
            peB = pe.v

            mu = b.stat[:, 0:T]
            var = b.stat[:, T:2 * T]
            al = b.stat[:, 2 * T:3 * T]
            be = b.stat[:, T:2 * T]
            t1 = b.srow[:, 0:T]
            t2 = b.srow[:, T:2 * T]

            def v1(vector):
                vector.wait_ge(pe.h, peB)
                dv.inc(vector.tensor_scalar(mu, p_s1[:, :], 1.0 / D, None, ALU.mult))
                dv.inc(vector.tensor_scalar(var, p_s2[:, :], 1.0 / D, None, ALU.mult))
                vector.wait_ge(dv.h, dv.v)
                dv.inc(vector.tensor_tensor(t1, mu, mu, ALU.mult))
                vector.wait_ge(dv.h, dv.v)
                dv.inc(vector.tensor_tensor(var, var, t1, ALU.subtract))
                vector.wait_ge(dv.h, dv.v)
            block.vector(v1)
            dv1 = dv.v

            def v1b(vector):
                vector.wait_ge(dv.h, dv1)
                dv.inc(vector.tensor_scalar(t1, var, EPS, None, ALU.add))
                vector.wait_ge(dv.h, dv.v)
            block.vector(v1b)
            dv1b = dv.v

            def s1(scalar):
                scalar.wait_ge(dv.h, dv1b)
                ac.inc(scalar.activation(t1, t1, AF.Sqrt))
            block.scalar(s1)
            ac1 = ac.v

            def v2(vector):
                vector.wait_ge(ac.h, ac1)
                dv.inc(vector.reciprocal(al, t1))
                vector.wait_ge(dv.h, dv.v)
                if double:
                    dv.inc(vector.tensor_tensor(t2, al, al, ALU.mult))
                    vector.wait_ge(dv.h, dv.v)
                    dv.inc(vector.tensor_tensor(t2, var, t2, ALU.mult))
                    vector.wait_ge(dv.h, dv.v)
            block.vector(v2)
            dv2 = dv.v

            if double:
                def v2b(vector):
                    vector.wait_ge(dv.h, dv2)
                    dv.inc(vector.tensor_scalar(t2, t2, EPS, None, ALU.add))
                    vector.wait_ge(dv.h, dv.v)
                block.vector(v2b)
                dv2b = dv.v

                def s2(scalar):
                    scalar.wait_ge(dv.h, dv2b)
                    ac.inc(scalar.activation(t2, t2, AF.Sqrt))
                block.scalar(s2)
                ac2 = ac.v

                def v3(vector):
                    vector.wait_ge(ac.h, ac2)
                    dv.inc(vector.reciprocal(t1, t2))
                    vector.wait_ge(dv.h, dv.v)
                    dv.inc(vector.tensor_tensor(al, al, t1, ALU.mult))
                    vector.wait_ge(dv.h, dv.v)
                block.vector(v3)

            def v4(vector):
                vector.wait_ge(dv.h, dv.v)
                dv.inc(vector.tensor_tensor(be, mu, al, ALU.mult))
                vector.wait_ge(dv.h, dv.v)
                dv.inc(vector.tensor_scalar(be, be, -1.0, None, ALU.mult))
                vector.wait_ge(dv.h, dv.v)
            block.vector(v4)
            dv4 = dv.v

            def tC(tensor):
                tensor.wait_ge(dv.h, dv4)
                for tt in range(2):
                    m = tensor.matmul(p_bc[:, tt * 512:(tt + 1) * 512],
                                      b.c_or32[:, :],
                                      al[0:1, tt * 512:(tt + 1) * 512],
                                      start=True, stop=True)
                    m = tensor.matmul(p_bc2[:, tt * 512:(tt + 1) * 512],
                                      b.c_or32[:, :],
                                      be[0:1, tt * 512:(tt + 1) * 512],
                                      start=True, stop=True)
                pe.inc(m)
            block.tensor(tC)
            peC = pe.v

            def v5(vector):
                vector.wait_ge(pe.h, peC)
                dv.inc(vector.tensor_copy(b.lnA[:, :], p_bc[:, :]))
                dv.inc(vector.tensor_copy(b.lnB[:, :], p_bc2[:, :]))
                vector.wait_ge(dv.h, dv.v)
                for c in range(NC):
                    dv.inc(vector.tensor_tensor(
                        b.xn_sb[:, c * T:(c + 1) * T],
                        b.r_sb[:, c * T:(c + 1) * T], b.lnA[:, :], ALU.mult))
                    vector.wait_ge(dv.h, dv.v)
                    dv.inc(vector.tensor_tensor(
                        b.xn_sb[:, c * T:(c + 1) * T],
                        b.xn_sb[:, c * T:(c + 1) * T], b.lnB[:, :], ALU.add))
                    vector.wait_ge(dv.h, dv.v)
            block.vector(v5)
            dv5 = dv.v

            if double:
                def s3(scalar):
                    scalar.wait_ge(dv.h, dv5)
                    for c in range(NC):
                        ac.inc(scalar.activation(
                            b.xn_sb[:, c * T:(c + 1) * T],
                            b.xn_sb[:, c * T:(c + 1) * T], AF.Identity,
                            bias=b.c_lngb[:, 16 + c:17 + c],
                            scale=b.c_lngb[:, c:c + 1]))
                block.scalar(s3)
            lnctx.close()
            barrier(gp=False)

        def emit_proj(ph, name, wten, wcol0, n_d, n_e, eg, tts, rhs_fn, evac,
                      pre_wait=()):
            width = 512 * len(tts)
            NBUF = 4
            prctx = ExitStack()
            wbuf = [prctx.enter_context(
                nc.sbuf_tensor(f"{name}_w{i}", [128, eg * 128], BF16))
                for i in range(NBUF)]
            pp = [prctx.enter_context(
                nc.psum_tensor(f"{name}_p{i}", [128, width], F32))
                for i in range(2 * eg)]
            n_g = n_e // eg
            steps = [(g, dc) for g in range(n_g) for dc in range(n_d)]
            ns = len(steps)
            pe0 = pe.v
            loadm = [qW[i % 4].v + 16 * (i // 4 + 1) for i in range(ns)]
            mmm = [pe0 + (i + 1) for i in range(ns)]
            dv0, ac0 = dv.v, ac.v
            ndv, nac = evac.ndv, evac.nac
            evdv = [dv0 + (g + 1) * ndv * eg for g in range(n_g)]
            evac_ = [ac0 + (g + 1) * nac * eg for g in range(n_g)]

            def psync(sync):
                for s_, v_ in pre_wait:
                    sync.wait_ge(s_.h, v_)
                for i, (g, dc) in enumerate(steps):
                    if i >= NBUF:
                        sync.wait_ge(pe.h, mmm[i - NBUF])
                    qW[i % 4].inc(sync.dma_start(
                        out=wbuf[i % NBUF][:, :],
                        in_=wten[dc * 128:(dc + 1) * 128,
                                 wcol0 + g * eg * 128:
                                 wcol0 + (g + 1) * eg * 128]), 16)
            block.sync(psync)
            assert qW[(ns - 1) % 4].v == loadm[-1]

            def ptens(tensor):
                for i, (g, dc) in enumerate(steps):
                    tensor.wait_ge(qW[i % 4].h, loadm[i])
                    if dc == 0 and g >= 2:
                        if ndv:
                            tensor.wait_ge(dv.h, evdv[g - 2])
                        if nac:
                            tensor.wait_ge(ac.h, evac_[g - 2])
                    m = None
                    for e2 in range(eg):
                        for j, tt in enumerate(tts):
                            m = tensor.matmul(
                                pp[(g % 2) * eg + e2][:, j * 512:(j + 1) * 512],
                                wbuf[i % NBUF][:, e2 * 128:(e2 + 1) * 128],
                                rhs_fn(dc, tt),
                                start=(dc == 0), stop=(dc == n_d - 1))
                    pe.inc(m)
            block.tensor(ptens)
            assert pe.v == mmm[-1]

            if ndv:
                def pvec(vector):
                    for g in range(n_g):
                        vector.wait_ge(pe.h, mmm[(g + 1) * n_d - 1])
                        for e2 in range(eg):
                            evac.dv(vector, g * eg + e2,
                                    pp[(g % 2) * eg + e2])
                block.vector(pvec)
                assert dv.v == evdv[-1], (name, dv.v, evdv[-1])
            if nac:
                def psca(scalar):
                    for g in range(n_g):
                        scalar.wait_ge(pe.h, mmm[(g + 1) * n_d - 1])
                        for e2 in range(eg):
                            evac.ac(scalar, g * eg + e2,
                                    pp[(g % 2) * eg + e2])
                block.scalar(psca)
                assert ac.v == evac_[-1], (name, ac.v, evac_[-1])
            prctx.close()

        class EvacBias:
            ndv, nac = 0, 1

            def __init__(self, dst, dstoff, bias_sb, bcol0, width):
                self.dst, self.off, self.bias, self.b0 = dst, dstoff, bias_sb, bcol0
                self.w = width

            def ac(self, scalar, ec, pap):
                ac.inc(scalar.activation(
                    self.dst[:, self.off + ec * self.w:
                             self.off + (ec + 1) * self.w],
                    pap[:, 0:self.w], AF.Identity,
                    bias=self.bias[:, self.b0 + ec:self.b0 + ec + 1]))

        class EvacResid:
            ndv, nac = 1, 0

            def __init__(self, bias_sb, bcol0, width):
                self.bias, self.b0, self.w = bias_sb, bcol0, width

            def dv(self, vector, ec, pap):
                if self.bias is not None:
                    dv.inc(vector.scalar_tensor_tensor(
                        b.r_sb[:, ec * T:ec * T + self.w],
                        pap[:, 0:self.w],
                        self.bias[:, self.b0 + ec:self.b0 + ec + 1],
                        b.r_sb[:, ec * T:ec * T + self.w],
                        ALU.add, ALU.add))
                else:
                    dv.inc(vector.tensor_tensor(
                        b.r_sb[:, ec * T:ec * T + self.w],
                        pap[:, 0:self.w],
                        b.r_sb[:, ec * T:ec * T + self.w], ALU.add))

        if "s1" in STAGES:
            with ExitStack() as ph:
                b.tstg = [ph.enter_context(
                    nc.sbuf_tensor(f"tstg{i}", [128, RG], BF16))
                    for i in range(4)]
                emit_ln(ph, double=True)

                c_stair = ph.enter_context(
                    nc.sbuf_tensor("c_stair", [128, 4 * 512], BF16))

                def ldst(sync):
                    qSTR.inc(sync.dma_start(
                        out=c_stair[:, :],
                        in_=bass.AP(b.stair, 0,
                                    [[512, 128], [128 * 512, 4], [1, 512]])),
                        16)
                block.sync(ldst)

                emit_proj(ph, "kp", b.wk_b, 0, NC, NC, 2, (0, 1),
                          lambda dc, tt: b.xn_sb[:, dc * T + tt * 512:
                                                 dc * T + (tt + 1) * 512],
                          EvacBias(b.pA, 0, b.c_bqkvo, 16, T),
                          pre_wait=[(qC["wk"], cast_done["wk"][1])])
                barrier(gp=False)

                knctx = ExitStack()
                knrow = knctx.enter_context(nc.sbuf_tensor("knrow", [1, T], BF16))
                sqt = knctx.enter_context(nc.sbuf_tensor("sqt", [128, T], BF16))
                p_kn = knctx.enter_context(nc.psum_tensor("p_kn", [1, T], F32))
                pe_prev = pe.v
                dv_prev = dv.v
                qs_prev = qST.v
                for h in range(NH):
                    def ksq(scalar, h=h, pe_prev=pe_prev):
                        scalar.wait_ge(pe.h, pe_prev)
                        ac.inc(scalar.activation(
                            sqt[:, :], b.pA[:, h * T:(h + 1) * T], AF.Square))
                    block.scalar(ksq)
                    ac_h = ac.v

                    def kmm(tensor, h=h, ac_h=ac_h, dv_prev=dv_prev):
                        tensor.wait_ge(ac.h, ac_h)
                        tensor.wait_ge(dv.h, dv_prev)
                        for tt in range(2):
                            m = tensor.matmul(
                                p_kn[:, tt * 512:(tt + 1) * 512], b.c_oc[:, :],
                                sqt[:, tt * 512:(tt + 1) * 512],
                                start=True, stop=True)
                        pe.inc(m)
                    block.tensor(kmm)
                    pe_h = pe.v
                    pe_prev = pe_h

                    def kev(vector, h=h, pe_h=pe_h, qs_prev=qs_prev):
                        vector.wait_ge(pe.h, pe_h)
                        vector.wait_ge(qST.h, qs_prev)
                        dv.inc(vector.tensor_copy(knrow[:, :], p_kn[:, :]))
                        vector.wait_ge(dv.h, dv.v)
                    block.vector(kev)
                    dv_h = dv.v
                    dv_prev = dv_h

                    def kst(sync, h=h, dv_h=dv_h):
                        sync.wait_ge(dv.h, dv_h)
                        qST.inc(sync.dma_start(
                            out=b.kvloc[2 * D + h:2 * D + h + 1, :],
                            in_=knrow[:, :]), 16)
                        qST.inc(sync.dma_start(
                            out=b.kvloc[h * 128:(h + 1) * 128, :],
                            in_=b.pA[:, h * T:(h + 1) * T]), 16)
                    block.sync(kst)
                    qs_prev = qST.v
                knctx.close()
                barrier(gp=False)

                emit_proj(ph, "vp", b.wv_b, 0, NC, NC, 2, (0, 1),
                          lambda dc, tt: b.xn_sb[:, dc * T + tt * 512:
                                                 dc * T + (tt + 1) * 512],
                          EvacBias(b.pA, 0, b.c_bqkvo, 32, T),
                          pre_wait=[(qC["wv"], cast_done["wv"][1])])
                barrier(gp=False)

                def vst(sync):
                    for h in range(NH):
                        qST.inc(sync.dma_start(
                            out=b.kvloc[D + h * 128:D + (h + 1) * 128, :],
                            in_=b.pA[:, h * T:(h + 1) * T]), 16)
                block.sync(vst)
                qs_vst = qST.v

                def gat(g):
                    g.wait_ge(qST.h, qs_vst)
                    cc.inc(g.collective_compute(
                        "AllGather", ALU.bypass,
                        replica_groups=[list(range(NCORES))],
                        ins=[b.kvloc.ap().opt()], outs=[b.kvg.ap().opt()]))
                block.gpsimd(gat)
                cc_gather = cc.v

                def qgate(scalar):
                    scalar.wait_ge(qST.h, qs_vst)
                block.scalar(qgate)
                emit_proj(ph, "qp", b.wq_b, 0, NC, NC, 2, (0, 1),
                          lambda dc, tt: b.xn_sb[:, dc * T + tt * 512:
                                                 dc * T + (tt + 1) * 512],
                          EvacBias(b.pA, 0, b.c_bqkvo, 0, T),
                          pre_wait=[(qC["wq"], cast_done["wq"][1])])
                barrier(gp=False)

                vt_items = []
                for r in range(NCORES):
                    for tc in range(8):
                        for hf in range(2):
                            vt_items.append((
                                [(cc.h, cc_gather)],
                                b.kvg[r * CROWS + D + hf * 1024:
                                      r * CROWS + D + (hf + 1) * 1024,
                                      tc * 128:(tc + 1) * 128],
                                b.vtokg[r * T + tc * 128:r * T + (tc + 1) * 128,
                                        hf * 1024:(hf + 1) * 1024]))
                for tc in range(8):
                    for hf in range(2):
                        vt_items.append((
                            [(qST.h, qs_vst)],
                            b.kvloc[D + hf * 1024:D + (hf + 1) * 1024,
                                    tc * 128:(tc + 1) * 128],
                            b.vtokl[tc * 128:(tc + 1) * 128,
                                    hf * 1024:(hf + 1) * 1024]))
                emit_xpose(vt_items)
                vt_done = [(s, s.v) for s in qTS]

                bias_hb = ph.enter_context(
                    nc.sbuf_tensor("bias_hb", [128, 32], F32))
                qnctx = ExitStack()
                sqt = qnctx.enter_context(nc.sbuf_tensor("sqt2", [128, T], BF16))
                knall = qnctx.enter_context(
                    nc.sbuf_tensor("knall", [1, 2 * T], BF16))
                p_qn = qnctx.enter_context(nc.psum_tensor("p_qn", [1, T], F32))
                p_bb = qnctx.enter_context(nc.psum_tensor("p_bb", [128, 32], F32))
                mrow = b.srow
                pe_prev = pe.v
                dv_prev = dv.v
                for h in range(NH):
                    def qsq(scalar, h=h, pe_prev=pe_prev):
                        scalar.wait_ge(pe.h, pe_prev)
                        ac.inc(scalar.activation(
                            sqt[:, :], b.pA[:, h * T:(h + 1) * T], AF.Square))
                    block.scalar(qsq)
                    ac_h = ac.v

                    def qmm(tensor, h=h, ac_h=ac_h, dv_prev=dv_prev):
                        tensor.wait_ge(ac.h, ac_h)
                        tensor.wait_ge(dv.h, dv_prev)
                        for tt in range(2):
                            m = tensor.matmul(
                                p_qn[:, tt * 512:(tt + 1) * 512], b.c_oc[:, :],
                                sqt[:, tt * 512:(tt + 1) * 512],
                                start=True, stop=True)
                        pe.inc(m)
                    block.tensor(qmm)
                    pe_h = pe.v
                    pe_prev = pe_h

                    for pp_ in range(4):
                        def qld(sync, h=h, pp_=pp_, dv_prev=dv_prev):
                            sync.wait_ge(cc.h, cc_gather)
                            sync.wait_ge(dv.h, dv_prev)
                            qKN.inc(sync.dma_start(
                                out=knall[:, :],
                                in_=bass.AP(b.kvg,
                                            (2 * pp_ * CROWS + 2 * D + h) * T,
                                            [[CROWS * T, 2], [1, T]])), 16)
                        block.sync(qld)
                        qs_hp = qKN.v

                        def qrd(vector, h=h, pp_=pp_, qs_hp=qs_hp,
                                pe_h=pe_h):
                            if pp_ == 0:
                                vector.wait_ge(pe.h, pe_h)
                                vector.wait_ge(dv.h, dv.v)
                                for blk in range(2):
                                    dv.inc(vector.reduce_max(
                                        mrow[:, 2 * h + blk:2 * h + blk + 1],
                                        p_qn[:, blk * 512:(blk + 1) * 512], AX))
                            vector.wait_ge(qKN.h, qs_hp)
                            vector.wait_ge(dv.h, dv.v)
                            slot = 32 if pp_ == 0 else 160
                            dv.inc(vector.reduce_max(
                                mrow[:, slot + h:slot + h + 1], knall[:, :], AX))
                            vector.wait_ge(dv.h, dv.v)
                            if pp_ > 0:
                                dv.inc(vector.tensor_tensor(
                                    mrow[:, 32 + h:33 + h],
                                    mrow[:, 32 + h:33 + h],
                                    mrow[:, 160 + h:161 + h], ALU.max))
                                vector.wait_ge(dv.h, dv.v)
                        block.vector(qrd)
                        dv_prev = dv.v
                    dv_prev = dv.v

                def mkb(vector):
                    vector.wait_ge(dv.h, dv.v)
                    dv.inc(vector.tensor_tensor(
                        bass.AP(b.srow, 96, [[2048, 1], [2, 16], [1, 2]]),
                        bass.AP(b.srow, 0, [[2048, 1], [2, 16], [1, 2]]),
                        bass.AP(b.srow, 32, [[2048, 1], [1, 16], [0, 2]]),
                        ALU.mult))
                    vector.wait_ge(dv.h, dv.v)
                block.vector(mkb)
                dv_m = dv.v

                def mkb2(scalar):
                    scalar.wait_ge(dv.h, dv_m)
                    ac.inc(scalar.activation(
                        mrow[:, 96:128], mrow[:, 96:128], AF.Sqrt))
                block.scalar(mkb2)
                ac_m = ac.v

                def mkb3(vector):
                    vector.wait_ge(ac.h, ac_m)
                    dv.inc(vector.tensor_scalar(
                        mrow[:, 96:128], mrow[:, 96:128], -SCALE, None, ALU.mult))
                    vector.wait_ge(dv.h, dv.v)
                block.vector(mkb3)
                dv_m2 = dv.v

                def mkb4(tensor):
                    tensor.wait_ge(dv.h, dv_m2)
                    pe.inc(tensor.matmul(p_bb[:, :], b.c_or32[:, :],
                                         mrow[0:1, 96:128], start=True, stop=True))
                block.tensor(mkb4)
                pe_m = pe.v

                def mkb5(vector):
                    vector.wait_ge(pe.h, pe_m)
                    dv.inc(vector.tensor_copy(bias_hb[:, :], p_bb[:, :]))
                    vector.wait_ge(dv.h, dv.v)
                block.vector(mkb5)
                qnctx.close()
                barrier(gp=False)

                flctx = ExitStack()
                pST = [flctx.enter_context(
                    nc.psum_tensor(f"pST{i}", [128, 1024], F32))
                    for i in range(3)]
                pAV = flctx.enter_context(nc.psum_tensor("pAV", [128, 512], F32))
                pL = flctx.enter_context(nc.psum_tensor("pL", [1, 512], F32))
                pLB = pST[0][:, 0:512]
                kbuf = [flctx.enter_context(
                    nc.sbuf_tensor(f"kb{i}", [128, 256], BF16)) for i in range(3)]
                vbuf = [flctx.enter_context(
                    nc.sbuf_tensor(f"vb{i}", [128, 256], BF16)) for i in range(3)]
                ptb = [flctx.enter_context(
                    nc.sbuf_tensor(f"pt{i}", [128, 1024], BF16)) for i in range(3)]
                linv = b.srow[0:1, 1024:1536]
                btab = b.xn_sb[:, 0:736]
                lbc = b.xn_sb[:, 736:1248]
                lacc = flctx.enter_context(
                    nc.sbuf_tensor("lacc", [128, 512], F32))

                def mk_btab(vector):
                    vector.wait_ge(dv.h, dv.v)
                    dv.inc(vector.tensor_tensor(
                        bass.AP(b.xn_sb, 0, [[NC * T, 128], [23, 32], [1, 23]]),
                        bass.AP(bias_hb, 0, [[32, 128], [1, 32], [0, 23]]),
                        bass.AP(b.c_mask, 0, [[23, 128], [0, 32], [1, 23]]),
                        ALU.add))
                    vector.wait_ge(dv.h, dv.v)
                block.vector(mk_btab)
                dv_btab = dv.v

                def tiles_of(blk):
                    out = []
                    for j in range(2):
                        out.append(("loc", 0, blk * 512 + 256 * j, 22, j))
                    nb = 7 if blk == 0 else 15
                    for bb_ in range(nb):
                        rb = rank_of_block(bb_) * CROWS
                        cb = col_of_block(bb_)
                        mi = bb_ if blk == 0 else 7 + bb_
                        for j in range(2):
                            out.append(("g", rb, cb + 256 * j, mi, None))
                    return out

                for h in range(NH):
                    for blk in range(2):
                        tl = tiles_of(blk)
                        nt = len(tl)
                        dv_ep0 = dv.v
                        pe_blk0 = pe.v
                        kld = [qK[i % 3].v + 16 * (i // 3 + 1)
                               for i in range(nt)]
                        vld = [qV[i % 3].v + 16 * (i // 3 + 1)
                               for i in range(nt)]
                        ac0 = ac.v
                        expm = [ac0 + (i + 1) for i in range(nt)]
                        dv0 = dv.v
                        stairm = [None] * nt
                        laccm = [None] * nt
                        cnt_dv = dv0 + 1
                        for i, t in enumerate(tl):
                            if t[4] is not None:
                                cnt_dv += 2
                            stairm[i] = cnt_dv
                            cnt_dv += 2
                            laccm[i] = cnt_dv
                        stm = [None] * nt
                        pvm = [None] * nt
                        cnt = pe.v
                        for i in range(nt):
                            cnt += 1
                            stm[i] = cnt
                            if i >= 1:
                                cnt += 1
                                pvm[i - 1] = cnt
                        cnt += 1
                        pvm[nt - 1] = cnt
                        cnt += 1
                        pe_lmm = cnt
                        pe_last = cnt

                        def fsync(sync, tl=tl, kld=kld, h=h, stm=stm,
                                  pe_blk0=pe_blk0):
                            for i, (src, rb, c0, mi, dj) in enumerate(tl):
                                if src == "loc":
                                    sync.wait_ge(qST.h, qs_vst)
                                    ten, roff = b.kvloc, h * 128
                                else:
                                    sync.wait_ge(cc.h, cc_gather)
                                    ten, roff = b.kvg, rb + h * 128
                                if i >= 3:
                                    sync.wait_ge(pe.h, stm[i - 3])
                                else:
                                    sync.wait_ge(pe.h, pe_blk0)
                                qK[i % 3].inc(sync.dma_start(
                                    out=kbuf[i % 3][:, :],
                                    in_=ten[roff:roff + 128, c0:c0 + 256]), 16)
                        block.sync(fsync)
                        assert qK[(nt - 1) % 3].v == kld[-1]
                        emit_tr_batch(18)

                        def fscal(scalar, tl=tl, vld=vld, kld=kld, h=h,
                                  blk=blk, expm=expm, stairm=stairm,
                                  laccm=laccm, stm=stm, pvm=pvm, nt=nt,
                                  pe_blk0=pe_blk0, dv_btab=dv_btab):
                            for i in range(nt + 2):
                                if i >= 2:
                                    t = i - 2
                                    src, rb, c0, mi, dj = tl[t]
                                    scalar.wait_ge(pe.h, stm[t])
                                    if dj is not None:
                                        scalar.wait_ge(dv.h, stairm[t])
                                    if t == 0:
                                        scalar.wait_ge(dv.h, dv_btab)
                                    if t >= 3:
                                        scalar.wait_ge(pe.h, pvm[t - 3])
                                        scalar.wait_ge(dv.h, laccm[t - 3])
                                    ac.inc(scalar.activation(
                                        ptb[t % 3][:, :], pST[t % 3][:, :],
                                        AF.Exp,
                                        bias=b.xn_sb[:, (2 * h + blk) * 23 + mi:
                                                     (2 * h + blk) * 23 + mi + 1],
                                        scale=SCALE))
                                if i < nt:
                                    src, rb, c0, mi, dj = tl[i]
                                    for sh_, sv_ in vt_done:
                                        scalar.wait_ge(sh_.h, sv_)
                                    if src == "loc":
                                        vten, vroff = b.vtokl, c0
                                    else:
                                        vten, vroff = (b.vtokg,
                                                       (rb // CROWS) * T + c0)
                                    if i >= 3:
                                        scalar.wait_ge(pe.h, pvm[i - 3])
                                    else:
                                        scalar.wait_ge(pe.h, pe_blk0)
                                    qV[i % 3].inc(scalar.dma_start(
                                        out=vbuf[i % 3][:, :],
                                        in_=bass.AP(
                                            vten, vroff * D + h * 128,
                                            [[D, 128], [128 * D, 2], [1, 128]])),
                                        16)
                        block.scalar(fscal)
                        assert ac.v == expm[-1]
                        assert qV[(nt - 1) % 3].v == vld[-1], (qV[(nt-1)%3].v, vld[-1])

                        def fvec(vector, tl=tl, h=h, blk=blk, stm=stm,
                                 expm=expm, nt=nt, pe_lmm_prev=None):
                            vector.wait_ge(pe.h, pe_blk0)
                            vector.wait_ge(dv.h, dv.v)
                            dv.inc(vector.memset(lacc[:, :], 0.0))
                            for i, (src, rb, c0, mi, dj) in enumerate(tl):
                                if dj is not None:
                                    vector.wait_ge(pe.h, stm[i])
                                    vector.wait_ge(dv.h, dv.v)
                                    for j in range(2):
                                        dv.inc(vector.tensor_tensor(
                                            pST[i % 3][:, j * 512:(j + 1) * 512],
                                            pST[i % 3][:, j * 512:(j + 1) * 512],
                                            c_stair[:, (2 * dj + j) * 512:
                                                    (2 * dj + j + 1) * 512],
                                            ALU.add))
                                vector.wait_ge(ac.h, expm[i])
                                vector.wait_ge(dv.h, dv.v)
                                for j in range(2):
                                    dv.inc(vector.tensor_tensor(
                                        lacc[:, :], lacc[:, :],
                                        ptb[i % 3][:, j * 512:(j + 1) * 512],
                                        ALU.add))
                                    vector.wait_ge(dv.h, dv.v)
                        block.vector(fvec)
                        assert dv.v == laccm[-1], (dv.v, laccm[-1])

                        def ftens(tensor, tl=tl, kld=kld, vld=vld, h=h,
                                  blk=blk, expm=expm, stairm=stairm,
                                  laccm=laccm, nt=nt, dv_ep0=dv_ep0):
                            qslice = b.pA[:, h * T + blk * 512:
                                          h * T + (blk + 1) * 512]
                            for i in range(nt):
                                tensor.wait_ge(qK[i % 3].h, kld[i])
                                if i >= 3:
                                    tensor.wait_ge(ac.h, expm[i - 3])
                                    tensor.wait_ge(dv.h, stairm[i - 3])
                                if i == 0:
                                    tensor.wait_ge(dv.h, dv_ep0)
                                for j in range(2):
                                    m = tensor.matmul(
                                        pST[i % 3][:, j * 512:(j + 1) * 512],
                                        kbuf[i % 3][:, j * 128:(j + 1) * 128],
                                        qslice, start=True, stop=True)
                                pe.inc(m)
                                if i >= 1:
                                    t = i - 1
                                    tensor.wait_ge(ac.h, expm[t])
                                    tensor.wait_ge(qV[t % 3].h, vld[t])
                                    for j in range(2):
                                        m = tensor.matmul(
                                            pAV[:, :],
                                            vbuf[t % 3][:, j * 128:(j + 1) * 128],
                                            ptb[t % 3][:, j * 512:(j + 1) * 512],
                                            start=(t == 0 and j == 0),
                                            stop=(t == nt - 1 and j == 1))
                                    pe.inc(m)
                            t = nt - 1
                            tensor.wait_ge(ac.h, expm[t])
                            tensor.wait_ge(qV[t % 3].h, vld[t])
                            for j in range(2):
                                m = tensor.matmul(
                                    pAV[:, :], vbuf[t % 3][:, j * 128:(j + 1) * 128],
                                    ptb[t % 3][:, j * 512:(j + 1) * 512],
                                    start=False, stop=(j == 1))
                            pe.inc(m)
                            tensor.wait_ge(dv.h, laccm[nt - 1])
                            pe.inc(tensor.matmul(
                                pL[:, :], b.c_oc32[:, :], lacc[:, :],
                                start=True, stop=True))
                        block.tensor(ftens)
                        assert pe.v == pe_last, (pe.v, pe_last)

                        def fep1(vector, h=h, blk=blk, pl=pe_last):
                            vector.wait_ge(pe.h, pl)
                            vector.wait_ge(dv.h, dv.v)
                            dv.inc(vector.reciprocal(linv[:, :], pL[:, :]))
                            vector.wait_ge(dv.h, dv.v)
                        block.vector(fep1)
                        dv_r = dv.v

                        def fep2(tensor, h=h, blk=blk):
                            tensor.wait_ge(dv.h, dv_r)
                            pe.inc(tensor.matmul(
                                pLB[:, :], b.c_or32[:, :], linv[:, :],
                                start=True, stop=True))
                        block.tensor(fep2)
                        pe_b = pe.v

                        def fep3(vector, h=h, blk=blk):
                            vector.wait_ge(pe.h, pe_b)
                            dv.inc(vector.tensor_copy(lbc[:, :], pLB[:, :]))
                            vector.wait_ge(dv.h, dv.v)
                            dv.inc(vector.tensor_tensor(
                                b.pB[:, h * T + blk * 512:
                                     h * T + (blk + 1) * 512],
                                pAV[:, :], lbc[:, :], ALU.mult))
                            vector.wait_ge(dv.h, dv.v)
                        block.vector(fep3)

                emit_tr_batch(nTR)
                b.pre_tr = [(s, s.v) for s in qTS if s.v > 0]

                flctx.close()
                barrier(gp=False)

                emit_proj(ph, "op", b.wo_b, 0, NC, NC, 2, (0, 1),
                          lambda dc, tt: b.pB[:, dc * T + tt * 512:
                                              dc * T + (tt + 1) * 512],
                          EvacResid(b.c_bqkvo, 48, T),
                          pre_wait=[(qC["wo"], cast_done["wo"][1])])
                barrier()

        def hsl(ec):
            if ec < 32:
                return b.pA[:, ec * 512:(ec + 1) * 512]
            return b.pB[:, (ec - 32) * 512:(ec - 31) * 512]

        def emit_ffn(tag):
            with ExitStack() as ph:
                emit_ln(ph, double=False)
                csb = ph.enter_context(nc.sbuf_tensor(f"{tag}csb", [128, 512], F32))
                tmp2 = [ph.enter_context(
                    nc.sbuf_tensor(f"{tag}tp{i}", [128, 512], F32))
                    for i in range(2)]

                for tt in range(2):
                    csctx = ExitStack()
                    p_cs = csctx.enter_context(
                        nc.psum_tensor(f"{tag}pcs{tt}", [1, 512], F32))
                    p_cb = csctx.enter_context(
                        nc.psum_tensor(f"{tag}pcb{tt}", [128, 512], F32))
                    def cs1(tensor, tt=tt):
                        for c in range(NC):
                            m = tensor.matmul(
                                p_cs[:, :], b.c_oc[:, :],
                                b.xn_sb[:, c * T + tt * 512:c * T + (tt + 1) * 512],
                                start=(c == 0), stop=(c == NC - 1))
                        pe.inc(m)
                    block.tensor(cs1)
                    pe_cs = pe.v

                    def cs2(vector, tt=tt):
                        vector.wait_ge(pe.h, pe_cs)
                        dv.inc(vector.tensor_scalar(
                            b.srow[:, 0:512], p_cs[:, :], 128.0, None, ALU.mult))
                        vector.wait_ge(dv.h, dv.v)
                    block.vector(cs2)
                    dv_cs = dv.v

                    def cs3(tensor, tt=tt):
                        tensor.wait_ge(dv.h, dv_cs)
                        pe.inc(tensor.matmul(p_cb[:, :], b.c_or32[:, :],
                                             b.srow[0:1, 0:512],
                                             start=True, stop=True))
                    block.tensor(cs3)
                    pe_cb = pe.v

                    def cs4(vector, tt=tt):
                        vector.wait_ge(pe.h, pe_cb)
                        dv.inc(vector.tensor_copy(csb[:, :], p_cb[:, :]))
                        vector.wait_ge(dv.h, dv.v)
                    block.vector(cs4)
                    csctx.close()
                    barrier()

                    class EvacG:
                        ndv, nac = 1, 1

                        def __init__(self):
                            self.dv0 = dv.v
                            self.ac0 = ac.v

                        def dv(self, vector, ec, pap):
                            vector.wait_ge(dv.h, dv.v)
                            if ec >= 2:
                                vector.wait_ge(ac.h, self.ac0 + ec - 1)
                            dv.inc(vector.tensor_tensor(
                                tmp2[ec % 2][:, :], pap[:, 0:512], csb[:, :],
                                ALU.subtract))

                        def ac(self, scalar, ec, pap):
                            scalar.wait_ge(dv.h, self.dv0 + ec + 1)
                            ac.inc(scalar.activation(
                                hsl(ec), tmp2[ec % 2][:, :], AF.Silu,
                                scale=b.c_gsc[:, ec:ec + 1]))
                    emit_proj(ph, f"{tag}g{tt}", b.gwT, 0, NC, NCI, 4, (tt,),
                              lambda dc, _tt: b.xn_sb[
                                  :, dc * T + tt * 512:dc * T + (tt + 1) * 512],
                              EvacG(), pre_wait=b.pre_tr)
                    barrier()

                    class EvacU:
                        ndv, nac = 2, 0

                        def dv(self, vector, ec, pap):
                            vector.wait_ge(dv.h, dv.v)
                            dv.inc(vector.tensor_tensor(
                                tmp2[ec % 2][:, :], pap[:, 0:512], csb[:, :],
                                ALU.subtract))
                            vector.wait_ge(dv.h, dv.v)
                            dv.inc(vector.scalar_tensor_tensor(
                                hsl(ec), tmp2[ec % 2][:, :],
                                b.c_usc[:, ec:ec + 1], hsl(ec),
                                ALU.mult, ALU.mult))
                    emit_proj(ph, f"{tag}u{tt}", b.uwT, 0, NC, NCI, 4, (tt,),
                              lambda dc, _tt: b.xn_sb[
                                  :, dc * T + tt * 512:dc * T + (tt + 1) * 512],
                              EvacU(), pre_wait=b.pre_tr)
                    barrier()

                    csctx2 = ExitStack()
                    p_cs = csctx2.enter_context(
                        nc.psum_tensor(f"{tag}pch{tt}", [1, 512], F32))
                    p_cb = csctx2.enter_context(
                        nc.psum_tensor(f"{tag}pcb2{tt}", [128, 512], F32))
                    def ch1(tensor, tt=tt):
                        for c in range(NCI):
                            m = tensor.matmul(
                                p_cs[:, :], b.c_oc[:, :], hsl(c),
                                start=(c == 0), stop=(c == NCI - 1))
                        pe.inc(m)
                    block.tensor(ch1)
                    pe_ch = pe.v

                    def ch2(vector, tt=tt):
                        vector.wait_ge(pe.h, pe_ch)
                        dv.inc(vector.tensor_scalar(
                            b.srow[:, 0:512], p_cs[:, :], 128.0, None, ALU.mult))
                        vector.wait_ge(dv.h, dv.v)
                    block.vector(ch2)
                    dv_ch = dv.v

                    def ch3(tensor, tt=tt):
                        tensor.wait_ge(dv.h, dv_ch)
                        pe.inc(tensor.matmul(p_cb[:, :], b.c_or32[:, :],
                                             b.srow[0:1, 0:512],
                                             start=True, stop=True))
                    block.tensor(ch3)
                    pe_cb2 = pe.v

                    def ch4(vector, tt=tt):
                        vector.wait_ge(pe.h, pe_cb2)
                        dv.inc(vector.tensor_copy(csb[:, :], p_cb[:, :]))
                        vector.wait_ge(dv.h, dv.v)
                    block.vector(ch4)
                    csctx2.close()
                    barrier()

                    class EvacD:
                        ndv, nac = 2, 0

                        def __init__(self, tt):
                            self.tt = tt

                        def dv(self, vector, ec, pap):
                            vector.wait_ge(dv.h, dv.v)
                            dv.inc(vector.tensor_tensor(
                                tmp2[ec % 2][:, :], pap[:, 0:512], csb[:, :],
                                ALU.subtract))
                            vector.wait_ge(dv.h, dv.v)
                            rsl = b.r_sb[:, ec * T + self.tt * 512:
                                         ec * T + (self.tt + 1) * 512]
                            dv.inc(vector.scalar_tensor_tensor(
                                rsl, tmp2[ec % 2][:, :],
                                b.c_dsc[:, ec:ec + 1], rsl,
                                ALU.mult, ALU.add))
                    emit_proj(ph, f"{tag}d{tt}", b.dwT, 0, NCI, NC, 4, (tt,),
                              lambda dc, _tt: hsl(dc),
                              EvacD(tt), pre_wait=b.pre_tr)
                    barrier()

        if "s2" in STAGES:
            emit_ffn("f2")

        if "s3" in STAGES:
            with ExitStack() as ph:
                emit_ln(ph, double=False)
                cs_sb = ph.enter_context(nc.sbuf_tensor("s3cs", [128, T], BF16))
                sn_sb = ph.enter_context(nc.sbuf_tensor("s3sn", [128, T], BF16))
                tmpf = ph.enter_context(nc.sbuf_tensor("s3tf", [128, T], F32))
                invc = ph.enter_context(nc.sbuf_tensor("s3inv", [128, 8], BF16))
                acol = ph.enter_context(nc.sbuf_tensor("s3acol", [128, 16], F32))
                invrow = ph.enter_context(nc.sbuf_tensor("s3ivr", [1, T], BF16))
                kvtok = ph.enter_context(nc.sbuf_tensor("s3kvt", [128, 2048], BF16))
                vstage = [ph.enter_context(
                    nc.sbuf_tensor(f"s3vs{i}", [128, T], BF16)) for i in range(2)]
                arow = b.srow[0:1, 0:2048]

                emit_proj(ph, "bk", b.winT, D, NC, NC, 2, (0, 1),
                          lambda dc, tt: b.xn_sb[:, dc * T + tt * 512:
                                                 dc * T + (tt + 1) * 512],
                          EvacBias(b.pA, 0, b.c_bin3, 16, T), pre_wait=b.pre_tr)
                barrier()

                b.rp_id = [0]

                def emit_rope(buf):
                    b.rp_id[0] += 1
                    rpctx = ExitStack()
                    p_rh = rpctx.enter_context(
                        nc.psum_tensor(f"p_rh{b.rp_id[0]}", [128, T], F32))
                    dv_prev = dv.v
                    for c in range(8):
                        def rl(sync, c=c, dv_prev=dv_prev):
                            if c > 0:
                                sync.wait_ge(dv.h, dv_prev)
                            qRP.inc(sync.dma_start(
                                out=cs_sb[:, :],
                                in_=b.cost[c * 128:(c + 1) * 128, :]), 16)
                            qRP.inc(sync.dma_start(
                                out=sn_sb[:, :],
                                in_=b.sint[c * 128:(c + 1) * 128, :]), 16)
                        block.sync(rl)
                        qs_c = qRP.v

                        def rp(tensor, c=c, dv_prev=dv_prev):
                            if c > 0:
                                tensor.wait_ge(dv.h, dv_prev)
                            for tt in range(2):
                                m = tensor.matmul(
                                    p_rh[:, tt * 512:(tt + 1) * 512],
                                    b.c_perm[:, :],
                                    buf[:, c * T + tt * 512:c * T + (tt + 1) * 512],
                                    start=True, stop=True)
                            pe.inc(m)
                        block.tensor(rp)
                        pe_c = pe.v

                        def rv(vector, c=c):
                            vector.wait_ge(qRP.h, qs_c)
                            vector.wait_ge(pe.h, pe_c)
                            vector.wait_ge(dv.h, dv.v)
                            dv.inc(vector.tensor_tensor(
                                tmpf[:, :], buf[:, c * T:(c + 1) * T],
                                cs_sb[:, :], ALU.mult))
                            vector.wait_ge(dv.h, dv.v)
                            dv.inc(vector.tensor_tensor(
                                p_rh[:, :], p_rh[:, :], sn_sb[:, :], ALU.mult))
                            vector.wait_ge(dv.h, dv.v)
                            dv.inc(vector.tensor_tensor(
                                buf[:, c * T:(c + 1) * T], tmpf[:, :],
                                p_rh[:, :], ALU.add))
                            vector.wait_ge(dv.h, dv.v)
                        block.vector(rv)
                        dv_prev = dv.v
                    rpctx.close()
                    barrier()
                emit_rope(b.pA)

                sq3 = vstage[0]
                knctx3 = ExitStack()
                p_kn = knctx3.enter_context(nc.psum_tensor("s3pkn", [1, T], F32))
                pe_prev = pe.v
                for c in range(NC):
                    def k2(scalar, c=c, pe_prev=pe_prev):
                        scalar.wait_ge(pe.h, pe_prev)
                        ac.inc(scalar.activation(
                            sq3[:, :], b.pA[:, c * T:(c + 1) * T], AF.Square))
                    block.scalar(k2)
                    ac_c = ac.v

                    def k2m(tensor, c=c, ac_c=ac_c):
                        tensor.wait_ge(ac.h, ac_c)
                        for tt in range(2):
                            m = tensor.matmul(
                                p_kn[:, tt * 512:(tt + 1) * 512], b.c_oc[:, :],
                                sq3[:, tt * 512:(tt + 1) * 512],
                                start=(c == 0), stop=(c == NC - 1))
                        pe.inc(m)
                    block.tensor(k2m)
                    pe_prev = pe.v
                barrier()

                def inv1(scalar):
                    ac.inc(scalar.activation(
                        b.srow[:, 0:T], p_kn[:, :], AF.Sqrt))
                block.scalar(inv1)
                ac_i = ac.v

                def inv2(vector):
                    vector.wait_ge(ac.h, ac_i)
                    dv.inc(vector.tensor_scalar(
                        b.srow[:, 0:T], b.srow[:, 0:T], 1e-12, None, ALU.max))
                    vector.wait_ge(dv.h, dv.v)
                    dv.inc(vector.reciprocal(b.srow[:, T:2 * T], b.srow[:, 0:T]))
                    vector.wait_ge(dv.h, dv.v)
                    dv.inc(vector.tensor_copy(invrow[:, :], b.srow[:, T:2 * T]))
                    vector.wait_ge(dv.h, dv.v)
                block.vector(inv2)
                dv_i = dv.v

                barrier()
                knctx3.close()
                trctx = ExitStack()
                p_tr32b = trctx.enter_context(
                    nc.psum_tensor("s3ptrb", [128, 8], F32))
                for j in range(8):
                    def trj(tensor, j=j):
                        pe.inc(tensor.matmul(
                            p_tr32b[:, j:j + 1],
                            invrow[0:1, j * 128:(j + 1) * 128],
                            b.c_oc[0:1, 0:1], start=True, stop=True))
                    block.tensor(trj)
                    pe_j = pe.v

                    def cvj(vector, j=j):
                        vector.wait_ge(pe.h, pe_j)
                        dv.inc(vector.tensor_copy(
                            invc[:, j:j + 1], p_tr32b[:, j:j + 1]))
                        vector.wait_ge(dv.h, dv.v)
                    block.vector(cvj)
                trctx.close()
                barrier()

                class EvacV2:
                    ndv, nac = 1, 1

                    def __init__(self):
                        self.ac0 = ac.v
                        self.dv0 = dv.v

                    def ac(self, scalar, ec, pap):
                        if ec >= 2:
                            scalar.wait_ge(dv.h, self.dv0 + ec - 1)
                        ac.inc(scalar.activation(
                            vstage[ec % 2][:, :], pap[:, 0:T], AF.Identity,
                            bias=b.c_bin3[:, 32 + ec:33 + ec]))

                    def dv(self, vector, ec, pap):
                        vector.wait_ge(ac.h, self.ac0 + ec + 1)
                        dv.inc(vector.tensor_tensor(
                            b.pA[:, ec * T:(ec + 1) * T],
                            b.pA[:, ec * T:(ec + 1) * T],
                            vstage[ec % 2][:, :], ALU.mult))
                emit_proj(ph, "bv", b.winT, 2 * D, NC, NC, 2, (0, 1),
                          lambda dc, tt: b.xn_sb[:, dc * T + tt * 512:
                                                 dc * T + (tt + 1) * 512],
                          EvacV2(), pre_wait=b.pre_tr)
                barrier()

                actx = ExitStack()
                p_tr = actx.enter_context(
                    nc.psum_tensor("s3ptr2", [128, 128], BF16))
                p_a = actx.enter_context(nc.psum_tensor("s3pa", [1, 2048], F32))
                amm_prev = pe.v
                for tc in range(8):
                    dv_ckv_prev = dv.v
                    for c in range(NC):
                        def tkv(tensor, tc=tc, c=c, amm_prev=amm_prev):
                            tensor.wait_ge(dv.h, dv.v)
                            if c == 0:
                                tensor.wait_ge(pe.h, amm_prev)
                            pe.inc(tensor.transpose(
                                p_tr[:, :],
                                b.pA[:, c * T + tc * 128:c * T + (tc + 1) * 128],
                                b.c_id[:, :]))
                        block.tensor(tkv)
                        pe_t = pe.v

                        def ckv(vector, tc=tc, c=c, pe_t=pe_t):
                            vector.wait_ge(pe.h, pe_t)
                            dv.inc(vector.tensor_copy(
                                kvtok[:, c * 128:(c + 1) * 128], p_tr[:, :]))
                            vector.wait_ge(dv.h, dv.v)
                        block.vector(ckv)
                    dv_t = dv.v

                    def amm(tensor, tc=tc, dv_t=dv_t):
                        tensor.wait_ge(dv.h, dv_t)
                        for j in range(4):
                            m = tensor.matmul(
                                p_a[:, j * 512:(j + 1) * 512],
                                invc[:, tc:tc + 1],
                                kvtok[:, j * 512:(j + 1) * 512],
                                start=(tc == 0), stop=(tc == 7))
                        pe.inc(m)
                    block.tensor(amm)
                    amm_prev = pe.v
                barrier()

                def aev(vector):
                    vector.wait_ge(pe.h, pe.v)
                    dv.inc(vector.tensor_copy(arow[:, :], p_a[:, :]))
                    vector.wait_ge(dv.h, dv.v)
                block.vector(aev)
                dv_a = dv.v

                def ast(g):
                    g.wait_ge(dv.h, dv_a)
                    qAB.inc(g.dma_start(out=b.abuf_i[:, :], in_=arow[:, :]), 16)
                    g.wait_ge(qAB.h, qAB.v)
                    cc.inc(g.collective_compute(
                        "AllReduce", ALU.add,
                        replica_groups=[list(range(NCORES))],
                        ins=[b.abuf_i.ap().opt()], outs=[b.abuf_o.ap().opt()]))
                    g.wait_ge(cc.h, cc.v)
                    qAB.inc(g.dma_start(out=arow[:, :], in_=b.abuf_o[:, :]), 16)
                    g.wait_ge(qAB.h, qAB.v)
                block.gpsimd(ast)
                actx.close()
                barrier()

                acctx = ExitStack()
                p_tr32 = acctx.enter_context(
                    nc.psum_tensor("s3ptr32", [128, 16], F32))
                for j in range(NC):
                    def taj(tensor, j=j):
                        pe.inc(tensor.matmul(
                            p_tr32[:, j:j + 1],
                            arow[0:1, j * 128:(j + 1) * 128],
                            b.c_or32[0:1, 0:1], start=True, stop=True))
                    block.tensor(taj)
                    pe_j = pe.v

                    def caj(vector, j=j):
                        vector.wait_ge(pe.h, pe_j)
                        dv.inc(vector.tensor_copy(
                            acol[:, j:j + 1], p_tr32[:, j:j + 1]))
                        vector.wait_ge(dv.h, dv.v)
                    block.vector(caj)
                acctx.close()
                barrier()

                emit_proj(ph, "bq", b.winT, 0, NC, NC, 2, (0, 1),
                          lambda dc, tt: b.xn_sb[:, dc * T + tt * 512:
                                                 dc * T + (tt + 1) * 512],
                          EvacBias(b.pB, 0, b.c_bin3, 0, T), pre_wait=b.pre_tr)
                barrier()
                emit_rope(b.pB)

                def yq(vector):
                    for c in range(NC):
                        dv.inc(vector.tensor_scalar(
                            b.pB[:, c * T:(c + 1) * T],
                            b.pB[:, c * T:(c + 1) * T],
                            acol[:, c:c + 1], None, ALU.mult))
                        vector.wait_ge(dv.h, dv.v)
                block.vector(yq)
                barrier()

                class EvacWO:
                    ndv, nac = 2, 0

                    def dv(self, vector, ec, pap):
                        rsl = b.r_sb[:, ec * T:(ec + 1) * T]
                        xsl = b.xn_sb[:, ec * T:(ec + 1) * T]
                        vector.wait_ge(dv.h, dv.v)
                        dv.inc(vector.scalar_tensor_tensor(
                            tmpf[:, :], pap[:, 0:T],
                            b.c_bout1[:, ec:ec + 1], xsl,
                            ALU.add, ALU.add))
                        vector.wait_ge(dv.h, dv.v)
                        dv.inc(vector.tensor_tensor(
                            rsl, rsl, tmpf[:, :], ALU.add))
                emit_proj(ph, "bo", b.woutT, 0, NC, NC, 2, (0, 1),
                          lambda dc, tt: b.pB[:, dc * T + tt * 512:
                                              dc * T + (tt + 1) * 512],
                          EvacWO(), pre_wait=b.pre_tr)
                barrier()

        if "s4" in STAGES:
            emit_ffn("f4")

        def fin(sync):
            for j in range(4):
                qS.inc(sync.dma_start(
                    out=bass.AP(b.yt, j * 4 * 128 * T,
                                [[T, 128], [128 * T, 4], [1, T]]),
                    in_=b.r_sb[:, j * 4 * T:(j + 1) * 4 * T]), 16)
            sync.wait_ge(qS.h, qS.v)
        block.sync(fin)

    return nc


_NC_CACHE = {}


def _get_nc():
    key = STAGES
    if key not in _NC_CACHE:
        _NC_CACHE[key] = build_nc()
    return _NC_CACHE[key]


def _host_consts():
    rot = 1024
    half = rot // 2
    invf = 1.0 / (10000.0 ** (np.arange(0, rot, 2, dtype=np.float64) / rot))
    pos = np.arange(S, dtype=np.float64)
    fr = pos[:, None] * invf[None, :]
    fr = np.repeat(fr, 2, axis=1)
    cosf = np.cos(fr).astype(np.float32)
    sinf = np.sin(fr).astype(np.float32)
    stair = np.where(np.arange(512)[None, :] >= np.arange(512)[:, None],
                     0.0, -1e9).astype(np.float32).astype(bfnp)
    ident = np.eye(128, dtype=np.float32).astype(bfnp)
    ident32 = np.eye(128, dtype=np.float32)
    onescol = np.ones((128, 1), np.float32).astype(bfnp)
    onescol32 = np.ones((128, 1), np.float32)
    onesrow32 = np.ones((1, 128), np.float32)
    perm = np.zeros((128, 128), np.float32)
    for jj in range(64):
        perm[2 * jj + 1, 2 * jj] = -1.0
        perm[2 * jj, 2 * jj + 1] = 1.0
    perm = perm.astype(bfnp)
    return cosf, sinf, stair, ident, ident32, onescol, onescol32, onesrow32, perm


def _chunkcols(vec):
    return np.ascontiguousarray(np.asarray(vec, np.float32).reshape(-1, 128).T)


def kernel(**inputs):
    nc = _get_nc()
    (cosf, sinf, stair, ident, ident32, onescol, onescol32, onesrow32,
     perm) = _host_consts()

    X = np.asarray(inputs["L0_attn_a_X"], np.float32).reshape(S, D)
    shared = dict(
        stair=stair, ident=ident, ident32=ident32, onescol=onescol,
        onescol32=onescol32, onesrow32=onesrow32, perm=perm,
        lngb=np.concatenate([_chunkcols(inputs["L0_attn_a_ln_g"]),
                             _chunkcols(inputs["L0_attn_a_ln_b"])], axis=1),
        bqkvo=np.concatenate([_chunkcols(inputs["L0_attn_a_bq"]),
                              _chunkcols(inputs["L0_attn_a_bk"]),
                              _chunkcols(inputs["L0_attn_a_bv"]),
                              _chunkcols(inputs["L0_attn_a_bo"])], axis=1),
        bin3=_chunkcols(inputs["L0_attn_b_b_in"]),
        bout1=_chunkcols(inputs["L0_attn_b_b_out"]),
        gsc=_chunkcols(inputs["L0_ffn_gate_scale"]),
        usc=_chunkcols(inputs["L0_ffn_up_scale"]),
        dsc=_chunkcols(inputs["L0_ffn_down_scale"]),
        wq=np.asarray(inputs["L0_attn_a_Wq"], np.float32),
        wk=np.asarray(inputs["L0_attn_a_Wk"], np.float32),
        wv=np.asarray(inputs["L0_attn_a_Wv"], np.float32),
        wo=np.asarray(inputs["L0_attn_a_Wo"], np.float32),
        win=np.asarray(inputs["L0_attn_b_W_in"], np.float32),
        wout=np.asarray(inputs["L0_attn_b_W_out"], np.float32),
        gw=np.ascontiguousarray(np.asarray(inputs["L0_ffn_gate_w"], np.int32)),
        uw=np.ascontiguousarray(np.asarray(inputs["L0_ffn_up_w"], np.int32)),
        dw=np.ascontiguousarray(np.asarray(inputs["L0_ffn_down_w"], np.int32)),
    )

    in_maps = []
    for c in range(NCORES):
        bA, bB = c, 15 - c
        toks = np.concatenate([np.arange(bA * 512, (bA + 1) * 512),
                               np.arange(bB * 512, (bB + 1) * 512)])
        xt = np.ascontiguousarray(X[toks, :].T)
        cost = np.ascontiguousarray(cosf[toks, :].T).astype(bfnp)
        sint = np.ascontiguousarray(sinf[toks, :].T).astype(bfnp)
        mask = np.zeros((128, 23), np.float32)
        for bb_ in range(7):
            if bb_ > c - 1:
                mask[:, bb_] = NEG
        for bb_ in range(15):
            if bb_ > 14 - c:
                mask[:, 7 + bb_] = NEG
        m = dict(shared)
        m.update(xt=xt, cost=cost, sint=sint, maskcol=mask)
        in_maps.append(m)

    res = run_bass_kernel_spmd(nc, in_maps, core_ids=list(range(NCORES)))

    out = np.zeros((1, S, D), np.float32)
    for c in range(NCORES):
        ytc = np.asarray(res.results[c]["yt"], np.float32)
        bA, bB = c, 15 - c
        out[0, bA * 512:(bA + 1) * 512, :] = ytc[:, 0:512].T
        out[0, bB * 512:(bB + 1) * 512, :] = ytc[:, 512:1024].T
    return out
